# version 02
# baseline (speedup 1.0000x reference)
import numpy as np
import concourse.bass as bass
import concourse.mybir as mybir
from concourse.bass_utils import run_bass_kernel_spmd

F32 = mybir.dt.float32
F32R = mybir.dt.float32r
AF = mybir.ActivationFunctionType
ALU = mybir.AluOpType
AX = mybir.AxisListType.X

P = 128
DM = 1024
FC = DM // P
MT = 2048
WARM = 384
TEXT = MT + WARM
LN_EPS = 1e-5

ENGS = ("sync", "tensor", "vector", "scalar", "gpsimd")


class Sem:
    __slots__ = ("h", "n")

    def __init__(self, nc, name):
        self.h = nc.alloc_semaphore(name)
        self.n = 0


AUTO_GUARD = {"vector": "dv", "scalar": "ac", "gpsimd": "gp"}


class B:

    def __init__(self, nc):
        self.nc = nc
        self.q = {e: [] for e in ENGS}
        self.S = {k: Sem(nc, "s_" + k) for k in
                  ("lc", "lx", "lw", "stx", "pe", "dv", "ac", "gp")}
        self._es = None

    def begin(self):
        from contextlib import ExitStack
        self._es = ExitStack()

    def sb(self, name, shape, dtype):
        return self._es.enter_context(self.nc.sbuf_tensor(name, shape, dtype))

    def psm(self, name, shape, dtype):
        return self._es.enter_context(self.nc.psum_tensor(name, shape, dtype))

    def end(self):
        for e in ENGS:
            for k in ("dv", "ac", "pe", "gp", "lc", "lx", "lw", "stx"):
                self.w(e, self.S[k])
        self._es.close()
        self._es = None

    def o(self, eng, fn, inc=None, k=1, dma=False):
        incs = []
        if eng in AUTO_GUARD and not dma:
            au = self.S[AUTO_GUARD[eng]]
            if au.n > 0:
                self.q[eng].append((("w", au.h, au.n), ()))
            incs.append((au.h, 1))
            au.n += 1
        elif inc is not None:
            inc.n += k
            incs.append((inc.h, k))
        self.q[eng].append((fn, tuple(incs)))

    def w(self, eng, sem, n=None):
        c = sem.n if n is None else n
        if c > 0:
            self.q[eng].append((("w", sem.h, c), ()))

    def emit(self):
        with self.nc.Block() as blk:
            for e in ENGS:
                ops = self.q[e]
                if not ops:
                    continue

                def runner(E, ops=ops):
                    for fn, incs in ops:
                        if isinstance(fn, tuple):
                            E.wait_ge(fn[1], fn[2])
                        else:
                            ins = fn(E)
                            for semh, k in incs:
                                ins = ins.then_inc(semh, k)

                getattr(blk, e)(runner)


def r32(ap):
    return ap.bitcast(F32R)



def emit_ln_tile(b, src_tile, dst_tile, sqb, st, eps_ap):
    S = b.S
    b.w("scalar", S["dv"])
    b.o("scalar", lambda E: E.activation(out=sqb[:], in_=src_tile, func=AF.Square), inc=S["ac"])
    b.w("vector", S["ac"])
    b.o("vector", lambda E: E.reduce_sum(out=st[:, 0:1], in_=src_tile, axis=AX))
    b.o("vector", lambda E: E.reduce_sum(out=st[:, 1:2], in_=sqb[:], axis=AX))
    b.o("vector", lambda E: E.tensor_tensor(out=st[:, 2:3], in0=st[:, 0:1], in1=st[:, 0:1], op=ALU.mult))
    b.o("vector", lambda E: E.scalar_tensor_tensor(out=st[:, 3:4], in0=st[:, 2:3], scalar=1.0 / DM,
                                                   in1=st[:, 1:2], op0=ALU.mult, op1=ALU.subtract),
        inc=S["dv"])
    b.w("scalar", S["dv"])
    b.o("scalar", lambda E: E.activation(out=st[:, 4:5], in_=st[:, 3:4], func=AF.Sqrt,
                                         scale=-1.0 / DM, bias=eps_ap), inc=S["ac"])
    b.w("vector", S["ac"])
    b.o("vector", lambda E: E.reciprocal(out=st[:, 5:6], in_=st[:, 4:5]))
    b.o("vector", lambda E: E.scalar_tensor_tensor(out=st[:, 6:7], in0=st[:, 0:1], scalar=-1.0 / DM,
                                                   in1=st[:, 5:6], op0=ALU.mult, op1=ALU.mult))
    b.o("vector", lambda E: E.tensor_scalar(out=dst_tile, in0=src_tile, scalar1=st[:, 5:6],
                                            scalar2=st[:, 6:7], op0=ALU.mult, op1=ALU.add),
        inc=S["dv"])


def emit_tm2fm(b, tm_tile, fm_tensor, col, pT, idt, out_r32=False):
    S = b.S
    b.w("tensor", S["dv"])
    b.w("tensor", S["ac"])
    for fc in range(FC):
        b.o("tensor", lambda E, fc=fc: E.transpose(out=pT[:, fc * P:(fc + 1) * P],
                                                   in_=tm_tile[:, fc * P:(fc + 1) * P],
                                                   identity=idt),
            inc=(S["pe"] if fc == FC - 1 else None))
    b.w("scalar", S["pe"])
    dst = fm_tensor[:, :, col:col + P]
    if out_r32:
        dst = dst.bitcast(F32R)
    b.o("scalar", lambda E, dst=dst: E.copy(out=dst,
                                   in_=pT[:].rearrange("p (a d) -> p a d", d=P)), inc=S["ac"])


def emit_fm2tm(b, fm_tensor, col, tm_tile, pT, idt):
    S = b.S
    b.w("tensor", S["dv"])
    b.w("tensor", S["ac"])
    for fc in range(FC):
        b.o("tensor", lambda E, fc=fc: E.transpose(out=pT[:, fc * P:(fc + 1) * P],
                                                   in_=fm_tensor[:, fc, col:col + P],
                                                   identity=idt),
            inc=(S["pe"] if fc == FC - 1 else None))
    b.w("scalar", S["pe"])
    b.o("scalar", lambda E: E.copy(out=tm_tile, in_=pT[:]), inc=S["ac"])


def emit_proj(b, wb, rhs_fm, rhs_c0, cw, banks, evac, nko=FC, kfc=None):
    S = b.S
    b.w("tensor", S["lw"])
    b.w("tensor", S["dv"])
    b.w("tensor", S["ac"])
    kr = range(FC) if kfc is None else kfc
    kl = list(kr)
    for mo in range(nko):
        for j, kc in enumerate(kl):
            b.o("tensor", lambda E, mo=mo, kc=kc, j=j:
                E.matmul(out=banks[mo][:, 0:cw], lhsT=r32(wb[:, kc, mo * P:(mo + 1) * P]),
                         rhs=r32(rhs_fm[:, kc, rhs_c0:rhs_c0 + cw]),
                         start=(j == 0), stop=(j == len(kl) - 1)),
                inc=(S["pe"] if (mo == nko - 1 and j == len(kl) - 1) else None))
    b.w("scalar", S["pe"])
    b.w("vector", S["pe"])
    for mo in range(nko):
        evac(mo)



def emit_l0_attn(b, io):
    nc = b.nc
    S = b.S
    a = lambda n_, sh_, dt_: b.sb(n_, sh_, dt_)
    H1 = WARM + MT // 2
    H2 = MT // 2
    HM = MT // 2

    xn = a("l0a_xn", [P, FC, H1 + 1], F32)
    kv = a("l0a_kv", [P, FC, H1], F32)
    rb = a("l0a_rb", [P, FC, HM], F32)
    sb = a("l0a_sb", [P, H1 + 1], F32)
    mix = a("l0a_mix", [P, FC, 512], F32R)
    wb = a("l0a_wb", [P, FC, DM], F32R)
    xt = [a(f"l0a_xt{i}", [P, DM], F32) for i in range(2)]
    xm = [a(f"l0a_xm{i}", [P, DM], F32) for i in range(2)]
    sqb = a("l0a_sq", [P, DM], F32)
    st = a("l0a_st", [P, 8], F32)
    idt = a("l0a_id", [P, P], F32)
    carry = a("l0a_carry", [P, FC], F32)
    V = {n: a("l0a_" + n, [P, FC], F32) for n in
         ("tmk", "tmv", "tmr", "omk", "omv", "omr", "decay", "tf", "g", "bWoT")}
    V["bvec"] = a("l0a_bvec", [P, FC], F32R)
    ztm = a("l0a_ztm", [P, DM], F32)
    epsb = a("l0a_eps", [P, 1], F32)
    zln = a("l0a_zln", [P, DM], F32)
    xold = a("l0a_xold", [P, DM], F32)

    scr_bwo = nc.dram_tensor("l0a_scr_bwo", [DM], F32, kind="Internal").ap()
    ps = [b.psm(f"l0a_p{i}", [P, 512], F32) for i in range(6)]
    pT = b.psm("l0a_pT", [P, 1024], F32)
    banks = [p[:] for p in ps] + [pT[:, 0:512], pT[:, 512:1024]]

    x_ext = io["x_ext"]
    x1 = io["x1"]

    b.o("sync", lambda E: E.dma_start(out=idt[:], in_=io["ident"]), inc=S["lc"], k=16)
    for dst, src in (("tmk", "tmk"), ("tmv", "tmv"), ("tmr", "tmr"),
                     ("decay", "td"), ("tf", "tf"), ("g", "g"), ("bvec", "bv")):
        b.o("sync", lambda E, o=V[dst][:], s_=io[src]: E.dma_start(out=o, in_=s_), inc=S["lc"], k=16)
    b.w("vector", S["lc"])
    for dst, m in (("omk", "tmk"), ("omv", "tmv"), ("omr", "tmr")):
        b.o("vector", lambda E, o=V[dst][:], i=V[m][:]:
            E.tensor_scalar(out=o, in0=i, scalar1=-1.0, scalar2=1.0, op0=ALU.mult, op1=ALU.add))
    b.o("vector", lambda E: E.memset(xn[:, :, 0:1], 0.0))
    b.o("vector", lambda E: E.memset(epsb[:], LN_EPS), inc=S["dv"])
    b.w("scalar", S["lc"])
    b.o("scalar", lambda E: E.activation(out=V["decay"][:], in_=V["decay"][:], func=AF.Exp), inc=S["ac"])

    for h in (0, 1):
        if h == 0:
            Th, m0, tt0, ntt = H1, WARM, 0, H1 // P
        else:
            Th, m0, tt0, ntt = H2, 0, H1 // P, H2 // P

        if h == 1:
            b.w("vector", S["pe"])
            b.w("vector", S["ac"])
            for fc in range(FC):
                b.o("vector", lambda E, fc=fc: E.tensor_copy(out=xn[:, fc, 0:1], in_=xn[:, fc, H1:H1 + 1]),
                    inc=(S["dv"] if fc == FC - 1 else None))

        for i in range(ntt):
            tt = tt0 + i
            b.w("sync", S["dv"])
            b.w("sync", S["pe"])
            b.o("sync", lambda E, o=xt[i % 2][:], r0=tt * P: E.dma_start(out=o, in_=x_ext[r0:r0 + P, :]),
                inc=S["lx"], k=16)
            b.w("vector", S["lx"])
            b.w("scalar", S["lx"])
            emit_ln_tile(b, xt[i % 2][:], xm[i % 2][:], sqb, st, epsb[:, 0:1])
            emit_tm2fm(b, xm[i % 2][:], xn, 1 + i * P, pT, idt[:])

        chunks = []
        c = 0
        while c < Th:
            chunks.append((c, min(512, Th - c)))
            c += 512
        for proj in ("k", "v"):
            wsrc = io["wkT"] if proj == "k" else io["wvT"]
            tmn, omn = ("tmk", "omk") if proj == "k" else ("tmv", "omv")
            b.w("sync", S["pe"])
            b.w("sync", S["ac"])
            b.o("sync", lambda E, s_=wsrc: E.dma_start(out=wb[:], in_=s_.rearrange("(a p) d -> p a d", p=P)),
                inc=S["lw"], k=16)
            for (c0, cw) in chunks:
                b.w("vector", S["ac"])
                b.w("vector", S["pe"])
                for fc in range(FC):
                    b.o("vector", lambda E, fc=fc, c0=c0, cw=cw, omn=omn:
                        E.tensor_scalar(out=mix[:, fc, 0:cw], in0=xn[:, fc, c0:c0 + cw],
                                        scalar1=V[omn][:, fc:fc + 1], scalar2=None, op0=ALU.mult))
                    b.o("vector", lambda E, fc=fc, c0=c0, cw=cw, tmn=tmn:
                        E.scalar_tensor_tensor(out=mix[:, fc, 0:cw], in0=xn[:, fc, 1 + c0:1 + c0 + cw],
                                               scalar=V[tmn][:, fc:fc + 1], in1=mix[:, fc, 0:cw],
                                               op0=ALU.mult, op1=ALU.add),
                        inc=(S["dv"] if fc == FC - 1 else None))

                if proj == "k":
                    def evac(mo, c0=c0, cw=cw):
                        b.o("scalar", lambda E, mo=mo: E.copy(out=kv[:, mo, c0:c0 + cw],
                                                              in_=banks[mo][:, 0:cw]),
                            inc=(S["ac"] if mo == FC - 1 else None))
                else:
                    def evac(mo, c0=c0, cw=cw):
                        b.o("vector", lambda E, mo=mo: E.tensor_tensor(out=kv[:, mo, c0:c0 + cw],
                                                                       in0=banks[mo][:, 0:cw],
                                                                       in1=kv[:, mo, c0:c0 + cw],
                                                                       op=ALU.mult),
                            inc=(S["dv"] if mo == FC - 1 else None))
                emit_proj(b, wb, mix, 0, cw, banks, evac)

        b.w("sync", S["pe"])
        b.w("sync", S["ac"])
        b.o("sync", lambda E: E.dma_start(out=wb[:], in_=io["wrT"].rearrange("(a p) d -> p a d", p=P)),
            inc=S["lw"], k=16)
        for ci in range(HM // 512):
            c0 = m0 + ci * 512
            b.w("vector", S["ac"])
            b.w("vector", S["pe"])
            for fc in range(FC):
                b.o("vector", lambda E, fc=fc, c0=c0:
                    E.tensor_scalar(out=mix[:, fc, 0:512], in0=xn[:, fc, c0:c0 + 512],
                                    scalar1=V["omr"][:, fc:fc + 1], scalar2=None, op0=ALU.mult))
                b.o("vector", lambda E, fc=fc, c0=c0:
                    E.scalar_tensor_tensor(out=mix[:, fc, 0:512], in0=xn[:, fc, 1 + c0:1 + c0 + 512],
                                           scalar=V["tmr"][:, fc:fc + 1], in1=mix[:, fc, 0:512],
                                           op0=ALU.mult, op1=ALU.add),
                    inc=(S["dv"] if fc == FC - 1 else None))

            def evac(mo, ci=ci):
                b.o("scalar", lambda E, mo=mo: E.copy(out=r32(rb[:, mo, ci * 512:(ci + 1) * 512]),
                                                      in_=banks[mo][:, 0:512]),
                    inc=(S["ac"] if mo == FC - 1 else None))
            emit_proj(b, wb, mix, 0, 512, banks, evac)

        b.w("vector", S["ac"])
        for fc in range(FC):
            if h == 0:
                b.o("vector", lambda E: E.memset(sb[:, 0:1], 0.0))
            else:
                b.o("vector", lambda E, fc=fc: E.tensor_copy(out=sb[:, 0:1], in_=carry[:, fc:fc + 1]))
            b.o("vector", lambda E, fc=fc, Th=Th:
                E.tensor_tensor_scan(out=sb[:, 1:1 + Th],
                                     data0=V["decay"][:, fc:fc + 1].broadcast_to([P, Th]),
                                     data1=kv[:, fc, 0:Th], initial=sb[:, 0:1],
                                     op0=ALU.mult, op1=ALU.add))
            if h == 0:
                b.o("vector", lambda E, fc=fc, Th=Th:
                    E.tensor_copy(out=carry[:, fc:fc + 1], in_=sb[:, Th:Th + 1]))
            b.o("vector", lambda E, fc=fc, m0=m0, Th=Th:
                E.scalar_tensor_tensor(out=kv[:, fc, m0:Th], in0=kv[:, fc, m0:Th],
                                       scalar=V["tf"][:, fc:fc + 1], in1=sb[:, 1 + m0:1 + Th],
                                       op0=ALU.mult, op1=ALU.add))
            b.o("vector", lambda E, fc=fc, m0=m0, Th=Th:
                E.tensor_tensor(out=kv[:, fc, m0:Th], in0=kv[:, fc, m0:Th],
                                in1=rb[:, fc, 0:HM], op=ALU.mult),
                inc=(S["dv"] if fc == FC - 1 else None))

        for i in range(HM // P):
            emit_fm2tm(b, kv, m0 + i * P, xt[i % 2][:], pT, idt[:])
            b.w("vector", S["ac"])
            emit_ln_tile(b, xt[i % 2][:], xm[i % 2][:], sqb, st, epsb[:, 0:1])
            emit_tm2fm(b, xm[i % 2][:], rb, i * P, pT, idt[:], out_r32=True)

        b.w("sync", S["pe"])
        b.w("sync", S["ac"])
        b.o("sync", lambda E: E.dma_start(out=wb[:], in_=io["woT"].rearrange("(a p) d -> p a d", p=P)),
            inc=S["lw"], k=16)
        if h == 0:
            b.w("tensor", S["lw"])
            b.w("tensor", S["ac"])
            b.w("tensor", S["dv"])
            b.w("tensor", S["lc"])
            for half_n in range(2):
                for kc in range(FC):
                    b.o("tensor", lambda E, half_n=half_n, kc=kc:
                        E.matmul(out=ps[half_n][0:1, 0:512], lhsT=r32(V["bvec"][:, kc:kc + 1]),
                                 rhs=r32(wb[:, kc, half_n * 512:(half_n + 1) * 512]),
                                 start=(kc == 0), stop=(kc == FC - 1)),
                        inc=(S["pe"] if (half_n == 1 and kc == FC - 1) else None))
            b.w("scalar", S["pe"])
            for half_n in range(2):
                b.o("scalar", lambda E, half_n=half_n:
                    E.copy(out=zln[0:1, half_n * 512:(half_n + 1) * 512], in_=ps[half_n][0:1, 0:512]),
                    inc=(S["ac"] if half_n == 1 else None))
            b.w("sync", S["ac"])
            b.o("sync", lambda E: E.dma_start(out=scr_bwo, in_=zln[0:1, :]), inc=S["lc"], k=16)
            b.w("sync", S["lc"])
            b.o("sync", lambda E: E.dma_start(out=V["bWoT"][:], in_=scr_bwo.rearrange("(a p) -> p a", p=P),
                                              allow_slow_non_contiguous=True), inc=S["lc"], k=16)
        b.w("vector", S["lw"])
        b.w("vector", S["pe"])
        for kc in range(FC):
            b.o("vector", lambda E, kc=kc: E.tensor_single_scalar(out=wb[:, kc, :], in_=wb[:, kc, :],
                                                                  scalar=V["g"][:, kc:kc + 1], op=ALU.mult),
                inc=(S["dv"] if kc == FC - 1 else None))
        zv = xn
        b.w("vector", S["lc"])
        for ci in range(HM // 512):
            def evac(mo, ci=ci):
                b.o("vector", lambda E, mo=mo: E.tensor_single_scalar(out=zv[:, mo, ci * 512:(ci + 1) * 512],
                                                                      in_=banks[mo][:, 0:512],
                                                                      scalar=V["bWoT"][:, mo:mo + 1],
                                                                      op=ALU.add),
                    inc=(S["dv"] if mo == FC - 1 else None))
            emit_proj(b, wb, rb, ci * 512, 512, banks, evac)

        for i in range(HM // P):
            emit_fm2tm(b, zv, i * P, ztm[:], pT, idt[:])
            b.w("vector", S["ac"])
            emit_ln_tile(b, ztm[:], zln[:], sqb, st, epsb[:, 0:1])
            r0 = WARM + h * HM + i * P
            b.w("sync", S["dv"], S["dv"].n - 1)
            b.o("sync", lambda E, r0=r0: E.dma_start(out=xold[:], in_=x_ext[r0:r0 + P, :]),
                inc=S["lx"], k=16)
            b.w("vector", S["lx"])
            b.o("vector", lambda E: E.tensor_tensor(out=zln[:], in0=zln[:], in1=xold[:], op=ALU.add),
                inc=S["dv"])
            b.w("sync", S["dv"])
            b.o("sync", lambda E, q0=h * HM + i * P: E.dma_start(out=x1[q0:q0 + P, :], in_=zln[:]),
                inc=S["stx"], k=16)
            b.w("vector", S["stx"])

    b.w("sync", S["stx"])



RMS_EPS = 1e-6
NUNIT = 8
HB = 1024


def emit_l0_ffn(b, io):
    dbg = io.get("dbg")
    nc = b.nc
    S = b.S
    a = lambda n_, sh_, dt_: b.sb(n_, sh_, dt_)

    xf = a("l0f_xf", [P, FC, HB], F32)
    hb = a("l0f_h", [P, FC, HB], F32)
    h2b = a("l0f_h2", [P, FC, HB], F32)
    zb = a("l0f_z", [P, 8, HB], F32)
    wbuf = a("l0f_wb", [P, FC, DM], F32R)
    rwb = a("l0f_rw", [P, FC, 6], F32R)
    wgt = a("l0f_wgt", [P, 48], F32)
    silb = a("l0f_sil", [P, 512], F32)
    xt = [a(f"l0f_xt{i}", [P, DM], F32) for i in range(2)]
    xm = [a(f"l0f_xm{i}", [P, DM], F32) for i in range(2)]
    sqb = a("l0f_sq", [P, DM], F32)
    st = a("l0f_st", [P, 8], F32)
    idt = a("l0f_id", [P, P], F32)
    epsb = a("l0f_eps", [P, 2], F32)
    bupv = a("l0f_bup", [P, 64], F32)
    bgv = a("l0f_bg", [P, 64], F32)
    onesr = a("l0f_ones", [1, P], F32R)
    brow = a("l0f_brow", [1, 12], F32R)
    bdals = a("l0f_bd", [1, DM], F32R)
    nwrow = a("l0f_nwr", [1, DM], F32R)
    nwb = a("l0f_nw", [P, DM], F32)
    gtile = a("l0f_gt", [P, 16], F32)

    ps = [b.psm(f"l0f_p{i}", [P, 512], F32) for i in range(6)]
    pT = b.psm("l0f_pT", [P, 1024], F32)

    x1 = io["x1"]
    x2 = io["x2"]

    b.o("sync", lambda E: E.dma_start(out=idt[:], in_=io["ident"]), inc=S["lc"], k=16)
    b.o("sync", lambda E: E.dma_start(out=bupv[:], in_=io["bup"]), inc=S["lc"], k=16)
    b.o("sync", lambda E: E.dma_start(out=bgv[:], in_=io["bg"]), inc=S["lc"], k=16)
    b.o("sync", lambda E: E.dma_start(out=onesr[:], in_=io["ones"]), inc=S["lc"], k=16)
    b.w("vector", S["lc"])
    b.o("vector", lambda E: E.memset(epsb[:, 0:1], LN_EPS))
    b.o("vector", lambda E: E.memset(epsb[:, 1:2], RMS_EPS))

    def rms_tile(src, dst, mul_nw):
        b.w("scalar", S["dv"])
        b.o("scalar", lambda E: E.activation(out=sqb[:], in_=src, func=AF.Square))
        b.w("vector", S["ac"])
        b.o("vector", lambda E: E.reduce_sum(out=st[:, 0:1], in_=sqb[:], axis=AX))
        b.w("scalar", S["dv"])
        b.o("scalar", lambda E: E.activation(out=st[:, 4:5], in_=st[:, 0:1], func=AF.Sqrt,
                                             scale=1.0 / DM, bias=epsb[:, 1:2]))
        b.w("vector", S["ac"])
        b.o("vector", lambda E: E.reciprocal(out=st[:, 5:6], in_=st[:, 4:5]))
        b.o("vector", lambda E: E.tensor_single_scalar(out=dst, in_=src, scalar=st[:, 5:6],
                                                       op=ALU.mult))
        if mul_nw:
            b.o("vector", lambda E: E.tensor_tensor(out=dst, in0=dst, in1=nwb[:], op=ALU.mult))

    for h in (0, 1):
        t0 = h * HB

        for i in range(HB // P):
            b.w("sync", S["dv"])
            b.w("sync", S["pe"])
            b.o("sync", lambda E, r0=t0 + i * P, i=i: E.dma_start(out=xt[i % 2][:], in_=x1[r0:r0 + P, :]),
                inc=S["lx"], k=16)
            b.w("vector", S["lx"])
            b.w("scalar", S["lx"])
            emit_ln_tile(b, xt[i % 2][:], xm[i % 2][:], sqb, st, epsb[:, 0:1])
            emit_tm2fm(b, xm[i % 2][:], xf, i * P, pT, idt[:], out_r32=True)

        for j in (0, 1):
            b.w("sync", S["dv"])
            b.w("sync", S["ac"])
            b.w("sync", S["pe"])
            b.o("sync", lambda E, j=j: E.dma_start(out=brow[0:1, 0:6], in_=io["rb"][j:j + 1, :]),
                inc=S["lc"], k=16)
            b.o("sync", lambda E, j=j: E.dma_start(out=brow[0:1, 6:12], in_=io["gb"][j:j + 1, :]),
                inc=S["lc"], k=16)
            b.o("sync", lambda E, j=j: E.dma_start(
                out=rwb[:], in_=io["routerW"][j].rearrange("(a p) e -> p a e", p=P)),
                inc=S["lw"], k=16)
            b.o("sync", lambda E, j=j: E.dma_start(out=nwrow[:], in_=io["normw"][j:j + 1, :]),
                inc=S["lc"], k=16)
            b.w("tensor", S["lc"])
            b.w("tensor", S["ac"])
            b.w("tensor", S["dv"])
            b.o("tensor", lambda E: E.matmul(out=pT[:, 0:512], lhsT=onesr[:],
                                             rhs=nwrow[0:1, 0:512], start=True, stop=True))
            b.o("tensor", lambda E: E.matmul(out=pT[:, 512:1024], lhsT=onesr[:],
                                             rhs=nwrow[0:1, 512:1024], start=True, stop=True),
                inc=S["pe"])
            b.w("vector", S["pe"])
            b.o("vector", lambda E: E.tensor_copy(out=nwb[:], in_=pT[:]))

            for tt in range(HB // P):
                b.w("tensor", S["lw"])
                b.w("tensor", S["dv"])
                b.w("tensor", S["ac"])
                for grp in range(2):
                    pb = ps[4 + grp]
                    for kc in range(FC):
                        b.o("tensor", lambda E, kc=kc, tt=tt, pb=pb:
                            E.matmul(out=pb[:, 0:6], lhsT=r32(xf[:, kc, tt * P:(tt + 1) * P]),
                                     rhs=rwb[:, kc, :], start=(kc == 0), stop=False))
                    if grp == 0:
                        b.o("tensor", lambda E, pb=pb:
                            E.matmul(out=pb[:, 0:6], lhsT=onesr[:], rhs=brow[0:1, 0:6],
                                     start=False, stop=True))
                    else:
                        b.o("tensor", lambda E, pb=pb:
                            E.matmul(out=pb[:, 0:6], lhsT=onesr[:], rhs=brow[0:1, 0:6],
                                     start=False, stop=False))
                        b.o("tensor", lambda E, pb=pb:
                            E.matmul(out=pb[:, 0:6], lhsT=onesr[:], rhs=brow[0:1, 6:12],
                                     start=False, stop=True), inc=S["pe"])
                b.w("vector", S["pe"])
                b.o("vector", lambda E: E.tensor_copy(out=gtile[:, 0:6], in_=ps[4][:, 0:6]))
                b.o("vector", lambda E: E.tensor_copy(out=gtile[:, 8:14], in_=ps[5][:, 0:6]))
                b.o("vector", lambda E: E.reduce_max(out=st[:, 0:1], in_=gtile[:, 8:14], axis=AX))
                b.o("vector", lambda E: E.tensor_scalar(out=sqb[:, 0:6], in0=gtile[:, 8:14],
                                                        scalar1=st[:, 0:1], scalar2=-1e30,
                                                        op0=ALU.is_ge, op1=ALU.mult))
                b.o("vector", lambda E: E.tensor_tensor(out=sqb[:, 8:14], in0=gtile[:, 8:14],
                                                        in1=sqb[:, 0:6], op=ALU.add))
                b.o("vector", lambda E: E.reduce_max(out=st[:, 1:2], in_=sqb[:, 8:14], axis=AX))
                b.o("vector", lambda E: E.tensor_scalar(out=sqb[:, 0:6], in0=gtile[:, 8:14],
                                                        scalar1=st[:, 1:2], scalar2=None,
                                                        op0=ALU.is_ge))
                b.o("vector", lambda E: E.tensor_scalar(out=sqb[:, 8:14], in0=sqb[:, 0:6],
                                                        scalar1=1e30, scalar2=-1e30,
                                                        op0=ALU.mult, op1=ALU.add))
                b.o("vector", lambda E: E.tensor_tensor(out=sqb[:, 16:22], in0=gtile[:, 0:6],
                                                        in1=sqb[:, 0:6], op=ALU.mult))
                b.o("vector", lambda E: E.tensor_tensor(out=sqb[:, 16:22], in0=sqb[:, 16:22],
                                                        in1=sqb[:, 8:14], op=ALU.add))
                b.o("vector", lambda E: E.reduce_max(out=st[:, 2:3], in_=sqb[:, 16:22], axis=AX))
                b.o("vector", lambda E: E.tensor_single_scalar(out=st[:, 3:4], in_=st[:, 2:3],
                                                               scalar=-1.0, op=ALU.mult))
                b.w("scalar", S["dv"])
                b.o("scalar", lambda E: E.activation(out=sqb[:, 24:30], in_=sqb[:, 16:22],
                                                     func=AF.Exp, bias=st[:, 3:4],
                                                     accum_out=st[:, 6:7]))
                b.w("vector", S["ac"])
                b.o("vector", lambda E: E.reciprocal(out=st[:, 7:8], in_=st[:, 6:7]))
                b.o("vector", lambda E, tt=tt: E.tensor_single_scalar(out=wgt[:, tt * 6:(tt + 1) * 6],
                                                                      in_=sqb[:, 24:30],
                                                                      scalar=st[:, 7:8], op=ALU.mult))
                if dbg is not None and h == 0 and j == 0 and tt == 0:
                    b.w("sync", S["dv"])
                    b.o("sync", lambda E: E.dma_start(out=dbg[0][:, 0:6], in_=gtile[:, 0:6]), inc=S["stx"], k=16)
                    b.o("sync", lambda E: E.dma_start(out=dbg[0][:, 8:14], in_=gtile[:, 8:14]), inc=S["stx"], k=16)
                    b.o("sync", lambda E: E.dma_start(out=dbg[1][:, 0:6], in_=wgt[:, 0:6]), inc=S["stx"], k=16)

            for u in range(NUNIT):
                is_sh = u < 2
                e = u - 2
                b.w("sync", S["pe"])
                b.w("sync", S["ac"])
                b.w("sync", S["dv"])
                if is_sh:
                    wsrc = io["shWup"].rearrange("(a p) d -> p a d", p=P)[:, :, u * DM:(u + 1) * DM]
                else:
                    wsrc = io["expWup"][e].rearrange("(a p) d -> p a d", p=P)
                b.o("sync", lambda E, wsrc=wsrc: E.dma_start(out=wbuf[:], in_=wsrc), inc=S["lw"], k=16)
                if dbg is not None and h == 0 and j == 0 and u == 0:
                    pass
                for ci in range(HB // 512):
                    def evac_h(mo, ci=ci, u=u):
                        b.o("vector", lambda E, mo=mo, ci=ci, u=u:
                            E.tensor_scalar(out=r32(hb[:, mo, ci * 512:(ci + 1) * 512]),
                                            in0=ps[mo % 4][:, 0:512],
                                            scalar1=bupv[:, u * FC + mo:u * FC + mo + 1],
                                            scalar2=None, op0=ALU.add))
                    emit_proj_rot(b, wbuf, xf, ci * 512, ps, evac_h)
                if dbg is not None and h == 0 and j == 0 and u == 0:
                    b.w("sync", S["dv"])
                    b.o("sync", lambda E: E.dma_start(out=dbg[10][:, :], in_=hb[:, 0, :]), inc=S["stx"], k=16)
                    for _e in ("vector", "tensor", "scalar"):
                        b.w(_e, S["stx"])
                b.w("sync", S["pe"])
                b.w("sync", S["ac"])
                b.w("sync", S["dv"])
                wsrc = (io["shWg"] if is_sh else io["expWg"][e]).rearrange("(a p) d -> p a d", p=P)
                b.o("sync", lambda E, wsrc=wsrc: E.dma_start(out=wbuf[:], in_=wsrc), inc=S["lw"], k=16)
                for ci in range(HB // 512):
                    def evac_g(mo, ci=ci, u=u):
                        b.w("scalar", S["dv"])
                        b.o("scalar", lambda E, mo=mo, u=u:
                            E.activation(out=silb[:, 0:512], in_=ps[mo % 4][:, 0:512], func=AF.Silu,
                                         bias=bgv[:, u * FC + mo:u * FC + mo + 1]))
                        b.w("vector", S["ac"])
                        b.o("vector", lambda E, mo=mo, ci=ci:
                            E.tensor_tensor(out=r32(h2b[:, mo, ci * 512:(ci + 1) * 512]),
                                            in0=silb[:, 0:512],
                                            in1=hb[:, mo, ci * 512:(ci + 1) * 512], op=ALU.mult))
                    emit_proj_rot(b, wbuf, hb, ci * 512, ps, evac_g)
                if dbg is not None and h == 0 and j == 0 and u == 0:
                    b.w("sync", S["dv"])
                    b.o("sync", lambda E: E.dma_start(out=dbg[11][:, :], in_=h2b[:, 0, :]), inc=S["stx"], k=16)
                    for _e in ("vector", "tensor", "scalar"):
                        b.w(_e, S["stx"])
                b.w("sync", S["pe"])
                b.w("sync", S["ac"])
                b.w("sync", S["dv"])
                wsrc = (io["shWdown"] if is_sh else io["expWdown"][e]).rearrange("(a p) d -> p a d", p=P)
                b.o("sync", lambda E, wsrc=wsrc: E.dma_start(out=wbuf[:], in_=wsrc), inc=S["lw"], k=16)
                b.o("sync", lambda E, u=u: E.dma_start(out=bdals[:], in_=io["bdall"][0:1, u * DM:(u + 1) * DM]),
                    inc=S["lc"], k=16)
                b.w("tensor", S["lw"])
                b.w("tensor", S["lc"])
                for tt in range(HB // P):
                    for nc_ in range(2):
                        bank = ps[(tt * 2 + nc_) % 4]
                        b.w("tensor", S["dv"])
                        for kc in range(FC):
                            b.o("tensor", lambda E, tt=tt, nc_=nc_, kc=kc, bank=bank:
                                E.matmul(out=bank[:, 0:512],
                                         lhsT=r32(h2b[:, kc, tt * P:(tt + 1) * P]),
                                         rhs=wbuf[:, kc, nc_ * 512:(nc_ + 1) * 512],
                                         start=(kc == 0), stop=False))
                        b.o("tensor", lambda E, nc_=nc_, bank=bank:
                            E.matmul(out=bank[:, 0:512], lhsT=onesr[:],
                                     rhs=bdals[0:1, nc_ * 512:(nc_ + 1) * 512],
                                     start=False, stop=True), inc=S["pe"])
                        b.w("vector", S["pe"])
                        if u == 0:
                            b.o("vector", lambda E, tt=tt, nc_=nc_, bank=bank:
                                E.tensor_copy(out=zb[:, tt, nc_ * 512:(nc_ + 1) * 512],
                                              in_=bank[:, 0:512]))
                        elif is_sh:
                            b.o("vector", lambda E, tt=tt, nc_=nc_, bank=bank:
                                E.tensor_tensor(out=zb[:, tt, nc_ * 512:(nc_ + 1) * 512],
                                                in0=bank[:, 0:512],
                                                in1=zb[:, tt, nc_ * 512:(nc_ + 1) * 512], op=ALU.add))
                        else:
                            b.o("vector", lambda E, tt=tt, nc_=nc_, e=e, bank=bank:
                                E.scalar_tensor_tensor(out=zb[:, tt, nc_ * 512:(nc_ + 1) * 512],
                                                       in0=bank[:, 0:512],
                                                       scalar=wgt[:, tt * 6 + e:tt * 6 + e + 1],
                                                       in1=zb[:, tt, nc_ * 512:(nc_ + 1) * 512],
                                                       op0=ALU.mult, op1=ALU.add))

                if dbg is not None and h == 0 and j == 0:
                    b.w("sync", S["dv"])
                    b.o("sync", lambda E, u=u: E.dma_start(out=dbg[2 + u][:, :], in_=zb[:, 0, :]),
                        inc=S["stx"], k=16)
                    for _e in ("vector", "tensor", "scalar"):
                        b.w(_e, S["stx"])

            last = (j == 1)
            for tt in range(HB // P):
                emit_fm2tm(b, xf, tt * P, xt[tt % 2][:], pT, idt[:])
                b.w("vector", S["ac"])
                b.o("vector", lambda E, tt=tt: E.tensor_tensor(out=xt[tt % 2][:], in0=xt[tt % 2][:],
                                                               in1=zb[:, tt, :], op=ALU.add))
                rms_tile(xt[tt % 2][:], xm[tt % 2][:], True)
                if not last:
                    emit_tm2fm(b, xm[tt % 2][:], xf, tt * P, pT, idt[:], out_r32=True)
                else:
                    emit_ln_tile(b, xm[tt % 2][:], xt[tt % 2][:], sqb, st, epsb[:, 0:1])
                    b.w("sync", S["dv"])
                    b.o("sync", lambda E, r0=t0 + tt * P: E.dma_start(out=sqb[:], in_=x1[r0:r0 + P, :]),
                        inc=S["lx"], k=16)
                    b.w("vector", S["lx"])
                    b.o("vector", lambda E, tt=tt: E.tensor_tensor(out=xt[tt % 2][:], in0=xt[tt % 2][:],
                                                                   in1=sqb[:], op=ALU.add))
                    b.w("sync", S["dv"])
                    b.o("sync", lambda E, r0=t0 + tt * P, tt=tt:
                        E.dma_start(out=x2[r0:r0 + P, :], in_=xt[tt % 2][:]), inc=S["stx"], k=16)
                    b.w("vector", S["stx"])
                    b.w("scalar", S["stx"])

    b.w("sync", S["stx"])


def emit_proj_rot(b, wb, rhs_fm, rhs_c0, ps, evac, nko=FC):
    S = b.S
    b.w("tensor", S["lw"])
    b.w("tensor", S["dv"])
    b.w("tensor", S["ac"])
    for mo in range(nko):
        if mo == 4:
            b.w("tensor", S["dv"])
            b.w("tensor", S["ac"])
        for kc in range(FC):
            b.o("tensor", lambda E, mo=mo, kc=kc:
                E.matmul(out=ps[mo % 4][:, 0:512], lhsT=wb[:, kc, mo * P:(mo + 1) * P],
                         rhs=r32(rhs_fm[:, kc, rhs_c0:rhs_c0 + 512]),
                         start=(kc == 0), stop=(kc == FC - 1)),
                inc=(S["pe"] if kc == FC - 1 else None))
        if mo == min(3, nko - 1) or mo == nko - 1:
            if mo == min(3, nko - 1) or mo == 7:
                b.w("vector", S["pe"])
                b.w("scalar", S["pe"])
                lo = 0 if mo <= 3 else 4
                for m2 in range(lo, mo + 1):
                    evac(m2)


def _vec_fm(v):
    return np.ascontiguousarray(v.reshape(FC, P).T)


def build_l0a(nc=None, races=True):
    nc = nc or bass.Bass("TRN2", target_bir_lowering=False, debug=False, detect_race_conditions=races)
    io = {
        "x_ext": nc.dram_tensor("x_ext", [TEXT, DM], F32, kind="ExternalInput").ap(),
        "wkT": nc.dram_tensor("wkT", [DM, DM], F32R, kind="ExternalInput").ap(),
        "wvT": nc.dram_tensor("wvT", [DM, DM], F32R, kind="ExternalInput").ap(),
        "wrT": nc.dram_tensor("wrT", [DM, DM], F32R, kind="ExternalInput").ap(),
        "woT": nc.dram_tensor("woT", [DM, DM], F32R, kind="ExternalInput").ap(),
        "tmk": nc.dram_tensor("tmk", [P, FC], F32, kind="ExternalInput").ap(),
        "tmv": nc.dram_tensor("tmv", [P, FC], F32, kind="ExternalInput").ap(),
        "tmr": nc.dram_tensor("tmr", [P, FC], F32, kind="ExternalInput").ap(),
        "td": nc.dram_tensor("td", [P, FC], F32, kind="ExternalInput").ap(),
        "tf": nc.dram_tensor("tf", [P, FC], F32, kind="ExternalInput").ap(),
        "g": nc.dram_tensor("g", [P, FC], F32, kind="ExternalInput").ap(),
        "bv": nc.dram_tensor("bv", [P, FC], F32R, kind="ExternalInput").ap(),
        "ident": nc.dram_tensor("ident", [P, P], F32, kind="ExternalInput").ap(),
        "x1": nc.dram_tensor("x1", [MT, DM], F32, kind="ExternalOutput").ap(),
    }
    b = B(nc)
    b.begin()
    emit_l0_attn(b, io)
    b.end()
    b.emit()
    return nc


def l0a_core_inputs(c, x, wk, wv, wr, wo, tmk, tmv, tmr, td, tf, g, bv):
    seq, q = c // 4, c % 4
    start = q * MT
    if start >= WARM:
        ext = x[seq, start - WARM:start + MT]
    else:
        ext = np.concatenate([np.zeros((WARM - start, DM), np.float32),
                              x[seq, 0:start + MT]], axis=0)
    return {
        "x_ext": np.ascontiguousarray(ext, np.float32),
        "wkT": np.ascontiguousarray(wk.T), "wvT": np.ascontiguousarray(wv.T),
        "wrT": np.ascontiguousarray(wr.T), "woT": np.ascontiguousarray(wo.T),
        "tmk": _vec_fm(np.asarray(tmk).reshape(-1)), "tmv": _vec_fm(np.asarray(tmv).reshape(-1)),
        "tmr": _vec_fm(np.asarray(tmr).reshape(-1)),
        "td": _vec_fm(np.asarray(td).reshape(-1)), "tf": _vec_fm(np.asarray(tf).reshape(-1)),
        "g": _vec_fm(np.asarray(g).reshape(-1)), "bv": _vec_fm(np.asarray(bv).reshape(-1)),
        "ident": np.eye(P, dtype=np.float32),
    }




NSEQ = 8192
NKB5 = NSEQ // 512
NKB1 = NSEQ // P
BIGNEG = -1e30


def emit_l1_qkv(b, io):
    nc = b.nc
    S = b.S
    a = lambda n_, sh_, dt_: b.sb(n_, sh_, dt_)
    xn = a("l1q_xn", [P, FC, MT], F32)
    wq = a("l1q_wq", [P, FC, 256], F32R)
    wk = a("l1q_wk", [P, FC, 256], F32R)
    wv = a("l1q_wv", [P, FC, 256], F32R)
    xt = [a(f"l1q_xt{i}", [P, DM], F32) for i in range(2)]
    xm = [a(f"l1q_xm{i}", [P, DM], F32) for i in range(2)]
    sqb = a("l1q_sq", [P, DM], F32)
    st = a("l1q_st", [P, 8], F32)
    idt = a("l1q_id", [P, P], F32)
    epsb = a("l1q_eps", [P, 1], F32)
    qkst = a("l1q_qk", [P, 2, MT], F32)
    vdst = a("l1q_vd", [P, 16, 256], F32)
    ps = [b.psm(f"l1q_p{i}", [P, 512], F32) for i in range(6)]
    pT = b.psm("l1q_pT", [P, 1024], F32)

    x2 = io["x2"]
    b.o("sync", lambda E: E.dma_start(out=idt[:], in_=io["ident"]), inc=S["lc"], k=16)
    b.o("sync", lambda E: E.dma_start(out=wq[:], in_=io["wq"].rearrange("(a p) d -> p a d", p=P)),
        inc=S["lw"], k=16)
    b.o("sync", lambda E: E.dma_start(out=wk[:], in_=io["wk"].rearrange("(a p) d -> p a d", p=P)),
        inc=S["lw"], k=16)
    b.o("sync", lambda E: E.dma_start(out=wv[:], in_=io["wvd"].rearrange("(a p) d -> p a d", p=P)),
        inc=S["lw"], k=16)
    b.w("vector", S["lc"])
    b.o("vector", lambda E: E.memset(epsb[:], LN_EPS))

    for i in range(MT // P):
        b.w("sync", S["dv"])
        b.w("sync", S["pe"])
        b.o("sync", lambda E, r0=i * P, i=i: E.dma_start(out=xt[i % 2][:], in_=x2[r0:r0 + P, :]),
            inc=S["lx"], k=16)
        b.w("vector", S["lx"])
        b.w("scalar", S["lx"])
        emit_ln_tile(b, xt[i % 2][:], xm[i % 2][:], sqb, st, epsb[:, 0:1])
        emit_tm2fm(b, xm[i % 2][:], xn, i * P, pT, idt[:], out_r32=True)

    for which, wbt, outd in (("q", wq, io["qf"]), ("k", wk, io["kf"])):
        b.w("tensor", S["lw"])
        b.w("tensor", S["ac"])
        b.w("tensor", S["dv"])
        for dc in range(2):
            for ci in range(MT // 512):
                bank = ps[(dc * 4 + ci) % 4]
                b.w("tensor", S["dv"])
                for kc in range(FC):
                    b.o("tensor", lambda E, dc=dc, ci=ci, kc=kc, bank=bank, wbt=wbt:
                        E.matmul(out=bank[:, 0:512], lhsT=wbt[:, kc, dc * P:(dc + 1) * P],
                                 rhs=r32(xn[:, kc, ci * 512:(ci + 1) * 512]),
                                 start=(kc == 0), stop=(kc == FC - 1)),
                        inc=(S["pe"] if kc == FC - 1 else None))
                b.w("vector", S["pe"])
                b.o("vector", lambda E, dc=dc, ci=ci, bank=bank:
                    E.tensor_copy(out=qkst[:, dc, ci * 512:(ci + 1) * 512], in_=bank[:, 0:512]))
        b.w("sync", S["dv"])
        b.o("sync", lambda E, outd=outd: E.dma_start(
            out=outd.rearrange("(a p) t -> p a t", p=P), in_=qkst[:]), inc=S["stx"], k=16)
        b.w("vector", S["stx"])

    b.w("tensor", S["lw"])
    for half in range(2):
        for i in range(8):
            tt = half * 8 + i
            bank = ps[i % 4]
            b.w("tensor", S["dv"])
            for kc in range(FC):
                b.o("tensor", lambda E, tt=tt, kc=kc, bank=bank:
                    E.matmul(out=bank[:, 0:256], lhsT=r32(xn[:, kc, tt * P:(tt + 1) * P]),
                             rhs=wv[:, kc, :], start=(kc == 0), stop=(kc == FC - 1)),
                    inc=(S["pe"] if kc == FC - 1 else None))
            b.w("vector", S["pe"])
            b.o("vector", lambda E, i=i, bank=bank:
                E.tensor_copy(out=vdst[:, i, :], in_=bank[:, 0:256]))
        b.w("sync", S["dv"])
        b.o("sync", lambda E, half=half: E.dma_start(
            out=io["vdf"][half * 1024:(half + 1) * 1024, :].rearrange("(a p) d -> p a d", p=P),
            in_=vdst[:, 0:8, :]), inc=S["stx"], k=16)
        b.w("vector", S["stx"])
    b.w("sync", S["stx"])


def emit_l1_attn(b, io):
    nc = b.nc
    S = b.S
    a = lambda n_, sh_, dt_: b.sb(n_, sh_, dt_)
    qfs = a("l1a_q", [P, 2, MT], F32R)
    kfs = a("l1a_k", [P, 2, NSEQ], F32R)
    vds = a("l1a_v", [P, NKB1, 256], F32R)
    attn_d = a("l1a_ad", [P, 2, MT], F32)
    wvupb = a("l1a_wu", [P, 2, DM], F32R)
    PTs = a("l1a_pt", [P, 512], F32)
    msk = a("l1a_msk", [P, 512], F32)
    tmpc = a("l1a_tmp", [P, 512], F32)
    mbc = a("l1a_mbc", [P, 512], F32)
    stat = a("l1a_stat", [P, 20], F32)
    qgl = a("l1a_qgl", [P, 16], F32)
    pcol = a("l1a_pc", [P, 1], F32)
    iotab = a("l1a_io", [P, 512], F32)
    qrowb = a("l1a_qr", [P, 4, 512], F32)
    onesr = a("l1a_ones", [1, P], F32R)
    drow = a("l1a_dr", [1, 512], F32R)
    dstg = a("l1a_ds", [4, P], F32)
    idt = a("l1a_id", [P, P], F32)
    epsb = a("l1a_eps", [P, 1], F32)
    xt = [a(f"l1a_xt{i}", [P, DM], F32) for i in range(2)]
    ztm = a("l1a_ztm", [P, DM], F32)
    sqb = a("l1a_sq", [P, DM], F32)
    st = a("l1a_st", [P, 8], F32)
    scr_d = nc.dram_tensor("l1a_scr", [512], F32R, kind="Internal").ap()

    ps = [b.psm(f"l1a_p{i}", [P, 512], F32) for i in range(6)]
    pT = b.psm("l1a_pT", [P, 1024], F32)

    x2 = io["x2"]
    x3 = io["x3"]
    SCL = 1.0 / 16.0

    b.o("sync", lambda E: E.dma_start(out=qfs[:], in_=io["qf"].rearrange("(a p) t -> p a t", p=P)),
        inc=S["lw"], k=16)
    b.o("sync", lambda E: E.dma_start(out=kfs[:], in_=io["kfull"].rearrange("(a p) t -> p a t", p=P)),
        inc=S["lw"], k=16)
    b.o("sync", lambda E: E.dma_start(out=vds[:], in_=io["vdfull"].rearrange("(a p) d -> p a d", p=P)),
        inc=S["lw"], k=16)
    b.o("sync", lambda E: E.dma_start(out=wvupb[:], in_=io["wvup"].rearrange("(a p) d -> p a d", p=P)),
        inc=S["lw"], k=16)
    b.o("sync", lambda E: E.dma_start(out=qgl[:], in_=io["qglob"]), inc=S["lc"], k=16)
    b.o("sync", lambda E: E.dma_start(out=pcol[:], in_=io["pcol"]), inc=S["lc"], k=16)
    b.o("sync", lambda E: E.dma_start(out=iotab[:], in_=io["iotab"]), inc=S["lc"], k=16)
    b.o("sync", lambda E: E.dma_start(out=qrowb[:], in_=io["qrowb"]), inc=S["lc"], k=16)
    b.o("sync", lambda E: E.dma_start(out=onesr[:], in_=io["ones"]), inc=S["lc"], k=16)
    b.o("sync", lambda E: E.dma_start(out=idt[:], in_=io["ident"]), inc=S["lc"], k=16)
    b.w("vector", S["lc"])
    b.w("vector", S["lw"])
    b.o("vector", lambda E: E.memset(epsb[:], LN_EPS))

    for qg in range(4):
        for sq_ in range(4):
            qt = qg * 4 + sq_
            b.o("vector", lambda E, sq_=sq_: E.memset(stat[:, sq_:sq_ + 1], 0.0))
            for kb in range(NKB5):
                bank = ps[kb % 2]
                b.w("tensor", S["dv"])
                b.w("tensor", S["ac"])
                for dc in range(2):
                    b.o("tensor", lambda E, qt=qt, kb=kb, dc=dc, bank=bank:
                        E.matmul(out=bank[:, 0:512], lhsT=qfs[:, dc, qt * P:(qt + 1) * P],
                                 rhs=kfs[:, dc, kb * 512:(kb + 1) * 512],
                                 start=(dc == 0), stop=(dc == 1)),
                        inc=(S["pe"] if dc == 1 else None))
                b.o("vector", lambda E, qt=qt, kb=kb: E.tensor_scalar(
                    out=stat[:, 8:9], in0=qgl[:, qt:qt + 1], scalar1=float(-kb * 512),
                    scalar2=None, op0=ALU.add))
                b.o("vector", lambda E: E.tensor_scalar(
                    out=msk[:], in0=iotab[:], scalar1=stat[:, 8:9], scalar2=BIGNEG,
                    op0=ALU.is_gt, op1=ALU.mult))
                b.w("vector", S["pe"])
                b.o("vector", lambda E, bank=bank: E.scalar_tensor_tensor(
                    out=tmpc[:], in0=bank[:, 0:512], scalar=SCL, in1=msk[:],
                    op0=ALU.mult, op1=ALU.add))
                b.w("scalar", S["dv"])
                b.o("scalar", lambda E: E.activation(out=r32(PTs[:]), in_=tmpc[:], func=AF.Exp,
                                                     accum_out=stat[:, 9:10]))
                b.w("vector", S["ac"])
                b.o("vector", lambda E, sq_=sq_: E.tensor_tensor(
                    out=stat[:, sq_:sq_ + 1], in0=stat[:, sq_:sq_ + 1], in1=stat[:, 9:10],
                    op=ALU.add))
        b.o("vector", lambda E: E.reciprocal(out=stat[:, 4:8], in_=stat[:, 0:4]))
        b.w("tensor", S["dv"])
        b.w("tensor", S["ac"])
        b.o("tensor", lambda E: E.transpose(out=pT[0:4, 0:P], in_=stat[:, 4:8], identity=idt[:]),
            inc=S["pe"])
        b.w("vector", S["pe"])
        b.o("vector", lambda E: E.tensor_copy(out=dstg[:], in_=pT[0:4, 0:P]))
        b.w("sync", S["dv"])
        b.o("sync", lambda E: E.dma_start(out=scr_d.rearrange("(a t) -> a t", a=4).bitcast(F32),
                                          in_=dstg[:]), inc=S["lc"], k=16)
        b.w("sync", S["lc"])
        b.o("sync", lambda E: E.dma_start(out=drow[:], in_=scr_d.rearrange("(o t) -> o t", o=1)),
            inc=S["lc"], k=16)
        b.w("tensor", S["lc"])
        b.o("tensor", lambda E: E.matmul(out=pT[:, 0:512], lhsT=onesr[:], rhs=drow[0:1, :],
                                         start=True, stop=True), inc=S["pe"])
        b.w("vector", S["pe"])
        b.o("vector", lambda E: E.tensor_copy(out=mbc[:], in_=pT[:, 0:512]))

        pvb = (ps[2], ps[3])
        for kb in range(NKB1):
            bank = ps[4 + kb % 2]
            b.w("tensor", S["dv"])
            b.w("tensor", S["ac"])
            for dc in range(2):
                b.o("tensor", lambda E, qg=qg, kb=kb, dc=dc, bank=bank:
                    E.matmul(out=bank[:, 0:512], lhsT=kfs[:, dc, kb * P:(kb + 1) * P],
                             rhs=qfs[:, dc, qg * 512:(qg + 1) * 512],
                             start=(dc == 0), stop=(dc == 1)),
                    inc=(S["pe"] if dc == 1 else None))
            b.o("vector", lambda E, kb=kb: E.tensor_scalar(
                out=stat[:, 8:9], in0=pcol[:], scalar1=float(kb * P), scalar2=None, op0=ALU.add))
            b.o("vector", lambda E, qg=qg: E.tensor_scalar(
                out=msk[:], in0=qrowb[:, qg, :], scalar1=stat[:, 8:9], scalar2=None,
                op0=ALU.is_ge))
            b.w("vector", S["pe"])
            b.o("vector", lambda E, bank=bank: E.tensor_single_scalar(
                out=tmpc[:], in_=bank[:, 0:512], scalar=SCL, op=ALU.mult))
            b.w("scalar", S["dv"])
            b.o("scalar", lambda E: E.activation(out=r32(PTs[:]), in_=tmpc[:], func=AF.Exp))
            b.w("vector", S["ac"])
            b.o("vector", lambda E: E.tensor_tensor(out=r32(PTs[:]), in0=PTs[:], in1=msk[:],
                                                    op=ALU.mult))
            b.w("tensor", S["dv"])
            for dc in range(2):
                b.o("tensor", lambda E, kb=kb, dc=dc:
                    E.matmul(out=pvb[dc][:, 0:512], lhsT=vds[:, kb, dc * P:(dc + 1) * P],
                             rhs=r32(PTs[:]), start=(kb == 0), stop=(kb == NKB1 - 1)),
                    inc=(S["pe"] if (kb == NKB1 - 1 and dc == 1) else None))
        b.w("vector", S["pe"])
        for dc in range(2):
            b.o("vector", lambda E, dc=dc, qg=qg: E.tensor_tensor(
                out=r32(attn_d[:, dc, qg * 512:(qg + 1) * 512]), in0=pvb[dc][:, 0:512],
                in1=mbc[:], op=ALU.mult))

    for tt in range(MT // P):
        b.w("vector", S["ac"])
        for nc_ in range(2):
            bank = ps[(tt * 2 + nc_) % 4]
            b.w("tensor", S["dv"])
            b.w("tensor", S["ac"])
            for kc in range(2):
                b.o("tensor", lambda E, tt=tt, nc_=nc_, kc=kc, bank=bank:
                    E.matmul(out=bank[:, 0:512], lhsT=r32(attn_d[:, kc, tt * P:(tt + 1) * P]),
                             rhs=wvupb[:, kc, nc_ * 512:(nc_ + 1) * 512],
                             start=(kc == 0), stop=(kc == 1)),
                    inc=(S["pe"] if kc == 1 else None))
            b.w("vector", S["pe"])
            b.o("vector", lambda E, nc_=nc_, bank=bank: E.tensor_copy(
                out=ztm[:, nc_ * 512:(nc_ + 1) * 512], in_=bank[:, 0:512]))
        emit_ln_tile(b, ztm[:], xt[tt % 2][:], sqb, st, epsb[:, 0:1])
        b.w("sync", S["dv"])
        b.o("sync", lambda E, r0=tt * P: E.dma_start(out=sqb[:], in_=x2[r0:r0 + P, :]),
            inc=S["lx"], k=16)
        b.w("vector", S["lx"])
        b.o("vector", lambda E, tt=tt: E.tensor_tensor(out=xt[tt % 2][:], in0=xt[tt % 2][:],
                                                       in1=sqb[:], op=ALU.add))
        b.w("sync", S["dv"])
        b.o("sync", lambda E, r0=tt * P, tt=tt: E.dma_start(out=x3[r0:r0 + P, :], in_=xt[tt % 2][:]),
            inc=S["stx"], k=16)
        b.w("vector", S["stx"])
        b.w("scalar", S["stx"])
    b.w("sync", S["stx"])



def build_l0f(nc=None, races=True, dbg=False):
    nc = nc or bass.Bass("TRN2", target_bir_lowering=False, debug=False, detect_race_conditions=races)
    io = {
        "x1": nc.dram_tensor("x1", [MT, DM], F32, kind="ExternalInput").ap(),
        "routerW": nc.dram_tensor("routerW", [2, DM, 6], F32R, kind="ExternalInput").ap(),
        "rb": nc.dram_tensor("rb", [2, 6], F32R, kind="ExternalInput").ap(),
        "gb": nc.dram_tensor("gb", [2, 6], F32R, kind="ExternalInput").ap(),
        "normw": nc.dram_tensor("normw", [2, DM], F32R, kind="ExternalInput").ap(),
        "shWup": nc.dram_tensor("shWup", [DM, 2 * DM], F32R, kind="ExternalInput").ap(),
        "shWg": nc.dram_tensor("shWg", [DM, DM], F32R, kind="ExternalInput").ap(),
        "shWdown": nc.dram_tensor("shWdown", [DM, DM], F32R, kind="ExternalInput").ap(),
        "expWup": nc.dram_tensor("expWup", [6, DM, DM], F32R, kind="ExternalInput").ap(),
        "expWg": nc.dram_tensor("expWg", [6, DM, DM], F32R, kind="ExternalInput").ap(),
        "expWdown": nc.dram_tensor("expWdown", [6, DM, DM], F32R, kind="ExternalInput").ap(),
        "bup": nc.dram_tensor("bup", [P, 64], F32, kind="ExternalInput").ap(),
        "bg": nc.dram_tensor("bg", [P, 64], F32, kind="ExternalInput").ap(),
        "bdall": nc.dram_tensor("bdall", [1, NUNIT * DM], F32R, kind="ExternalInput").ap(),
        "ones": nc.dram_tensor("ones", [1, P], F32R, kind="ExternalInput").ap(),
        "ident": nc.dram_tensor("ident", [P, P], F32, kind="ExternalInput").ap(),
        "x2": nc.dram_tensor("x2", [MT, DM], F32, kind="ExternalOutput").ap(),
    }
    if dbg:
        io["dbg"] = [nc.dram_tensor(f"dbg{i}", [P, DM], F32, kind="ExternalOutput").ap() for i in range(12)]
    b = B(nc)
    b.begin()
    emit_l0_ffn(b, io)
    b.end()
    b.emit()
    return nc


def l0f_shared_inputs(z):
    f = np.float32
    bup = np.empty((P, 64), f)
    bg = np.empty((P, 64), f)
    bd = np.empty((NUNIT, DM), f)
    sbup = np.asarray(z["L0_ffn_share_bup"], f)
    for u in range(2):
        bup[:, u * 8:(u + 1) * 8] = sbup[u * DM:(u + 1) * DM].reshape(FC, P).T
        bg[:, u * 8:(u + 1) * 8] = np.asarray(z["L0_ffn_share_bg"], f).reshape(FC, P).T
        bd[u] = np.asarray(z["L0_ffn_share_bdown"], f)
    for e in range(6):
        bup[:, (e + 2) * 8:(e + 3) * 8] = np.asarray(z["L0_ffn_exp_bup"], f)[e].reshape(FC, P).T
        bg[:, (e + 2) * 8:(e + 3) * 8] = np.asarray(z["L0_ffn_exp_bg"], f)[e].reshape(FC, P).T
        bd[e + 2] = np.asarray(z["L0_ffn_exp_bdown"], f)[e]
    return {
        "routerW": np.ascontiguousarray(np.asarray(z["L0_ffn_router_W"], f)),
        "rb": np.ascontiguousarray(np.asarray(z["L0_ffn_router_b"], f)),
        "gb": np.ascontiguousarray(np.asarray(z["L0_ffn_gate_bias"], f)),
        "normw": np.ascontiguousarray(np.asarray(z["L0_ffn_norm_w"], f)),
        "shWup": np.ascontiguousarray(np.asarray(z["L0_ffn_share_Wup"], f)),
        "shWg": np.ascontiguousarray(np.asarray(z["L0_ffn_share_Wg"], f)),
        "shWdown": np.ascontiguousarray(np.asarray(z["L0_ffn_share_Wdown"], f)),
        "expWup": np.ascontiguousarray(np.asarray(z["L0_ffn_exp_Wup"], f)),
        "expWg": np.ascontiguousarray(np.asarray(z["L0_ffn_exp_Wg"], f)),
        "expWdown": np.ascontiguousarray(np.asarray(z["L0_ffn_exp_Wdown"], f)),
        "bup": bup, "bg": bg, "bdall": bd.reshape(1, -1).copy(),
        "ones": np.ones((1, P), f),
        "ident": np.eye(P, dtype=f),
    }


def build_l1q(nc=None, races=True):
    nc = nc or bass.Bass("TRN2", target_bir_lowering=False, debug=False, detect_race_conditions=races)
    io = {
        "x2": nc.dram_tensor("x2", [MT, DM], F32, kind="ExternalInput").ap(),
        "wq": nc.dram_tensor("wq", [DM, 256], F32R, kind="ExternalInput").ap(),
        "wk": nc.dram_tensor("wk", [DM, 256], F32R, kind="ExternalInput").ap(),
        "wvd": nc.dram_tensor("wvd", [DM, 256], F32R, kind="ExternalInput").ap(),
        "ident": nc.dram_tensor("ident", [P, P], F32, kind="ExternalInput").ap(),
        "qf": nc.dram_tensor("qf", [256, MT], F32, kind="ExternalOutput").ap(),
        "kf": nc.dram_tensor("kf", [256, MT], F32, kind="ExternalOutput").ap(),
        "vdf": nc.dram_tensor("vdf", [MT, 256], F32, kind="ExternalOutput").ap(),
    }
    b = B(nc)
    b.begin()
    emit_l1_qkv(b, io)
    b.end()
    b.emit()
    return nc


def build_l1a(nc=None, races=True):
    nc = nc or bass.Bass("TRN2", target_bir_lowering=False, debug=False, detect_race_conditions=races)
    io = {
        "x2": nc.dram_tensor("x2", [MT, DM], F32, kind="ExternalInput").ap(),
        "qf": nc.dram_tensor("qf", [256, MT], F32R, kind="ExternalInput").ap(),
        "kfull": nc.dram_tensor("kfull", [256, NSEQ], F32R, kind="ExternalInput").ap(),
        "vdfull": nc.dram_tensor("vdfull", [NSEQ, 256], F32R, kind="ExternalInput").ap(),
        "wvup": nc.dram_tensor("wvup", [256, DM], F32R, kind="ExternalInput").ap(),
        "qglob": nc.dram_tensor("qglob", [P, 16], F32, kind="ExternalInput").ap(),
        "pcol": nc.dram_tensor("pcol", [P, 1], F32, kind="ExternalInput").ap(),
        "iotab": nc.dram_tensor("iotab", [P, 512], F32, kind="ExternalInput").ap(),
        "qrowb": nc.dram_tensor("qrowb", [P, 4, 512], F32, kind="ExternalInput").ap(),
        "ones": nc.dram_tensor("ones", [1, P], F32R, kind="ExternalInput").ap(),
        "ident": nc.dram_tensor("ident", [P, P], F32, kind="ExternalInput").ap(),
        "x3": nc.dram_tensor("x3", [MT, DM], F32, kind="ExternalOutput").ap(),
    }
    b = B(nc)
    b.begin()
    emit_l1_attn(b, io)
    b.end()
    b.emit()
    return nc


def l1a_consts(c):
    qpos = c % 4
    f = np.float32
    qglob = (qpos * MT + np.arange(16)[None, :] * P + np.arange(P)[:, None]).astype(f)
    pcol = np.arange(P, dtype=f).reshape(P, 1)
    iotab = np.broadcast_to(np.arange(512, dtype=f), (P, 512)).copy()
    qrowb = np.broadcast_to((qpos * MT + np.arange(4)[:, None] * 512 +
                             np.arange(512)[None, :]).astype(f), (P, 4, 512)).copy()
    return {"qglob": qglob, "pcol": pcol, "iotab": iotab, "qrowb": qrowb,
            "ones": np.ones((1, P), f), "ident": np.eye(P, dtype=f)}



def emit_l1_ffn(b, io):
    nc = b.nc
    S = b.S
    a = lambda n_, sh_, dt_: b.sb(n_, sh_, dt_)

    xf = a("l1f_xf", [P, FC, HB], F32)
    hb = a("l1f_h", [P, FC, HB], F32)
    h2b = a("l1f_h2", [P, FC, HB], F32)
    zb = a("l1f_z", [P, 8, HB], F32)
    wbuf = a("l1f_wb", [P, FC, DM], F32R)
    rwb = a("l1f_rw", [P, FC, 8], F32R)
    wgt = a("l1f_wgt", [P, 128], F32)
    xt = [a(f"l1f_xt{i}", [P, DM], F32) for i in range(2)]
    xm = [a(f"l1f_xm{i}", [P, DM], F32) for i in range(2)]
    sqb = a("l1f_sq", [P, DM], F32)
    st = a("l1f_st", [P, 8], F32)
    idt = a("l1f_id", [P, P], F32)
    epsb = a("l1f_eps", [P, 1], F32)
    gtile = a("l1f_gt", [P, 8], F32)

    ps = [b.psm(f"l1f_p{i}", [P, 512], F32) for i in range(6)]
    pT = b.psm("l1f_pT", [P, 1024], F32)

    x3 = io["x3"]
    x4 = io["x4"]

    b.o("sync", lambda E: E.dma_start(out=idt[:], in_=io["ident"]), inc=S["lc"], k=16)
    b.o("sync", lambda E: E.dma_start(out=rwb[:], in_=io["router"].rearrange("(a p) e -> p a e", p=P)),
        inc=S["lw"], k=16)
    b.w("vector", S["lc"])
    b.o("vector", lambda E: E.memset(epsb[:], LN_EPS))

    for h in (0, 1):
        t0 = h * HB

        for i in range(HB // P):
            b.w("sync", S["dv"])
            b.w("sync", S["pe"])
            b.o("sync", lambda E, r0=t0 + i * P, i=i: E.dma_start(out=xt[i % 2][:], in_=x3[r0:r0 + P, :]),
                inc=S["lx"], k=16)
            b.w("vector", S["lx"])
            b.w("scalar", S["lx"])
            emit_ln_tile(b, xt[i % 2][:], xm[i % 2][:], sqb, st, epsb[:, 0:1])
            emit_tm2fm(b, xm[i % 2][:], xf, i * P, pT, idt[:], out_r32=True)

        for tt in range(HB // P):
            b.w("tensor", S["lw"])
            b.w("tensor", S["dv"])
            b.w("tensor", S["ac"])
            for kc in range(FC):
                b.o("tensor", lambda E, kc=kc, tt=tt:
                    E.matmul(out=ps[4][:, 0:8], lhsT=r32(xf[:, kc, tt * P:(tt + 1) * P]),
                             rhs=rwb[:, kc, :], start=(kc == 0), stop=(kc == FC - 1)),
                    inc=(S["pe"] if kc == FC - 1 else None))
            b.w("vector", S["pe"])
            b.o("vector", lambda E: E.tensor_copy(out=gtile[:], in_=ps[4][:, 0:8]))
            b.o("vector", lambda E: E.reduce_max(out=st[:, 0:1], in_=gtile[:], axis=AX))
            b.o("vector", lambda E: E.tensor_single_scalar(out=st[:, 1:2], in_=st[:, 0:1],
                                                           scalar=-1.0, op=ALU.mult))
            b.w("scalar", S["dv"])
            b.o("scalar", lambda E: E.activation(out=sqb[:, 0:8], in_=gtile[:], func=AF.Exp,
                                                 bias=st[:, 1:2], accum_out=st[:, 2:3]))
            b.w("vector", S["ac"])
            b.o("vector", lambda E: E.reciprocal(out=st[:, 3:4], in_=st[:, 2:3]))
            b.o("vector", lambda E: E.tensor_single_scalar(out=sqb[:, 0:8], in_=sqb[:, 0:8],
                                                           scalar=st[:, 3:4], op=ALU.mult))
            b.o("vector", lambda E: E.reduce_max(out=st[:, 4:5], in_=sqb[:, 0:8], axis=AX))
            b.o("vector", lambda E: E.tensor_scalar(out=sqb[:, 8:16], in0=sqb[:, 0:8],
                                                    scalar1=st[:, 4:5], scalar2=BIGNEG,
                                                    op0=ALU.is_ge, op1=ALU.mult))
            b.o("vector", lambda E: E.tensor_tensor(out=sqb[:, 8:16], in0=sqb[:, 0:8],
                                                    in1=sqb[:, 8:16], op=ALU.add))
            b.o("vector", lambda E: E.reduce_max(out=st[:, 5:6], in_=sqb[:, 8:16], axis=AX))
            b.o("vector", lambda E: E.tensor_scalar(out=sqb[:, 8:16], in0=sqb[:, 0:8],
                                                    scalar1=st[:, 5:6], scalar2=None, op0=ALU.is_ge))
            b.o("vector", lambda E, tt=tt: E.tensor_tensor(out=wgt[:, tt * 8:(tt + 1) * 8],
                                                           in0=sqb[:, 0:8], in1=sqb[:, 8:16],
                                                           op=ALU.mult))

        for u in range(9):
            is_sh = u == 0
            e = u - 1
            nI = FC if is_sh else 4
            b.w("sync", S["pe"])
            b.w("sync", S["ac"])
            b.w("sync", S["dv"])
            wsrc = (io["shg"] if is_sh else io["eg"][e]).rearrange("(a p) d -> p a d", p=P)
            b.o("sync", lambda E, wsrc=wsrc, nI=nI: E.dma_start(out=wbuf[:, :, 0:nI * P], in_=wsrc),
                inc=S["lw"], k=16)
            for ci in range(HB // 512):
                def evac_gate(mo, ci=ci):
                    b.o("scalar", lambda E, mo=mo, ci=ci:
                        E.activation(out=hb[:, mo, ci * 512:(ci + 1) * 512],
                                     in_=ps[mo % 4][:, 0:512], func=AF.Silu))
                emit_proj_rot(b, wbuf, xf, ci * 512, ps, evac_gate, nko=nI)
            b.w("sync", S["pe"])
            b.w("sync", S["ac"])
            b.w("sync", S["dv"])
            wsrc = (io["shu"] if is_sh else io["eu"][e]).rearrange("(a p) d -> p a d", p=P)
            b.o("sync", lambda E, wsrc=wsrc, nI=nI: E.dma_start(out=wbuf[:, :, 0:nI * P], in_=wsrc),
                inc=S["lw"], k=16)
            for ci in range(HB // 512):
                def evac_up(mo, ci=ci):
                    b.w("vector", S["ac"])
                    b.o("vector", lambda E, mo=mo, ci=ci:
                        E.tensor_tensor(out=r32(h2b[:, mo, ci * 512:(ci + 1) * 512]),
                                        in0=ps[mo % 4][:, 0:512],
                                        in1=hb[:, mo, ci * 512:(ci + 1) * 512], op=ALU.mult))
                emit_proj_rot(b, wbuf, xf, ci * 512, ps, evac_up, nko=nI)
            b.w("sync", S["pe"])
            b.w("sync", S["ac"])
            b.w("sync", S["dv"])
            wsrc = (io["shd"] if is_sh else io["ed"][e]).rearrange("(a p) d -> p a d", p=P)
            b.o("sync", lambda E, wsrc=wsrc, nI=nI: E.dma_start(out=wbuf[:, 0:nI, :], in_=wsrc),
                inc=S["lw"], k=16)
            b.w("tensor", S["lw"])
            for tt in range(HB // P):
                for nc_ in range(2):
                    bank = ps[(tt * 2 + nc_) % 4]
                    b.w("tensor", S["dv"])
                    for kc in range(nI):
                        b.o("tensor", lambda E, tt=tt, nc_=nc_, kc=kc, bank=bank, nI=nI:
                            E.matmul(out=bank[:, 0:512],
                                     lhsT=r32(h2b[:, kc, tt * P:(tt + 1) * P]),
                                     rhs=wbuf[:, kc, nc_ * 512:(nc_ + 1) * 512],
                                     start=(kc == 0), stop=(kc == nI - 1)),
                            inc=(S["pe"] if kc == nI - 1 else None))
                    b.w("vector", S["pe"])
                    if is_sh:
                        b.o("vector", lambda E, tt=tt, nc_=nc_, bank=bank:
                            E.tensor_copy(out=zb[:, tt, nc_ * 512:(nc_ + 1) * 512],
                                          in_=bank[:, 0:512]))
                    else:
                        b.o("vector", lambda E, tt=tt, nc_=nc_, e=e, bank=bank:
                            E.scalar_tensor_tensor(out=zb[:, tt, nc_ * 512:(nc_ + 1) * 512],
                                                   in0=bank[:, 0:512],
                                                   scalar=wgt[:, tt * 8 + e:tt * 8 + e + 1],
                                                   in1=zb[:, tt, nc_ * 512:(nc_ + 1) * 512],
                                                   op0=ALU.mult, op1=ALU.add))

        for tt in range(HB // P):
            b.w("vector", S["ac"])
            emit_ln_tile(b, zb[:, tt, :], xt[tt % 2][:], sqb, st, epsb[:, 0:1])
            b.w("sync", S["dv"])
            b.o("sync", lambda E, r0=t0 + tt * P: E.dma_start(out=sqb[:], in_=x3[r0:r0 + P, :]),
                inc=S["lx"], k=16)
            b.w("vector", S["lx"])
            b.o("vector", lambda E, tt=tt: E.tensor_tensor(out=xt[tt % 2][:], in0=xt[tt % 2][:],
                                                           in1=sqb[:], op=ALU.add))
            b.w("sync", S["dv"])
            b.o("sync", lambda E, r0=t0 + tt * P, tt=tt:
                E.dma_start(out=x4[r0:r0 + P, :], in_=xt[tt % 2][:]), inc=S["stx"], k=16)
            b.w("vector", S["stx"])
            b.w("scalar", S["stx"])
    b.w("sync", S["stx"])


def build_l1f(nc=None, races=True):
    nc = nc or bass.Bass("TRN2", target_bir_lowering=False, debug=False, detect_race_conditions=races)
    io = {
        "x3": nc.dram_tensor("x3", [MT, DM], F32, kind="ExternalInput").ap(),
        "router": nc.dram_tensor("router", [DM, 8], F32R, kind="ExternalInput").ap(),
        "shg": nc.dram_tensor("shg", [DM, DM], F32R, kind="ExternalInput").ap(),
        "shu": nc.dram_tensor("shu", [DM, DM], F32R, kind="ExternalInput").ap(),
        "shd": nc.dram_tensor("shd", [DM, DM], F32R, kind="ExternalInput").ap(),
        "eg": nc.dram_tensor("eg", [8, DM, 512], F32R, kind="ExternalInput").ap(),
        "eu": nc.dram_tensor("eu", [8, DM, 512], F32R, kind="ExternalInput").ap(),
        "ed": nc.dram_tensor("ed", [8, 512, DM], F32R, kind="ExternalInput").ap(),
        "ident": nc.dram_tensor("ident", [P, P], F32, kind="ExternalInput").ap(),
        "x4": nc.dram_tensor("x4", [MT, DM], F32, kind="ExternalOutput").ap(),
    }
    b = B(nc)
    b.begin()
    emit_l1_ffn(b, io)
    b.end()
    b.emit()
    return nc




def build_fused1(races=True):
    nc = bass.Bass("TRN2", target_bir_lowering=False, debug=False, detect_race_conditions=races)
    dt_ = nc.dram_tensor
    ident = dt_("ident", [P, P], F32, kind="ExternalInput").ap()
    ones = dt_("ones", [1, P], F32R, kind="ExternalInput").ap()
    x1 = dt_("x1", [MT, DM], F32, kind="Internal").ap()
    x2 = dt_("x2", [MT, DM], F32, kind="ExternalOutput").ap()
    ioA = {
        "x_ext": dt_("x_ext", [TEXT, DM], F32, kind="ExternalInput").ap(),
        "wkT": dt_("wkT", [DM, DM], F32R, kind="ExternalInput").ap(),
        "wvT": dt_("wvT", [DM, DM], F32R, kind="ExternalInput").ap(),
        "wrT": dt_("wrT", [DM, DM], F32R, kind="ExternalInput").ap(),
        "woT": dt_("woT", [DM, DM], F32R, kind="ExternalInput").ap(),
        "tmk": dt_("tmk", [P, FC], F32, kind="ExternalInput").ap(),
        "tmv": dt_("tmv", [P, FC], F32, kind="ExternalInput").ap(),
        "tmr": dt_("tmr", [P, FC], F32, kind="ExternalInput").ap(),
        "td": dt_("td", [P, FC], F32, kind="ExternalInput").ap(),
        "tf": dt_("tf", [P, FC], F32, kind="ExternalInput").ap(),
        "g": dt_("g", [P, FC], F32, kind="ExternalInput").ap(),
        "bv": dt_("bv", [P, FC], F32R, kind="ExternalInput").ap(),
        "ident": ident, "x1": x1,
    }
    ioB = {
        "x1": x1, "x2": x2, "ident": ident, "ones": ones,
        "routerW": dt_("routerW", [2, DM, 6], F32R, kind="ExternalInput").ap(),
        "rb": dt_("rb", [2, 6], F32R, kind="ExternalInput").ap(),
        "gb": dt_("gb", [2, 6], F32R, kind="ExternalInput").ap(),
        "normw": dt_("normw", [2, DM], F32R, kind="ExternalInput").ap(),
        "shWup": dt_("shWup", [DM, 2 * DM], F32R, kind="ExternalInput").ap(),
        "shWg": dt_("shWg", [DM, DM], F32R, kind="ExternalInput").ap(),
        "shWdown": dt_("shWdown", [DM, DM], F32R, kind="ExternalInput").ap(),
        "expWup": dt_("expWup", [6, DM, DM], F32R, kind="ExternalInput").ap(),
        "expWg": dt_("expWg", [6, DM, DM], F32R, kind="ExternalInput").ap(),
        "expWdown": dt_("expWdown", [6, DM, DM], F32R, kind="ExternalInput").ap(),
        "bup": dt_("bup", [P, 64], F32, kind="ExternalInput").ap(),
        "bg": dt_("bg", [P, 64], F32, kind="ExternalInput").ap(),
        "bdall": dt_("bdall", [1, NUNIT * DM], F32R, kind="ExternalInput").ap(),
    }
    ioC = {
        "x2": x2, "ident": ident,
        "wq": dt_("wq", [DM, 256], F32R, kind="ExternalInput").ap(),
        "wk": dt_("wk", [DM, 256], F32R, kind="ExternalInput").ap(),
        "wvd": dt_("wvd", [DM, 256], F32R, kind="ExternalInput").ap(),
        "qf": dt_("qf", [256, MT], F32, kind="ExternalOutput").ap(),
        "kf": dt_("kf", [256, MT], F32, kind="ExternalOutput").ap(),
        "vdf": dt_("vdf", [MT, 256], F32, kind="ExternalOutput").ap(),
    }
    b = B(nc)
    b.begin(); emit_l0_attn(b, ioA); b.end()
    b.begin(); emit_l0_ffn(b, ioB); b.end()
    b.begin(); emit_l1_qkv(b, ioC); b.end()
    b.emit()
    return nc


def build_fused2(races=True):
    nc = bass.Bass("TRN2", target_bir_lowering=False, debug=False, detect_race_conditions=races)
    dt_ = nc.dram_tensor
    ident = dt_("ident", [P, P], F32, kind="ExternalInput").ap()
    x3 = dt_("x3", [MT, DM], F32, kind="Internal").ap()
    ioD = {
        "x2": dt_("x2", [MT, DM], F32, kind="ExternalInput").ap(),
        "qf": dt_("qf", [256, MT], F32R, kind="ExternalInput").ap(),
        "kfull": dt_("kfull", [256, NSEQ], F32R, kind="ExternalInput").ap(),
        "vdfull": dt_("vdfull", [NSEQ, 256], F32R, kind="ExternalInput").ap(),
        "wvup": dt_("wvup", [256, DM], F32R, kind="ExternalInput").ap(),
        "qglob": dt_("qglob", [P, 16], F32, kind="ExternalInput").ap(),
        "pcol": dt_("pcol", [P, 1], F32, kind="ExternalInput").ap(),
        "iotab": dt_("iotab", [P, 512], F32, kind="ExternalInput").ap(),
        "qrowb": dt_("qrowb", [P, 4, 512], F32, kind="ExternalInput").ap(),
        "ones": dt_("ones", [1, P], F32R, kind="ExternalInput").ap(),
        "ident": ident, "x3": x3,
    }
    ioE = {
        "x3": x3, "ident": ident,
        "router": dt_("router", [DM, 8], F32R, kind="ExternalInput").ap(),
        "shg": dt_("shg", [DM, DM], F32R, kind="ExternalInput").ap(),
        "shu": dt_("shu", [DM, DM], F32R, kind="ExternalInput").ap(),
        "shd": dt_("shd", [DM, DM], F32R, kind="ExternalInput").ap(),
        "eg": dt_("eg", [8, DM, 512], F32R, kind="ExternalInput").ap(),
        "eu": dt_("eu", [8, DM, 512], F32R, kind="ExternalInput").ap(),
        "ed": dt_("ed", [8, 512, DM], F32R, kind="ExternalInput").ap(),
        "x4": dt_("x4", [MT, DM], F32, kind="ExternalOutput").ap(),
    }
    b = B(nc)
    b.begin(); emit_l1_attn(b, ioD); b.end()
    b.begin(); emit_l1_ffn(b, ioE); b.end()
    b.emit()
    return nc


_BUILD_CACHE = {}


def _get(name, fn):
    if name not in _BUILD_CACHE:
        _BUILD_CACHE[name] = fn()
    return _BUILD_CACHE[name]


def kernel(**inputs):
    f = np.float32
    z = inputs
    x = np.asarray(z["L0_attn_x"], f)
    cores = list(range(8))
    idm = np.eye(P, dtype=f)
    onesv = np.ones((1, P), f)

    nc1 = _get("F1", build_fused1)
    sh_b = l0f_shared_inputs(z)
    del sh_b["ident"], sh_b["ones"]
    wq = np.ascontiguousarray(np.asarray(z["L1_attn_Wq"], f))
    wkq = np.ascontiguousarray(np.asarray(z["L1_attn_Wk"], f))
    wvd = np.ascontiguousarray(np.asarray(z["L1_attn_Wvdown"], f))
    ins1 = []
    for c in cores:
        d = l0a_core_inputs(c, x,
                            np.asarray(z["L0_attn_Wk"], f), np.asarray(z["L0_attn_Wv"], f),
                            np.asarray(z["L0_attn_Wr"], f), np.asarray(z["L0_attn_Wo"], f),
                            np.asarray(z["L0_attn_time_mix_k"], f), np.asarray(z["L0_attn_time_mix_v"], f),
                            np.asarray(z["L0_attn_time_mix_r"], f), np.asarray(z["L0_attn_time_decay"], f),
                            np.asarray(z["L0_attn_time_first"], f), np.asarray(z["L0_attn_ln_g"], f),
                            np.asarray(z["L0_attn_ln_b"], f))
        d.update(sh_b)
        d.update({"ones": onesv, "wq": wq, "wk": wkq, "wvd": wvd})
        ins1.append(d)
    res = run_bass_kernel_spmd(nc1, ins1, core_ids=cores)
    x2 = [np.asarray(res.results[c]["x2"]) for c in cores]
    qf = [np.asarray(res.results[c]["qf"]) for c in cores]
    kf = [np.asarray(res.results[c]["kf"]) for c in cores]
    vdf = [np.asarray(res.results[c]["vdf"]) for c in cores]

    kfull = [np.ascontiguousarray(np.concatenate([kf[s * 4 + q] for q in range(4)], axis=1))
             for s in range(2)]
    vdfull = [np.ascontiguousarray(np.concatenate([vdf[s * 4 + q] for q in range(4)], axis=0))
              for s in range(2)]

    nc2 = _get("F2", build_fused2)
    wvu = np.ascontiguousarray(np.asarray(z["L1_attn_Wvup"], f))
    sh_e = {
        "router": np.ascontiguousarray(np.asarray(z["L1_ffn_router"], f)),
        "shg": np.ascontiguousarray(np.asarray(z["L1_ffn_shared_gate"], f)),
        "shu": np.ascontiguousarray(np.asarray(z["L1_ffn_shared_up"], f)),
        "shd": np.ascontiguousarray(np.asarray(z["L1_ffn_shared_down"], f)),
        "eg": np.ascontiguousarray(np.asarray(z["L1_ffn_experts_gate"], f)),
        "eu": np.ascontiguousarray(np.asarray(z["L1_ffn_experts_up"], f)),
        "ed": np.ascontiguousarray(np.asarray(z["L1_ffn_experts_down"], f)),
    }
    ins2 = []
    for c in cores:
        d = l1a_consts(c)
        d.update(sh_e)
        d.update({"x2": x2[c], "qf": qf[c], "kfull": kfull[c // 4],
                  "vdfull": vdfull[c // 4], "wvup": wvu})
        ins2.append(d)
    res = run_bass_kernel_spmd(nc2, ins2, core_ids=cores)
    out = np.empty((2, 8192, DM), f)
    for c in cores:
        seq, q = c // 4, c % 4
        out[seq, q * MT:(q + 1) * MT] = np.asarray(res.results[c]["x4"])
    return out


# version 04
# speedup vs baseline: 1.4360x; 1.4360x over previous
import numpy as np
import concourse.bass as bass
import concourse.mybir as mybir
from concourse.bass_utils import run_bass_kernel_spmd

F32 = mybir.dt.float32
F32R = mybir.dt.float32r
AF = mybir.ActivationFunctionType
ALU = mybir.AluOpType
AX = mybir.AxisListType.X

P = 128
DM = 1024
FC = DM // P
MT = 2048
WARM = 384
TEXT = MT + WARM
LN_EPS = 1e-5

ENGS = ("sync", "tensor", "vector", "scalar", "gpsimd")


class Sem:
    __slots__ = ("h", "n")

    def __init__(self, nc, name):
        self.h = nc.alloc_semaphore(name)
        self.n = 0


AUTO_GUARD = {"vector": "dv", "scalar": "ac", "gpsimd": "gp"}


class B:

    def __init__(self, nc):
        self.nc = nc
        self.q = {e: [] for e in ENGS}
        self.S = {k: Sem(nc, "s_" + k) for k in
                  ("lc", "lx", "lw", "lw2", "stx", "pe", "dv", "ac", "gp")}
        self._es = None

    def begin(self):
        from contextlib import ExitStack
        self._es = ExitStack()

    def sb(self, name, shape, dtype):
        return self._es.enter_context(self.nc.sbuf_tensor(name, shape, dtype))

    def psm(self, name, shape, dtype):
        return self._es.enter_context(self.nc.psum_tensor(name, shape, dtype))

    def end(self):
        for e in ENGS:
            for k in ("dv", "ac", "pe", "gp", "lc", "lx", "lw", "stx"):
                self.w(e, self.S[k])
        self._es.close()
        self._es = None

    def o(self, eng, fn, inc=None, k=1, dma=False):
        incs = []
        if eng in AUTO_GUARD and not dma:
            au = self.S[AUTO_GUARD[eng]]
            if au.n > 0:
                self.q[eng].append((("w", au.h, au.n), ()))
            incs.append((au.h, 1))
            au.n += 1
        elif inc is not None:
            inc.n += k
            incs.append((inc.h, k))
        self.q[eng].append((fn, tuple(incs)))

    def w(self, eng, sem, n=None):
        c = sem.n if n is None else n
        if c > 0:
            self.q[eng].append((("w", sem.h, c), ()))

    def emit(self):
        with self.nc.Block() as blk:
            for e in ENGS:
                ops = self.q[e]
                if not ops:
                    continue

                def runner(E, ops=ops):
                    for fn, incs in ops:
                        if isinstance(fn, tuple):
                            E.wait_ge(fn[1], fn[2])
                        else:
                            ins = fn(E)
                            for semh, k in incs:
                                ins = ins.then_inc(semh, k)

                getattr(blk, e)(runner)


def r32(ap):
    return ap.bitcast(F32R)



def emit_ln_tile(b, src_tile, dst_tile, sqb, st, eps_ap):
    S = b.S
    b.w("scalar", S["dv"])
    b.o("scalar", lambda E: E.activation(out=sqb[:], in_=src_tile, func=AF.Square), inc=S["ac"])
    b.w("vector", S["ac"])
    b.o("vector", lambda E: E.reduce_sum(out=st[:, 0:1], in_=src_tile, axis=AX))
    b.o("vector", lambda E: E.reduce_sum(out=st[:, 1:2], in_=sqb[:], axis=AX))
    b.o("vector", lambda E: E.tensor_tensor(out=st[:, 2:3], in0=st[:, 0:1], in1=st[:, 0:1], op=ALU.mult))
    b.o("vector", lambda E: E.scalar_tensor_tensor(out=st[:, 3:4], in0=st[:, 2:3], scalar=1.0 / DM,
                                                   in1=st[:, 1:2], op0=ALU.mult, op1=ALU.subtract),
        inc=S["dv"])
    b.w("scalar", S["dv"])
    b.o("scalar", lambda E: E.activation(out=st[:, 4:5], in_=st[:, 3:4], func=AF.Sqrt,
                                         scale=-1.0 / DM, bias=eps_ap), inc=S["ac"])
    b.w("vector", S["ac"])
    b.o("vector", lambda E: E.reciprocal(out=st[:, 5:6], in_=st[:, 4:5]))
    b.o("vector", lambda E: E.scalar_tensor_tensor(out=st[:, 6:7], in0=st[:, 0:1], scalar=-1.0 / DM,
                                                   in1=st[:, 5:6], op0=ALU.mult, op1=ALU.mult))
    b.o("vector", lambda E: E.tensor_scalar(out=dst_tile, in0=src_tile, scalar1=st[:, 5:6],
                                            scalar2=st[:, 6:7], op0=ALU.mult, op1=ALU.add),
        inc=S["dv"])


def emit_tm2fm(b, tm_tile, fm_tensor, col, pT, idt, out_r32=False):
    S = b.S
    b.w("tensor", S["dv"])
    b.w("tensor", S["ac"])
    for fc in range(FC):
        b.o("tensor", lambda E, fc=fc: E.transpose(out=pT[:, fc * P:(fc + 1) * P],
                                                   in_=tm_tile[:, fc * P:(fc + 1) * P],
                                                   identity=idt),
            inc=(S["pe"] if fc == FC - 1 else None))
    b.w("scalar", S["pe"])
    dst = fm_tensor[:, :, col:col + P]
    if out_r32:
        dst = dst.bitcast(F32R)
    b.o("scalar", lambda E, dst=dst: E.copy(out=dst,
                                   in_=pT[:].rearrange("p (a d) -> p a d", d=P)), inc=S["ac"])


def emit_fm2tm(b, fm_tensor, col, tm_tile, pT, idt):
    S = b.S
    b.w("tensor", S["dv"])
    b.w("tensor", S["ac"])
    for fc in range(FC):
        b.o("tensor", lambda E, fc=fc: E.transpose(out=pT[:, fc * P:(fc + 1) * P],
                                                   in_=fm_tensor[:, fc, col:col + P],
                                                   identity=idt),
            inc=(S["pe"] if fc == FC - 1 else None))
    b.w("scalar", S["pe"])
    b.o("scalar", lambda E: E.copy(out=tm_tile, in_=pT[:]), inc=S["ac"])


def emit_proj(b, wb, rhs_fm, rhs_c0, cw, banks, evac, nko=FC, kfc=None):
    S = b.S
    b.w("tensor", S["lw"])
    b.w("tensor", S["dv"])
    b.w("tensor", S["ac"])
    kr = range(FC) if kfc is None else kfc
    kl = list(kr)
    for mo in range(nko):
        for j, kc in enumerate(kl):
            b.o("tensor", lambda E, mo=mo, kc=kc, j=j:
                E.matmul(out=banks[mo][:, 0:cw], lhsT=r32(wb[:, kc, mo * P:(mo + 1) * P]),
                         rhs=r32(rhs_fm[:, kc, rhs_c0:rhs_c0 + cw]),
                         start=(j == 0), stop=(j == len(kl) - 1)),
                inc=(S["pe"] if (mo == nko - 1 and j == len(kl) - 1) else None))
    b.w("scalar", S["pe"])
    b.w("vector", S["pe"])
    for mo in range(nko):
        evac(mo)



def emit_l0_attn(b, io):
    nc = b.nc
    S = b.S
    a = lambda n_, sh_, dt_: b.sb(n_, sh_, dt_)
    H1 = WARM + MT // 2
    H2 = MT // 2
    HM = MT // 2

    xn = a("l0a_xn", [P, FC, H1 + 1], F32)
    kv = a("l0a_kv", [P, FC, H1], F32)
    rb = a("l0a_rb", [P, FC, HM], F32)
    sb = a("l0a_sb", [P, H1 + 1], F32)
    mix = a("l0a_mix", [P, FC, 512], F32R)
    wb = a("l0a_wb", [P, FC, DM], F32R)
    xt = [a(f"l0a_xt{i}", [P, DM], F32) for i in range(2)]
    xm = [a(f"l0a_xm{i}", [P, DM], F32) for i in range(2)]
    sqb = a("l0a_sq", [P, DM], F32)
    st = a("l0a_st", [P, 8], F32)
    idt = a("l0a_id", [P, P], F32)
    carry = a("l0a_carry", [P, FC], F32)
    V = {n: a("l0a_" + n, [P, FC], F32) for n in
         ("tmk", "tmv", "tmr", "omk", "omv", "omr", "decay", "tf", "g", "bWoT")}
    V["bvec"] = a("l0a_bvec", [P, FC], F32R)
    ztm = a("l0a_ztm", [P, DM], F32)
    epsb = a("l0a_eps", [P, 1], F32)
    zln = a("l0a_zln", [P, DM], F32)
    xold = a("l0a_xold", [P, DM], F32)

    scr_bwo = nc.dram_tensor("l0a_scr_bwo", [DM], F32, kind="Internal").ap()
    ps = [b.psm(f"l0a_p{i}", [P, 512], F32) for i in range(6)]
    pT = b.psm("l0a_pT", [P, 1024], F32)
    banks = [p[:] for p in ps] + [pT[:, 0:512], pT[:, 512:1024]]

    x_ext = io["x_ext"]
    x1 = io["x1"]

    b.o("sync", lambda E: E.dma_start(out=idt[:], in_=io["ident"]), inc=S["lc"], k=16)
    for dst, src in (("tmk", "tmk"), ("tmv", "tmv"), ("tmr", "tmr"),
                     ("decay", "td"), ("tf", "tf"), ("g", "g"), ("bvec", "bv")):
        b.o("sync", lambda E, o=V[dst][:], s_=io[src]: E.dma_start(out=o, in_=s_), inc=S["lc"], k=16)
    b.w("vector", S["lc"])
    for dst, m in (("omk", "tmk"), ("omv", "tmv"), ("omr", "tmr")):
        b.o("vector", lambda E, o=V[dst][:], i=V[m][:]:
            E.tensor_scalar(out=o, in0=i, scalar1=-1.0, scalar2=1.0, op0=ALU.mult, op1=ALU.add))
    b.o("vector", lambda E: E.memset(xn[:, :, 0:1], 0.0))
    b.o("vector", lambda E: E.memset(epsb[:], LN_EPS), inc=S["dv"])
    b.w("scalar", S["lc"])
    b.o("scalar", lambda E: E.activation(out=V["decay"][:], in_=V["decay"][:], func=AF.Exp), inc=S["ac"])

    for h in (0, 1):
        if h == 0:
            Th, m0, tt0, ntt = H1, WARM, 0, H1 // P
        else:
            Th, m0, tt0, ntt = H2, 0, H1 // P, H2 // P

        if h == 1:
            b.w("vector", S["pe"])
            b.w("vector", S["ac"])
            for fc in range(FC):
                b.o("vector", lambda E, fc=fc: E.tensor_copy(out=xn[:, fc, 0:1], in_=xn[:, fc, H1:H1 + 1]),
                    inc=(S["dv"] if fc == FC - 1 else None))

        for i in range(ntt):
            tt = tt0 + i
            b.w("sync", S["dv"])
            b.w("sync", S["pe"])
            b.o("sync", lambda E, o=xt[i % 2][:], r0=tt * P: E.dma_start(out=o, in_=x_ext[r0:r0 + P, :]),
                inc=S["lx"], k=16)
            b.w("vector", S["lx"])
            b.w("scalar", S["lx"])
            emit_ln_tile(b, xt[i % 2][:], xm[i % 2][:], sqb, st, epsb[:, 0:1])
            emit_tm2fm(b, xm[i % 2][:], xn, 1 + i * P, pT, idt[:])

        chunks = []
        c = 0
        while c < Th:
            chunks.append((c, min(512, Th - c)))
            c += 512
        for proj in ("k", "v"):
            wsrc = io["wkT"] if proj == "k" else io["wvT"]
            tmn, omn = ("tmk", "omk") if proj == "k" else ("tmv", "omv")
            b.w("sync", S["pe"])
            b.w("sync", S["ac"])
            b.o("sync", lambda E, s_=wsrc: E.dma_start(out=wb[:], in_=s_.rearrange("(a p) d -> p a d", p=P)),
                inc=S["lw"], k=16)
            for (c0, cw) in chunks:
                b.w("vector", S["ac"])
                b.w("vector", S["pe"])
                for fc in range(FC):
                    b.o("vector", lambda E, fc=fc, c0=c0, cw=cw, omn=omn:
                        E.tensor_scalar(out=mix[:, fc, 0:cw], in0=xn[:, fc, c0:c0 + cw],
                                        scalar1=V[omn][:, fc:fc + 1], scalar2=None, op0=ALU.mult))
                    b.o("vector", lambda E, fc=fc, c0=c0, cw=cw, tmn=tmn:
                        E.scalar_tensor_tensor(out=mix[:, fc, 0:cw], in0=xn[:, fc, 1 + c0:1 + c0 + cw],
                                               scalar=V[tmn][:, fc:fc + 1], in1=mix[:, fc, 0:cw],
                                               op0=ALU.mult, op1=ALU.add),
                        inc=(S["dv"] if fc == FC - 1 else None))

                if proj == "k":
                    def evac(mo, c0=c0, cw=cw):
                        b.o("scalar", lambda E, mo=mo: E.copy(out=kv[:, mo, c0:c0 + cw],
                                                              in_=banks[mo][:, 0:cw]),
                            inc=(S["ac"] if mo == FC - 1 else None))
                else:
                    def evac(mo, c0=c0, cw=cw):
                        b.o("vector", lambda E, mo=mo: E.tensor_tensor(out=kv[:, mo, c0:c0 + cw],
                                                                       in0=banks[mo][:, 0:cw],
                                                                       in1=kv[:, mo, c0:c0 + cw],
                                                                       op=ALU.mult),
                            inc=(S["dv"] if mo == FC - 1 else None))
                emit_proj(b, wb, mix, 0, cw, banks, evac)

        b.w("sync", S["pe"])
        b.w("sync", S["ac"])
        b.o("sync", lambda E: E.dma_start(out=wb[:], in_=io["wrT"].rearrange("(a p) d -> p a d", p=P)),
            inc=S["lw"], k=16)
        for ci in range(HM // 512):
            c0 = m0 + ci * 512
            b.w("vector", S["ac"])
            b.w("vector", S["pe"])
            for fc in range(FC):
                b.o("vector", lambda E, fc=fc, c0=c0:
                    E.tensor_scalar(out=mix[:, fc, 0:512], in0=xn[:, fc, c0:c0 + 512],
                                    scalar1=V["omr"][:, fc:fc + 1], scalar2=None, op0=ALU.mult))
                b.o("vector", lambda E, fc=fc, c0=c0:
                    E.scalar_tensor_tensor(out=mix[:, fc, 0:512], in0=xn[:, fc, 1 + c0:1 + c0 + 512],
                                           scalar=V["tmr"][:, fc:fc + 1], in1=mix[:, fc, 0:512],
                                           op0=ALU.mult, op1=ALU.add),
                    inc=(S["dv"] if fc == FC - 1 else None))

            def evac(mo, ci=ci):
                b.o("scalar", lambda E, mo=mo: E.copy(out=r32(rb[:, mo, ci * 512:(ci + 1) * 512]),
                                                      in_=banks[mo][:, 0:512]),
                    inc=(S["ac"] if mo == FC - 1 else None))
            emit_proj(b, wb, mix, 0, 512, banks, evac)

        b.w("vector", S["ac"])
        for fc in range(FC):
            if h == 0:
                b.o("vector", lambda E: E.memset(sb[:, 0:1], 0.0))
            else:
                b.o("vector", lambda E, fc=fc: E.tensor_copy(out=sb[:, 0:1], in_=carry[:, fc:fc + 1]))
            b.o("vector", lambda E, fc=fc, Th=Th:
                E.tensor_tensor_scan(out=sb[:, 1:1 + Th],
                                     data0=V["decay"][:, fc:fc + 1].broadcast_to([P, Th]),
                                     data1=kv[:, fc, 0:Th], initial=sb[:, 0:1],
                                     op0=ALU.mult, op1=ALU.add))
            if h == 0:
                b.o("vector", lambda E, fc=fc, Th=Th:
                    E.tensor_copy(out=carry[:, fc:fc + 1], in_=sb[:, Th:Th + 1]))
            b.o("vector", lambda E, fc=fc, m0=m0, Th=Th:
                E.scalar_tensor_tensor(out=kv[:, fc, m0:Th], in0=kv[:, fc, m0:Th],
                                       scalar=V["tf"][:, fc:fc + 1], in1=sb[:, 1 + m0:1 + Th],
                                       op0=ALU.mult, op1=ALU.add))
            b.o("vector", lambda E, fc=fc, m0=m0, Th=Th:
                E.tensor_tensor(out=kv[:, fc, m0:Th], in0=kv[:, fc, m0:Th],
                                in1=rb[:, fc, 0:HM], op=ALU.mult),
                inc=(S["dv"] if fc == FC - 1 else None))

        for i in range(HM // P):
            emit_fm2tm(b, kv, m0 + i * P, xt[i % 2][:], pT, idt[:])
            b.w("vector", S["ac"])
            emit_ln_tile(b, xt[i % 2][:], xm[i % 2][:], sqb, st, epsb[:, 0:1])
            emit_tm2fm(b, xm[i % 2][:], rb, i * P, pT, idt[:], out_r32=True)

        b.w("sync", S["pe"])
        b.w("sync", S["ac"])
        b.o("sync", lambda E: E.dma_start(out=wb[:], in_=io["woT"].rearrange("(a p) d -> p a d", p=P)),
            inc=S["lw"], k=16)
        if h == 0:
            b.w("tensor", S["lw"])
            b.w("tensor", S["ac"])
            b.w("tensor", S["dv"])
            b.w("tensor", S["lc"])
            for half_n in range(2):
                for kc in range(FC):
                    b.o("tensor", lambda E, half_n=half_n, kc=kc:
                        E.matmul(out=ps[half_n][0:1, 0:512], lhsT=r32(V["bvec"][:, kc:kc + 1]),
                                 rhs=r32(wb[:, kc, half_n * 512:(half_n + 1) * 512]),
                                 start=(kc == 0), stop=(kc == FC - 1)),
                        inc=(S["pe"] if (half_n == 1 and kc == FC - 1) else None))
            b.w("scalar", S["pe"])
            for half_n in range(2):
                b.o("scalar", lambda E, half_n=half_n:
                    E.copy(out=zln[0:1, half_n * 512:(half_n + 1) * 512], in_=ps[half_n][0:1, 0:512]),
                    inc=(S["ac"] if half_n == 1 else None))
            b.w("sync", S["ac"])
            b.o("sync", lambda E: E.dma_start(out=scr_bwo, in_=zln[0:1, :]), inc=S["lc"], k=16)
            b.w("sync", S["lc"])
            b.o("sync", lambda E: E.dma_start(out=V["bWoT"][:], in_=scr_bwo.rearrange("(a p) -> p a", p=P),
                                              allow_slow_non_contiguous=True), inc=S["lc"], k=16)
        b.w("vector", S["lw"])
        b.w("vector", S["pe"])
        for kc in range(FC):
            b.o("vector", lambda E, kc=kc: E.tensor_single_scalar(out=wb[:, kc, :], in_=wb[:, kc, :],
                                                                  scalar=V["g"][:, kc:kc + 1], op=ALU.mult),
                inc=(S["dv"] if kc == FC - 1 else None))
        zv = xn
        b.w("vector", S["lc"])
        for ci in range(HM // 512):
            def evac(mo, ci=ci):
                b.o("vector", lambda E, mo=mo: E.tensor_single_scalar(out=zv[:, mo, ci * 512:(ci + 1) * 512],
                                                                      in_=banks[mo][:, 0:512],
                                                                      scalar=V["bWoT"][:, mo:mo + 1],
                                                                      op=ALU.add),
                    inc=(S["dv"] if mo == FC - 1 else None))
            emit_proj(b, wb, rb, ci * 512, 512, banks, evac)

        for i in range(HM // P):
            emit_fm2tm(b, zv, i * P, ztm[:], pT, idt[:])
            b.w("vector", S["ac"])
            emit_ln_tile(b, ztm[:], zln[:], sqb, st, epsb[:, 0:1])
            r0 = WARM + h * HM + i * P
            b.w("sync", S["dv"], S["dv"].n - 1)
            b.o("sync", lambda E, r0=r0: E.dma_start(out=xold[:], in_=x_ext[r0:r0 + P, :]),
                inc=S["lx"], k=16)
            b.w("vector", S["lx"])
            b.o("vector", lambda E: E.tensor_tensor(out=zln[:], in0=zln[:], in1=xold[:], op=ALU.add),
                inc=S["dv"])
            b.w("sync", S["dv"])
            b.o("sync", lambda E, q0=h * HM + i * P: E.dma_start(out=x1[q0:q0 + P, :], in_=zln[:]),
                inc=S["stx"], k=16)
            b.w("vector", S["stx"])

    b.w("sync", S["stx"])



RMS_EPS = 1e-6
NUNIT = 8
HB = 1024


def emit_l0_ffn(b, io):
    dbg = io.get("dbg")
    nc = b.nc
    S = b.S
    a = lambda n_, sh_, dt_: b.sb(n_, sh_, dt_)

    xf = a("l0f_xf", [P, FC, HB], F32)
    hb = a("l0f_h", [P, FC, HB], F32)
    h2b = a("l0f_h2", [P, FC, HB], F32)
    zb = a("l0f_z", [P, 8, HB], F32)
    wsl = [a(f"l0f_w{i}", [P, 4096], F32R) for i in range(2)]
    rwb = a("l0f_rw", [P, FC, 6], F32R)
    wgt = a("l0f_wgt", [P, 48], F32)
    silb = a("l0f_sil", [P, 512], F32)
    xt = [a(f"l0f_xt{i}", [P, DM], F32) for i in range(2)]
    xm = [a(f"l0f_xm{i}", [P, DM], F32) for i in range(2)]
    sqb = a("l0f_sq", [P, DM], F32)
    st = a("l0f_st", [P, 8], F32)
    idt = a("l0f_id", [P, P], F32)
    epsb = a("l0f_eps", [P, 2], F32)
    bupv = a("l0f_bup", [P, 64], F32)
    bgv = a("l0f_bg", [P, 64], F32)
    onesr = a("l0f_ones", [1, P], F32R)
    brow = a("l0f_brow", [1, 12], F32R)
    bdals = a("l0f_bd", [1, DM], F32R)
    nwrow = a("l0f_nwr", [1, DM], F32R)
    nwb = a("l0f_nw", [P, DM], F32)
    gtile = a("l0f_gt", [P, 16], F32)

    ps = [b.psm(f"l0f_p{i}", [P, 512], F32) for i in range(6)]
    pT = b.psm("l0f_pT", [P, 1024], F32)

    x1 = io["x1"]
    x2 = io["x2"]

    b.o("sync", lambda E: E.dma_start(out=idt[:], in_=io["ident"]), inc=S["lc"], k=16)
    b.o("sync", lambda E: E.dma_start(out=bupv[:], in_=io["bup"]), inc=S["lc"], k=16)
    b.o("sync", lambda E: E.dma_start(out=bgv[:], in_=io["bg"]), inc=S["lc"], k=16)
    b.o("sync", lambda E: E.dma_start(out=onesr[:], in_=io["ones"]), inc=S["lc"], k=16)
    b.w("vector", S["lc"])
    b.o("vector", lambda E: E.memset(epsb[:, 0:1], LN_EPS))
    b.o("vector", lambda E: E.memset(epsb[:, 1:2], RMS_EPS))

    def rms_tile(src, dst, mul_nw):
        b.w("scalar", S["dv"])
        b.o("scalar", lambda E: E.activation(out=sqb[:], in_=src, func=AF.Square))
        b.w("vector", S["ac"])
        b.o("vector", lambda E: E.reduce_sum(out=st[:, 0:1], in_=sqb[:], axis=AX))
        b.w("scalar", S["dv"])
        b.o("scalar", lambda E: E.activation(out=st[:, 4:5], in_=st[:, 0:1], func=AF.Sqrt,
                                             scale=1.0 / DM, bias=epsb[:, 1:2]))
        b.w("vector", S["ac"])
        b.o("vector", lambda E: E.reciprocal(out=st[:, 5:6], in_=st[:, 4:5]))
        b.o("vector", lambda E: E.tensor_single_scalar(out=dst, in_=src, scalar=st[:, 5:6],
                                                       op=ALU.mult))
        if mul_nw:
            b.o("vector", lambda E: E.tensor_tensor(out=dst, in0=dst, in1=nwb[:], op=ALU.mult))

    for h in (0, 1):
        t0 = h * HB

        for i in range(HB // P):
            b.w("sync", S["dv"])
            b.w("sync", S["pe"])
            b.o("sync", lambda E, r0=t0 + i * P, i=i: E.dma_start(out=xt[i % 2][:], in_=x1[r0:r0 + P, :]),
                inc=S["lx"], k=16)
            b.w("vector", S["lx"])
            b.w("scalar", S["lx"])
            emit_ln_tile(b, xt[i % 2][:], xm[i % 2][:], sqb, st, epsb[:, 0:1])
            emit_tm2fm(b, xm[i % 2][:], xf, i * P, pT, idt[:], out_r32=True)

        for j in (0, 1):
            b.w("sync", S["dv"])
            b.w("sync", S["ac"])
            b.w("sync", S["pe"])
            b.o("sync", lambda E, j=j: E.dma_start(out=brow[0:1, 0:6], in_=io["rb"][j:j + 1, :]),
                inc=S["lc"], k=16)
            b.o("sync", lambda E, j=j: E.dma_start(out=brow[0:1, 6:12], in_=io["gb"][j:j + 1, :]),
                inc=S["lc"], k=16)
            b.o("sync", lambda E, j=j: E.dma_start(
                out=rwb[:], in_=io["routerW"][j].rearrange("(a p) e -> p a e", p=P)),
                inc=S["lw"], k=16)
            b.o("sync", lambda E, j=j: E.dma_start(out=nwrow[:], in_=io["normw"][j:j + 1, :]),
                inc=S["lc"], k=16)
            b.w("tensor", S["lc"])
            b.w("tensor", S["ac"])
            b.w("tensor", S["dv"])
            b.o("tensor", lambda E: E.matmul(out=pT[:, 0:512], lhsT=onesr[:],
                                             rhs=nwrow[0:1, 0:512], start=True, stop=True))
            b.o("tensor", lambda E: E.matmul(out=pT[:, 512:1024], lhsT=onesr[:],
                                             rhs=nwrow[0:1, 512:1024], start=True, stop=True),
                inc=S["pe"])
            b.w("vector", S["pe"])
            b.o("vector", lambda E: E.tensor_copy(out=nwb[:], in_=pT[:]))

            for tt in range(HB // P):
                b.w("tensor", S["lw"])
                b.w("tensor", S["dv"])
                b.w("tensor", S["ac"])
                for grp in range(2):
                    pb = ps[4 + grp]
                    for kc in range(FC):
                        b.o("tensor", lambda E, kc=kc, tt=tt, pb=pb:
                            E.matmul(out=pb[:, 0:6], lhsT=r32(xf[:, kc, tt * P:(tt + 1) * P]),
                                     rhs=rwb[:, kc, :], start=(kc == 0), stop=False))
                    if grp == 0:
                        b.o("tensor", lambda E, pb=pb:
                            E.matmul(out=pb[:, 0:6], lhsT=onesr[:], rhs=brow[0:1, 0:6],
                                     start=False, stop=True))
                    else:
                        b.o("tensor", lambda E, pb=pb:
                            E.matmul(out=pb[:, 0:6], lhsT=onesr[:], rhs=brow[0:1, 0:6],
                                     start=False, stop=False))
                        b.o("tensor", lambda E, pb=pb:
                            E.matmul(out=pb[:, 0:6], lhsT=onesr[:], rhs=brow[0:1, 6:12],
                                     start=False, stop=True), inc=S["pe"])
                b.w("vector", S["pe"])
                b.o("vector", lambda E: E.tensor_copy(out=gtile[:, 0:6], in_=ps[4][:, 0:6]))
                b.o("vector", lambda E: E.tensor_copy(out=gtile[:, 8:14], in_=ps[5][:, 0:6]))
                b.o("vector", lambda E: E.reduce_max(out=st[:, 0:1], in_=gtile[:, 8:14], axis=AX))
                b.o("vector", lambda E: E.tensor_scalar(out=sqb[:, 0:6], in0=gtile[:, 8:14],
                                                        scalar1=st[:, 0:1], scalar2=-1e30,
                                                        op0=ALU.is_ge, op1=ALU.mult))
                b.o("vector", lambda E: E.tensor_tensor(out=sqb[:, 8:14], in0=gtile[:, 8:14],
                                                        in1=sqb[:, 0:6], op=ALU.add))
                b.o("vector", lambda E: E.reduce_max(out=st[:, 1:2], in_=sqb[:, 8:14], axis=AX))
                b.o("vector", lambda E: E.tensor_scalar(out=sqb[:, 0:6], in0=gtile[:, 8:14],
                                                        scalar1=st[:, 1:2], scalar2=None,
                                                        op0=ALU.is_ge))
                b.o("vector", lambda E: E.tensor_scalar(out=sqb[:, 8:14], in0=sqb[:, 0:6],
                                                        scalar1=1e30, scalar2=-1e30,
                                                        op0=ALU.mult, op1=ALU.add))
                b.o("vector", lambda E: E.tensor_tensor(out=sqb[:, 16:22], in0=gtile[:, 0:6],
                                                        in1=sqb[:, 0:6], op=ALU.mult))
                b.o("vector", lambda E: E.tensor_tensor(out=sqb[:, 16:22], in0=sqb[:, 16:22],
                                                        in1=sqb[:, 8:14], op=ALU.add))
                b.o("vector", lambda E: E.reduce_max(out=st[:, 2:3], in_=sqb[:, 16:22], axis=AX))
                b.o("vector", lambda E: E.tensor_single_scalar(out=st[:, 3:4], in_=st[:, 2:3],
                                                               scalar=-1.0, op=ALU.mult))
                b.w("scalar", S["dv"])
                b.o("scalar", lambda E: E.activation(out=sqb[:, 24:30], in_=sqb[:, 16:22],
                                                     func=AF.Exp, bias=st[:, 3:4],
                                                     accum_out=st[:, 6:7]))
                b.w("vector", S["ac"])
                b.o("vector", lambda E: E.reciprocal(out=st[:, 7:8], in_=st[:, 6:7]))
                b.o("vector", lambda E, tt=tt: E.tensor_single_scalar(out=wgt[:, tt * 6:(tt + 1) * 6],
                                                                      in_=sqb[:, 24:30],
                                                                      scalar=st[:, 7:8], op=ALU.mult))
                if dbg is not None and h == 0 and j == 0 and tt == 0:
                    b.w("sync", S["dv"])
                    b.o("sync", lambda E: E.dma_start(out=dbg[0][:, 0:6], in_=gtile[:, 0:6]), inc=S["stx"], k=16)
                    b.o("sync", lambda E: E.dma_start(out=dbg[0][:, 8:14], in_=gtile[:, 8:14]), inc=S["stx"], k=16)
                    b.o("sync", lambda E: E.dma_start(out=dbg[1][:, 0:6], in_=wgt[:, 0:6]), inc=S["stx"], k=16)

            banks8 = [p[:] for p in ps] + [pT[:, 0:512], pT[:, 512:1024]]
            ring = {}
            lwp = [S["lw"], S["lw2"]]
            slabs = []

            def mk_half(u, hm, kind):
                def body(wv, lw):
                    b.w("tensor", lw)
                    b.w("tensor", S["dv"])
                    b.w("tensor", S["ac"])
                    if kind == "up":
                        def evac_h(gmo, u=u):
                            b.o("vector", lambda E, gmo=gmo, u=u, ci=body.ci:
                                E.tensor_scalar(out=r32(hb[:, gmo, ci * 512:(ci + 1) * 512]),
                                                in0=banks8[gmo % 8][:, 0:512],
                                                scalar1=bupv[:, u * FC + gmo:u * FC + gmo + 1],
                                                scalar2=None, op0=ALU.add))
                        ev = evac_h
                        rhs = xf
                    else:
                        def evac_g(gmo, u=u):
                            b.w("scalar", S["dv"])
                            b.o("scalar", lambda E, gmo=gmo, u=u:
                                E.activation(out=silb[:, 0:512], in_=banks8[gmo % 8][:, 0:512],
                                             func=AF.Silu,
                                             bias=bgv[:, u * FC + gmo:u * FC + gmo + 1]))
                            b.w("vector", S["ac"])
                            b.o("vector", lambda E, gmo=gmo, ci=body.ci:
                                E.tensor_tensor(out=r32(h2b[:, gmo, ci * 512:(ci + 1) * 512]),
                                                in0=silb[:, 0:512],
                                                in1=hb[:, gmo, ci * 512:(ci + 1) * 512], op=ALU.mult))
                        ev = evac_g
                        rhs = hb
                    wv3 = wv.rearrange("p (a d) -> p a d", d=512)
                    for ci in range(HB // 512):
                        body.ci = ci
                        emit_proj_rot(b, wv3, rhs, ci * 512, banks8, ev, nko=4,
                                      ring=ring, mo_off=hm * 4)
                return body

            def mk_down(u, nh):
                is_sh = u < 2
                e = u - 2
                def body(wv, lw):
                    b.w("tensor", lw)
                    b.w("tensor", S["dv"])
                    if nh == 0:
                        b.w("sync", S["pe"])
                        b.o("sync", lambda E, u=u: E.dma_start(
                            out=bdals[:], in_=io["bdall"][0:1, u * DM:(u + 1) * DM]),
                            inc=S["lc"], k=16)
                    b.w("tensor", S["lc"])
                    wv3 = wv.rearrange("p (a d) -> p a d", d=512)
                    for tt in range(HB // P):
                        gb_ = tt % 8
                        bank = banks8[gb_]
                        if gb_ in ring:
                            _, cdv, cac = ring[gb_]
                            b.w("tensor", S["dv"], cdv)
                            b.w("tensor", S["ac"], cac)
                        for kc in range(FC):
                            b.o("tensor", lambda E, tt=tt, kc=kc, bank=bank:
                                E.matmul(out=bank[:, 0:512],
                                         lhsT=r32(h2b[:, kc, tt * P:(tt + 1) * P]),
                                         rhs=wv3[:, kc, :],
                                         start=(kc == 0), stop=False))
                        b.o("tensor", lambda E, nh=nh, bank=bank:
                            E.matmul(out=bank[:, 0:512], lhsT=onesr[:],
                                     rhs=bdals[0:1, nh * 512:(nh + 1) * 512],
                                     start=False, stop=True), inc=S["pe"])
                        b.w("vector", S["pe"])
                        if u == 0:
                            b.o("vector", lambda E, tt=tt, nh=nh, bank=bank:
                                E.tensor_copy(out=zb[:, tt, nh * 512:(nh + 1) * 512],
                                              in_=bank[:, 0:512]))
                        elif is_sh:
                            b.o("vector", lambda E, tt=tt, nh=nh, bank=bank:
                                E.tensor_tensor(out=zb[:, tt, nh * 512:(nh + 1) * 512],
                                                in0=bank[:, 0:512],
                                                in1=zb[:, tt, nh * 512:(nh + 1) * 512], op=ALU.add))
                        else:
                            b.o("vector", lambda E, tt=tt, nh=nh, e=e, bank=bank:
                                E.scalar_tensor_tensor(out=zb[:, tt, nh * 512:(nh + 1) * 512],
                                                       in0=bank[:, 0:512],
                                                       scalar=wgt[:, tt * 6 + e:tt * 6 + e + 1],
                                                       in1=zb[:, tt, nh * 512:(nh + 1) * 512],
                                                       op0=ALU.mult, op1=ALU.add))
                        ring[gb_] = ("both", S["dv"].n, S["ac"].n)
                return body

            for u in range(NUNIT):
                is_sh = u < 2
                e = u - 2
                if is_sh:
                    upsrc = io["shWup"].rearrange("(a p) d -> p a d", p=P)[:, :, u * DM:(u + 1) * DM]
                    gsrc = io["shWg"].rearrange("(a p) d -> p a d", p=P)
                    dsrc = io["shWdown"].rearrange("(a p) d -> p a d", p=P)
                else:
                    upsrc = io["expWup"][e].rearrange("(a p) d -> p a d", p=P)
                    gsrc = io["expWg"][e].rearrange("(a p) d -> p a d", p=P)
                    dsrc = io["expWdown"][e].rearrange("(a p) d -> p a d", p=P)
                for hm in range(2):
                    slabs.append((upsrc[:, :, hm * 512:(hm + 1) * 512], mk_half(u, hm, "up")))
                for hm in range(2):
                    slabs.append((gsrc[:, :, hm * 512:(hm + 1) * 512], mk_half(u, hm, "g")))
                for nh in range(2):
                    slabs.append((dsrc[:, :, nh * 512:(nh + 1) * 512], mk_down(u, nh)))

            for i, (ws, body) in enumerate(slabs):
                if i == 0:
                    b.w("sync", S["pe"])
                    b.o("sync", lambda E, ws=ws: E.dma_start(
                        out=wsl[0][:].rearrange("p (a d) -> p a d", d=512), in_=ws),
                        inc=lwp[0], k=16)
                if i + 1 < len(slabs):
                    ws2 = slabs[i + 1][0]
                    b.w("sync", S["pe"])
                    b.o("sync", lambda E, ws2=ws2, j=(i + 1) % 2: E.dma_start(
                        out=wsl[j][:].rearrange("p (a d) -> p a d", d=512), in_=ws2),
                        inc=lwp[(i + 1) % 2], k=16)
                body(wsl[i % 2][:], lwp[i % 2])

            last = (j == 1)
            for tt in range(HB // P):
                emit_fm2tm(b, xf, tt * P, xt[tt % 2][:], pT, idt[:])
                b.w("vector", S["ac"])
                b.o("vector", lambda E, tt=tt: E.tensor_tensor(out=xt[tt % 2][:], in0=xt[tt % 2][:],
                                                               in1=zb[:, tt, :], op=ALU.add))
                rms_tile(xt[tt % 2][:], xm[tt % 2][:], True)
                if not last:
                    emit_tm2fm(b, xm[tt % 2][:], xf, tt * P, pT, idt[:], out_r32=True)
                else:
                    emit_ln_tile(b, xm[tt % 2][:], xt[tt % 2][:], sqb, st, epsb[:, 0:1])
                    b.w("sync", S["dv"])
                    b.o("sync", lambda E, r0=t0 + tt * P: E.dma_start(out=sqb[:], in_=x1[r0:r0 + P, :]),
                        inc=S["lx"], k=16)
                    b.w("vector", S["lx"])
                    b.o("vector", lambda E, tt=tt: E.tensor_tensor(out=xt[tt % 2][:], in0=xt[tt % 2][:],
                                                                   in1=sqb[:], op=ALU.add))
                    b.w("sync", S["dv"])
                    b.o("sync", lambda E, r0=t0 + tt * P, tt=tt:
                        E.dma_start(out=x2[r0:r0 + P, :], in_=xt[tt % 2][:]), inc=S["stx"], k=16)
                    b.w("vector", S["stx"])
                    b.w("scalar", S["stx"])

    b.w("sync", S["stx"])


def emit_proj_rot(b, wb, rhs_fm, rhs_c0, banks8, evac, nko=FC, ring=None, mo_off=0):
    S = b.S
    for mo in range(nko):
        gmo = mo_off + mo
        bi = gmo % 8
        if ring is not None and bi in ring:
            _, cdv, cac = ring[bi]
            b.w("tensor", S["dv"], cdv)
            b.w("tensor", S["ac"], cac)
        for kc in range(FC):
            b.o("tensor", lambda E, mo=mo, kc=kc, bi=bi:
                E.matmul(out=banks8[bi][:, 0:512], lhsT=wb[:, kc, mo * P:(mo + 1) * P],
                         rhs=r32(rhs_fm[:, kc, rhs_c0:rhs_c0 + 512]),
                         start=(kc == 0), stop=(kc == FC - 1)),
                inc=(S["pe"] if kc == FC - 1 else None))
        b.w("vector", S["pe"])
        b.w("scalar", S["pe"])
        evac(gmo)
        if ring is not None:
            ring[bi] = ("both", S["dv"].n, S["ac"].n)


def _vec_fm(v):
    return np.ascontiguousarray(v.reshape(FC, P).T)


def build_l0a(nc=None, races=True):
    nc = nc or bass.Bass("TRN2", target_bir_lowering=False, debug=False, detect_race_conditions=races)
    io = {
        "x_ext": nc.dram_tensor("x_ext", [TEXT, DM], F32, kind="ExternalInput").ap(),
        "wkT": nc.dram_tensor("wkT", [DM, DM], F32R, kind="ExternalInput").ap(),
        "wvT": nc.dram_tensor("wvT", [DM, DM], F32R, kind="ExternalInput").ap(),
        "wrT": nc.dram_tensor("wrT", [DM, DM], F32R, kind="ExternalInput").ap(),
        "woT": nc.dram_tensor("woT", [DM, DM], F32R, kind="ExternalInput").ap(),
        "tmk": nc.dram_tensor("tmk", [P, FC], F32, kind="ExternalInput").ap(),
        "tmv": nc.dram_tensor("tmv", [P, FC], F32, kind="ExternalInput").ap(),
        "tmr": nc.dram_tensor("tmr", [P, FC], F32, kind="ExternalInput").ap(),
        "td": nc.dram_tensor("td", [P, FC], F32, kind="ExternalInput").ap(),
        "tf": nc.dram_tensor("tf", [P, FC], F32, kind="ExternalInput").ap(),
        "g": nc.dram_tensor("g", [P, FC], F32, kind="ExternalInput").ap(),
        "bv": nc.dram_tensor("bv", [P, FC], F32R, kind="ExternalInput").ap(),
        "ident": nc.dram_tensor("ident", [P, P], F32, kind="ExternalInput").ap(),
        "x1": nc.dram_tensor("x1", [MT, DM], F32, kind="ExternalOutput").ap(),
    }
    b = B(nc)
    b.begin()
    emit_l0_attn(b, io)
    b.end()
    b.emit()
    return nc


def l0a_core_inputs(c, x, wk, wv, wr, wo, tmk, tmv, tmr, td, tf, g, bv):
    seq, q = c // 4, c % 4
    start = q * MT
    if start >= WARM:
        ext = x[seq, start - WARM:start + MT]
    else:
        ext = np.concatenate([np.zeros((WARM - start, DM), np.float32),
                              x[seq, 0:start + MT]], axis=0)
    return {
        "x_ext": np.ascontiguousarray(ext, np.float32),
        "wkT": np.ascontiguousarray(wk.T), "wvT": np.ascontiguousarray(wv.T),
        "wrT": np.ascontiguousarray(wr.T), "woT": np.ascontiguousarray(wo.T),
        "tmk": _vec_fm(np.asarray(tmk).reshape(-1)), "tmv": _vec_fm(np.asarray(tmv).reshape(-1)),
        "tmr": _vec_fm(np.asarray(tmr).reshape(-1)),
        "td": _vec_fm(np.asarray(td).reshape(-1)), "tf": _vec_fm(np.asarray(tf).reshape(-1)),
        "g": _vec_fm(np.asarray(g).reshape(-1)), "bv": _vec_fm(np.asarray(bv).reshape(-1)),
        "ident": np.eye(P, dtype=np.float32),
    }




NSEQ = 8192
NKB5 = NSEQ // 512
NKB1 = NSEQ // P
BIGNEG = -1e30


def emit_l1_qkv(b, io):
    nc = b.nc
    S = b.S
    a = lambda n_, sh_, dt_: b.sb(n_, sh_, dt_)
    xn = a("l1q_xn", [P, FC, MT], F32)
    wq = a("l1q_wq", [P, FC, 256], F32R)
    wk = a("l1q_wk", [P, FC, 256], F32R)
    wv = a("l1q_wv", [P, FC, 256], F32R)
    xt = [a(f"l1q_xt{i}", [P, DM], F32) for i in range(2)]
    xm = [a(f"l1q_xm{i}", [P, DM], F32) for i in range(2)]
    sqb = a("l1q_sq", [P, DM], F32)
    st = a("l1q_st", [P, 8], F32)
    idt = a("l1q_id", [P, P], F32)
    epsb = a("l1q_eps", [P, 1], F32)
    qkst = a("l1q_qk", [P, 2, MT], F32)
    vdst = a("l1q_vd", [P, 16, 256], F32)
    ps = [b.psm(f"l1q_p{i}", [P, 512], F32) for i in range(6)]
    pT = b.psm("l1q_pT", [P, 1024], F32)

    x2 = io["x2"]
    b.o("sync", lambda E: E.dma_start(out=idt[:], in_=io["ident"]), inc=S["lc"], k=16)
    b.o("sync", lambda E: E.dma_start(out=wq[:], in_=io["wq"].rearrange("(a p) d -> p a d", p=P)),
        inc=S["lw"], k=16)
    b.o("sync", lambda E: E.dma_start(out=wk[:], in_=io["wk"].rearrange("(a p) d -> p a d", p=P)),
        inc=S["lw"], k=16)
    b.o("sync", lambda E: E.dma_start(out=wv[:], in_=io["wvd"].rearrange("(a p) d -> p a d", p=P)),
        inc=S["lw"], k=16)
    b.w("vector", S["lc"])
    b.o("vector", lambda E: E.memset(epsb[:], LN_EPS))

    for i in range(MT // P):
        b.w("sync", S["dv"])
        b.w("sync", S["pe"])
        b.o("sync", lambda E, r0=i * P, i=i: E.dma_start(out=xt[i % 2][:], in_=x2[r0:r0 + P, :]),
            inc=S["lx"], k=16)
        b.w("vector", S["lx"])
        b.w("scalar", S["lx"])
        emit_ln_tile(b, xt[i % 2][:], xm[i % 2][:], sqb, st, epsb[:, 0:1])
        emit_tm2fm(b, xm[i % 2][:], xn, i * P, pT, idt[:], out_r32=True)

    for which, wbt, outd in (("q", wq, io["qf"]), ("k", wk, io["kf"])):
        b.w("tensor", S["lw"])
        b.w("tensor", S["ac"])
        b.w("tensor", S["dv"])
        for dc in range(2):
            for ci in range(MT // 512):
                bank = ps[(dc * 4 + ci) % 4]
                b.w("tensor", S["dv"])
                for kc in range(FC):
                    b.o("tensor", lambda E, dc=dc, ci=ci, kc=kc, bank=bank, wbt=wbt:
                        E.matmul(out=bank[:, 0:512], lhsT=wbt[:, kc, dc * P:(dc + 1) * P],
                                 rhs=r32(xn[:, kc, ci * 512:(ci + 1) * 512]),
                                 start=(kc == 0), stop=(kc == FC - 1)),
                        inc=(S["pe"] if kc == FC - 1 else None))
                b.w("vector", S["pe"])
                b.o("vector", lambda E, dc=dc, ci=ci, bank=bank:
                    E.tensor_copy(out=qkst[:, dc, ci * 512:(ci + 1) * 512], in_=bank[:, 0:512]))
        b.w("sync", S["dv"])
        b.o("sync", lambda E, outd=outd: E.dma_start(
            out=outd.rearrange("(a p) t -> p a t", p=P), in_=qkst[:]), inc=S["stx"], k=16)
        b.w("vector", S["stx"])

    b.w("tensor", S["lw"])
    for half in range(2):
        for i in range(8):
            tt = half * 8 + i
            bank = ps[i % 4]
            b.w("tensor", S["dv"])
            for kc in range(FC):
                b.o("tensor", lambda E, tt=tt, kc=kc, bank=bank:
                    E.matmul(out=bank[:, 0:256], lhsT=r32(xn[:, kc, tt * P:(tt + 1) * P]),
                             rhs=wv[:, kc, :], start=(kc == 0), stop=(kc == FC - 1)),
                    inc=(S["pe"] if kc == FC - 1 else None))
            b.w("vector", S["pe"])
            b.o("vector", lambda E, i=i, bank=bank:
                E.tensor_copy(out=vdst[:, i, :], in_=bank[:, 0:256]))
        b.w("sync", S["dv"])
        b.o("sync", lambda E, half=half: E.dma_start(
            out=io["vdf"][half * 1024:(half + 1) * 1024, :].rearrange("(a p) d -> p a d", p=P),
            in_=vdst[:, 0:8, :]), inc=S["stx"], k=16)
        b.w("vector", S["stx"])
    b.w("sync", S["stx"])


def emit_l1_attn(b, io):
    nc = b.nc
    S = b.S
    a = lambda n_, sh_, dt_: b.sb(n_, sh_, dt_)
    qfs = a("l1a_q", [P, 2, MT], F32R)
    kfs = a("l1a_k", [P, 2, NSEQ], F32R)
    vds = a("l1a_v", [P, NKB1, 256], F32R)
    attn_d = a("l1a_ad", [P, 2, MT], F32)
    wvupb = a("l1a_wu", [P, 2, DM], F32R)
    PTs = a("l1a_pt", [P, 512], F32)
    msk = a("l1a_msk", [P, 512], F32)
    tmpc = a("l1a_tmp", [P, 512], F32)
    mbc = a("l1a_mbc", [P, 512], F32)
    stat = a("l1a_stat", [P, 20], F32)
    qgl = a("l1a_qgl", [P, 16], F32)
    pcol = a("l1a_pc", [P, 1], F32)
    iotab = a("l1a_io", [P, 512], F32)
    qrowb = a("l1a_qr", [P, 4, 512], F32)
    onesr = a("l1a_ones", [1, P], F32R)
    drow = a("l1a_dr", [1, 512], F32R)
    dstg = a("l1a_ds", [4, P], F32)
    idt = a("l1a_id", [P, P], F32)
    epsb = a("l1a_eps", [P, 1], F32)
    xt = [a(f"l1a_xt{i}", [P, DM], F32) for i in range(2)]
    ztm = a("l1a_ztm", [P, DM], F32)
    sqb = a("l1a_sq", [P, DM], F32)
    st = a("l1a_st", [P, 8], F32)
    scr_d = nc.dram_tensor("l1a_scr", [512], F32R, kind="Internal").ap()

    ps = [b.psm(f"l1a_p{i}", [P, 512], F32) for i in range(6)]
    pT = b.psm("l1a_pT", [P, 1024], F32)

    x2 = io["x2"]
    x3 = io["x3"]
    SCL = 1.0 / 16.0

    b.o("sync", lambda E: E.dma_start(out=qfs[:], in_=io["qf"].rearrange("(a p) t -> p a t", p=P)),
        inc=S["lw"], k=16)
    b.o("sync", lambda E: E.dma_start(out=kfs[:], in_=io["kfull"].rearrange("(a p) t -> p a t", p=P)),
        inc=S["lw"], k=16)
    b.o("sync", lambda E: E.dma_start(out=vds[:], in_=io["vdfull"].rearrange("(a p) d -> p a d", p=P)),
        inc=S["lw"], k=16)
    b.o("sync", lambda E: E.dma_start(out=wvupb[:], in_=io["wvup"].rearrange("(a p) d -> p a d", p=P)),
        inc=S["lw"], k=16)
    b.o("sync", lambda E: E.dma_start(out=qgl[:], in_=io["qglob"]), inc=S["lc"], k=16)
    b.o("sync", lambda E: E.dma_start(out=pcol[:], in_=io["pcol"]), inc=S["lc"], k=16)
    b.o("sync", lambda E: E.dma_start(out=iotab[:], in_=io["iotab"]), inc=S["lc"], k=16)
    b.o("sync", lambda E: E.dma_start(out=qrowb[:], in_=io["qrowb"]), inc=S["lc"], k=16)
    b.o("sync", lambda E: E.dma_start(out=onesr[:], in_=io["ones"]), inc=S["lc"], k=16)
    b.o("sync", lambda E: E.dma_start(out=idt[:], in_=io["ident"]), inc=S["lc"], k=16)
    b.w("vector", S["lc"])
    b.w("vector", S["lw"])
    b.o("vector", lambda E: E.memset(epsb[:], LN_EPS))

    for qg in range(4):
        for sq_ in range(4):
            qt = qg * 4 + sq_
            b.o("vector", lambda E, sq_=sq_: E.memset(stat[:, sq_:sq_ + 1], 0.0))
            for kb in range(NKB5):
                bank = ps[kb % 2]
                b.w("tensor", S["dv"])
                b.w("tensor", S["ac"])
                for dc in range(2):
                    b.o("tensor", lambda E, qt=qt, kb=kb, dc=dc, bank=bank:
                        E.matmul(out=bank[:, 0:512], lhsT=qfs[:, dc, qt * P:(qt + 1) * P],
                                 rhs=kfs[:, dc, kb * 512:(kb + 1) * 512],
                                 start=(dc == 0), stop=(dc == 1)),
                        inc=(S["pe"] if dc == 1 else None))
                b.o("vector", lambda E, qt=qt, kb=kb: E.tensor_scalar(
                    out=stat[:, 8:9], in0=qgl[:, qt:qt + 1], scalar1=float(-kb * 512),
                    scalar2=None, op0=ALU.add))
                b.o("vector", lambda E: E.tensor_scalar(
                    out=msk[:], in0=iotab[:], scalar1=stat[:, 8:9], scalar2=BIGNEG,
                    op0=ALU.is_gt, op1=ALU.mult))
                b.w("vector", S["pe"])
                b.o("vector", lambda E, bank=bank: E.scalar_tensor_tensor(
                    out=tmpc[:], in0=bank[:, 0:512], scalar=SCL, in1=msk[:],
                    op0=ALU.mult, op1=ALU.add))
                b.w("scalar", S["dv"])
                b.o("scalar", lambda E: E.activation(out=r32(PTs[:]), in_=tmpc[:], func=AF.Exp,
                                                     accum_out=stat[:, 9:10]))
                b.w("vector", S["ac"])
                b.o("vector", lambda E, sq_=sq_: E.tensor_tensor(
                    out=stat[:, sq_:sq_ + 1], in0=stat[:, sq_:sq_ + 1], in1=stat[:, 9:10],
                    op=ALU.add))
        b.o("vector", lambda E: E.reciprocal(out=stat[:, 4:8], in_=stat[:, 0:4]))
        b.w("tensor", S["dv"])
        b.w("tensor", S["ac"])
        b.o("tensor", lambda E: E.transpose(out=pT[0:4, 0:P], in_=stat[:, 4:8], identity=idt[:]),
            inc=S["pe"])
        b.w("vector", S["pe"])
        b.o("vector", lambda E: E.tensor_copy(out=dstg[:], in_=pT[0:4, 0:P]))
        b.w("sync", S["dv"])
        b.o("sync", lambda E: E.dma_start(out=scr_d.rearrange("(a t) -> a t", a=4).bitcast(F32),
                                          in_=dstg[:]), inc=S["lc"], k=16)
        b.w("sync", S["lc"])
        b.o("sync", lambda E: E.dma_start(out=drow[:], in_=scr_d.rearrange("(o t) -> o t", o=1)),
            inc=S["lc"], k=16)
        b.w("tensor", S["lc"])
        b.o("tensor", lambda E: E.matmul(out=pT[:, 0:512], lhsT=onesr[:], rhs=drow[0:1, :],
                                         start=True, stop=True), inc=S["pe"])
        b.w("vector", S["pe"])
        b.o("vector", lambda E: E.tensor_copy(out=mbc[:], in_=pT[:, 0:512]))

        pvb = (ps[2], ps[3])
        for kb in range(NKB1):
            bank = ps[4 + kb % 2]
            b.w("tensor", S["dv"])
            b.w("tensor", S["ac"])
            for dc in range(2):
                b.o("tensor", lambda E, qg=qg, kb=kb, dc=dc, bank=bank:
                    E.matmul(out=bank[:, 0:512], lhsT=kfs[:, dc, kb * P:(kb + 1) * P],
                             rhs=qfs[:, dc, qg * 512:(qg + 1) * 512],
                             start=(dc == 0), stop=(dc == 1)),
                    inc=(S["pe"] if dc == 1 else None))
            b.o("vector", lambda E, kb=kb: E.tensor_scalar(
                out=stat[:, 8:9], in0=pcol[:], scalar1=float(kb * P), scalar2=None, op0=ALU.add))
            b.o("vector", lambda E, qg=qg: E.tensor_scalar(
                out=msk[:], in0=qrowb[:, qg, :], scalar1=stat[:, 8:9], scalar2=None,
                op0=ALU.is_ge))
            b.w("vector", S["pe"])
            b.o("vector", lambda E, bank=bank: E.tensor_single_scalar(
                out=tmpc[:], in_=bank[:, 0:512], scalar=SCL, op=ALU.mult))
            b.w("scalar", S["dv"])
            b.o("scalar", lambda E: E.activation(out=r32(PTs[:]), in_=tmpc[:], func=AF.Exp))
            b.w("vector", S["ac"])
            b.o("vector", lambda E: E.tensor_tensor(out=r32(PTs[:]), in0=PTs[:], in1=msk[:],
                                                    op=ALU.mult))
            b.w("tensor", S["dv"])
            for dc in range(2):
                b.o("tensor", lambda E, kb=kb, dc=dc:
                    E.matmul(out=pvb[dc][:, 0:512], lhsT=vds[:, kb, dc * P:(dc + 1) * P],
                             rhs=r32(PTs[:]), start=(kb == 0), stop=(kb == NKB1 - 1)),
                    inc=(S["pe"] if (kb == NKB1 - 1 and dc == 1) else None))
        b.w("vector", S["pe"])
        for dc in range(2):
            b.o("vector", lambda E, dc=dc, qg=qg: E.tensor_tensor(
                out=r32(attn_d[:, dc, qg * 512:(qg + 1) * 512]), in0=pvb[dc][:, 0:512],
                in1=mbc[:], op=ALU.mult))

    for tt in range(MT // P):
        b.w("vector", S["ac"])
        for nc_ in range(2):
            bank = ps[(tt * 2 + nc_) % 4]
            b.w("tensor", S["dv"])
            b.w("tensor", S["ac"])
            for kc in range(2):
                b.o("tensor", lambda E, tt=tt, nc_=nc_, kc=kc, bank=bank:
                    E.matmul(out=bank[:, 0:512], lhsT=r32(attn_d[:, kc, tt * P:(tt + 1) * P]),
                             rhs=wvupb[:, kc, nc_ * 512:(nc_ + 1) * 512],
                             start=(kc == 0), stop=(kc == 1)),
                    inc=(S["pe"] if kc == 1 else None))
            b.w("vector", S["pe"])
            b.o("vector", lambda E, nc_=nc_, bank=bank: E.tensor_copy(
                out=ztm[:, nc_ * 512:(nc_ + 1) * 512], in_=bank[:, 0:512]))
        emit_ln_tile(b, ztm[:], xt[tt % 2][:], sqb, st, epsb[:, 0:1])
        b.w("sync", S["dv"])
        b.o("sync", lambda E, r0=tt * P: E.dma_start(out=sqb[:], in_=x2[r0:r0 + P, :]),
            inc=S["lx"], k=16)
        b.w("vector", S["lx"])
        b.o("vector", lambda E, tt=tt: E.tensor_tensor(out=xt[tt % 2][:], in0=xt[tt % 2][:],
                                                       in1=sqb[:], op=ALU.add))
        b.w("sync", S["dv"])
        b.o("sync", lambda E, r0=tt * P, tt=tt: E.dma_start(out=x3[r0:r0 + P, :], in_=xt[tt % 2][:]),
            inc=S["stx"], k=16)
        b.w("vector", S["stx"])
        b.w("scalar", S["stx"])
    b.w("sync", S["stx"])



def build_l0f(nc=None, races=True, dbg=False):
    nc = nc or bass.Bass("TRN2", target_bir_lowering=False, debug=False, detect_race_conditions=races)
    io = {
        "x1": nc.dram_tensor("x1", [MT, DM], F32, kind="ExternalInput").ap(),
        "routerW": nc.dram_tensor("routerW", [2, DM, 6], F32R, kind="ExternalInput").ap(),
        "rb": nc.dram_tensor("rb", [2, 6], F32R, kind="ExternalInput").ap(),
        "gb": nc.dram_tensor("gb", [2, 6], F32R, kind="ExternalInput").ap(),
        "normw": nc.dram_tensor("normw", [2, DM], F32R, kind="ExternalInput").ap(),
        "shWup": nc.dram_tensor("shWup", [DM, 2 * DM], F32R, kind="ExternalInput").ap(),
        "shWg": nc.dram_tensor("shWg", [DM, DM], F32R, kind="ExternalInput").ap(),
        "shWdown": nc.dram_tensor("shWdown", [DM, DM], F32R, kind="ExternalInput").ap(),
        "expWup": nc.dram_tensor("expWup", [6, DM, DM], F32R, kind="ExternalInput").ap(),
        "expWg": nc.dram_tensor("expWg", [6, DM, DM], F32R, kind="ExternalInput").ap(),
        "expWdown": nc.dram_tensor("expWdown", [6, DM, DM], F32R, kind="ExternalInput").ap(),
        "bup": nc.dram_tensor("bup", [P, 64], F32, kind="ExternalInput").ap(),
        "bg": nc.dram_tensor("bg", [P, 64], F32, kind="ExternalInput").ap(),
        "bdall": nc.dram_tensor("bdall", [1, NUNIT * DM], F32R, kind="ExternalInput").ap(),
        "ones": nc.dram_tensor("ones", [1, P], F32R, kind="ExternalInput").ap(),
        "ident": nc.dram_tensor("ident", [P, P], F32, kind="ExternalInput").ap(),
        "x2": nc.dram_tensor("x2", [MT, DM], F32, kind="ExternalOutput").ap(),
    }
    if dbg:
        io["dbg"] = [nc.dram_tensor(f"dbg{i}", [P, DM], F32, kind="ExternalOutput").ap() for i in range(12)]
    b = B(nc)
    b.begin()
    emit_l0_ffn(b, io)
    b.end()
    b.emit()
    return nc


def l0f_shared_inputs(z):
    f = np.float32
    bup = np.empty((P, 64), f)
    bg = np.empty((P, 64), f)
    bd = np.empty((NUNIT, DM), f)
    sbup = np.asarray(z["L0_ffn_share_bup"], f)
    for u in range(2):
        bup[:, u * 8:(u + 1) * 8] = sbup[u * DM:(u + 1) * DM].reshape(FC, P).T
        bg[:, u * 8:(u + 1) * 8] = np.asarray(z["L0_ffn_share_bg"], f).reshape(FC, P).T
        bd[u] = np.asarray(z["L0_ffn_share_bdown"], f)
    for e in range(6):
        bup[:, (e + 2) * 8:(e + 3) * 8] = np.asarray(z["L0_ffn_exp_bup"], f)[e].reshape(FC, P).T
        bg[:, (e + 2) * 8:(e + 3) * 8] = np.asarray(z["L0_ffn_exp_bg"], f)[e].reshape(FC, P).T
        bd[e + 2] = np.asarray(z["L0_ffn_exp_bdown"], f)[e]
    return {
        "routerW": np.ascontiguousarray(np.asarray(z["L0_ffn_router_W"], f)),
        "rb": np.ascontiguousarray(np.asarray(z["L0_ffn_router_b"], f)),
        "gb": np.ascontiguousarray(np.asarray(z["L0_ffn_gate_bias"], f)),
        "normw": np.ascontiguousarray(np.asarray(z["L0_ffn_norm_w"], f)),
        "shWup": np.ascontiguousarray(np.asarray(z["L0_ffn_share_Wup"], f)),
        "shWg": np.ascontiguousarray(np.asarray(z["L0_ffn_share_Wg"], f)),
        "shWdown": np.ascontiguousarray(np.asarray(z["L0_ffn_share_Wdown"], f)),
        "expWup": np.ascontiguousarray(np.asarray(z["L0_ffn_exp_Wup"], f)),
        "expWg": np.ascontiguousarray(np.asarray(z["L0_ffn_exp_Wg"], f)),
        "expWdown": np.ascontiguousarray(np.asarray(z["L0_ffn_exp_Wdown"], f)),
        "bup": bup, "bg": bg, "bdall": bd.reshape(1, -1).copy(),
        "ones": np.ones((1, P), f),
        "ident": np.eye(P, dtype=f),
    }


def build_l1q(nc=None, races=True):
    nc = nc or bass.Bass("TRN2", target_bir_lowering=False, debug=False, detect_race_conditions=races)
    io = {
        "x2": nc.dram_tensor("x2", [MT, DM], F32, kind="ExternalInput").ap(),
        "wq": nc.dram_tensor("wq", [DM, 256], F32R, kind="ExternalInput").ap(),
        "wk": nc.dram_tensor("wk", [DM, 256], F32R, kind="ExternalInput").ap(),
        "wvd": nc.dram_tensor("wvd", [DM, 256], F32R, kind="ExternalInput").ap(),
        "ident": nc.dram_tensor("ident", [P, P], F32, kind="ExternalInput").ap(),
        "qf": nc.dram_tensor("qf", [256, MT], F32, kind="ExternalOutput").ap(),
        "kf": nc.dram_tensor("kf", [256, MT], F32, kind="ExternalOutput").ap(),
        "vdf": nc.dram_tensor("vdf", [MT, 256], F32, kind="ExternalOutput").ap(),
    }
    b = B(nc)
    b.begin()
    emit_l1_qkv(b, io)
    b.end()
    b.emit()
    return nc


def build_l1a(nc=None, races=True):
    nc = nc or bass.Bass("TRN2", target_bir_lowering=False, debug=False, detect_race_conditions=races)
    io = {
        "x2": nc.dram_tensor("x2", [MT, DM], F32, kind="ExternalInput").ap(),
        "qf": nc.dram_tensor("qf", [256, MT], F32R, kind="ExternalInput").ap(),
        "kfull": nc.dram_tensor("kfull", [256, NSEQ], F32R, kind="ExternalInput").ap(),
        "vdfull": nc.dram_tensor("vdfull", [NSEQ, 256], F32R, kind="ExternalInput").ap(),
        "wvup": nc.dram_tensor("wvup", [256, DM], F32R, kind="ExternalInput").ap(),
        "qglob": nc.dram_tensor("qglob", [P, 16], F32, kind="ExternalInput").ap(),
        "pcol": nc.dram_tensor("pcol", [P, 1], F32, kind="ExternalInput").ap(),
        "iotab": nc.dram_tensor("iotab", [P, 512], F32, kind="ExternalInput").ap(),
        "qrowb": nc.dram_tensor("qrowb", [P, 4, 512], F32, kind="ExternalInput").ap(),
        "ones": nc.dram_tensor("ones", [1, P], F32R, kind="ExternalInput").ap(),
        "ident": nc.dram_tensor("ident", [P, P], F32, kind="ExternalInput").ap(),
        "x3": nc.dram_tensor("x3", [MT, DM], F32, kind="ExternalOutput").ap(),
    }
    b = B(nc)
    b.begin()
    emit_l1_attn(b, io)
    b.end()
    b.emit()
    return nc


def l1a_consts(c):
    qpos = c % 4
    f = np.float32
    qglob = (qpos * MT + np.arange(16)[None, :] * P + np.arange(P)[:, None]).astype(f)
    pcol = np.arange(P, dtype=f).reshape(P, 1)
    iotab = np.broadcast_to(np.arange(512, dtype=f), (P, 512)).copy()
    qrowb = np.broadcast_to((qpos * MT + np.arange(4)[:, None] * 512 +
                             np.arange(512)[None, :]).astype(f), (P, 4, 512)).copy()
    return {"qglob": qglob, "pcol": pcol, "iotab": iotab, "qrowb": qrowb,
            "ones": np.ones((1, P), f), "ident": np.eye(P, dtype=f)}



def emit_l1_ffn(b, io):
    nc = b.nc
    S = b.S
    a = lambda n_, sh_, dt_: b.sb(n_, sh_, dt_)

    xf = a("l1f_xf", [P, FC, HB], F32)
    hb = a("l1f_h", [P, FC, HB], F32)
    h2b = a("l1f_h2", [P, FC, HB], F32)
    zb = a("l1f_z", [P, 8, HB], F32)
    wsl = [a(f"l1f_w{i}", [P, 4096], F32R) for i in range(2)]
    rwb = a("l1f_rw", [P, FC, 8], F32R)
    wgt = a("l1f_wgt", [P, 128], F32)
    xt = [a(f"l1f_xt{i}", [P, DM], F32) for i in range(2)]
    xm = [a(f"l1f_xm{i}", [P, DM], F32) for i in range(2)]
    sqb = a("l1f_sq", [P, DM], F32)
    st = a("l1f_st", [P, 8], F32)
    idt = a("l1f_id", [P, P], F32)
    epsb = a("l1f_eps", [P, 1], F32)
    gtile = a("l1f_gt", [P, 8], F32)

    ps = [b.psm(f"l1f_p{i}", [P, 512], F32) for i in range(6)]
    pT = b.psm("l1f_pT", [P, 1024], F32)

    x3 = io["x3"]
    x4 = io["x4"]

    b.o("sync", lambda E: E.dma_start(out=idt[:], in_=io["ident"]), inc=S["lc"], k=16)
    b.o("sync", lambda E: E.dma_start(out=rwb[:], in_=io["router"].rearrange("(a p) e -> p a e", p=P)),
        inc=S["lw"], k=16)
    b.w("vector", S["lc"])
    b.o("vector", lambda E: E.memset(epsb[:], LN_EPS))

    for h in (0, 1):
        t0 = h * HB

        for i in range(HB // P):
            b.w("sync", S["dv"])
            b.w("sync", S["pe"])
            b.o("sync", lambda E, r0=t0 + i * P, i=i: E.dma_start(out=xt[i % 2][:], in_=x3[r0:r0 + P, :]),
                inc=S["lx"], k=16)
            b.w("vector", S["lx"])
            b.w("scalar", S["lx"])
            emit_ln_tile(b, xt[i % 2][:], xm[i % 2][:], sqb, st, epsb[:, 0:1])
            emit_tm2fm(b, xm[i % 2][:], xf, i * P, pT, idt[:], out_r32=True)

        for tt in range(HB // P):
            b.w("tensor", S["lw"])
            b.w("tensor", S["dv"])
            b.w("tensor", S["ac"])
            for kc in range(FC):
                b.o("tensor", lambda E, kc=kc, tt=tt:
                    E.matmul(out=ps[4][:, 0:8], lhsT=r32(xf[:, kc, tt * P:(tt + 1) * P]),
                             rhs=rwb[:, kc, :], start=(kc == 0), stop=(kc == FC - 1)),
                    inc=(S["pe"] if kc == FC - 1 else None))
            b.w("vector", S["pe"])
            b.o("vector", lambda E: E.tensor_copy(out=gtile[:], in_=ps[4][:, 0:8]))
            b.o("vector", lambda E: E.reduce_max(out=st[:, 0:1], in_=gtile[:], axis=AX))
            b.o("vector", lambda E: E.tensor_single_scalar(out=st[:, 1:2], in_=st[:, 0:1],
                                                           scalar=-1.0, op=ALU.mult))
            b.w("scalar", S["dv"])
            b.o("scalar", lambda E: E.activation(out=sqb[:, 0:8], in_=gtile[:], func=AF.Exp,
                                                 bias=st[:, 1:2], accum_out=st[:, 2:3]))
            b.w("vector", S["ac"])
            b.o("vector", lambda E: E.reciprocal(out=st[:, 3:4], in_=st[:, 2:3]))
            b.o("vector", lambda E: E.tensor_single_scalar(out=sqb[:, 0:8], in_=sqb[:, 0:8],
                                                           scalar=st[:, 3:4], op=ALU.mult))
            b.o("vector", lambda E: E.reduce_max(out=st[:, 4:5], in_=sqb[:, 0:8], axis=AX))
            b.o("vector", lambda E: E.tensor_scalar(out=sqb[:, 8:16], in0=sqb[:, 0:8],
                                                    scalar1=st[:, 4:5], scalar2=BIGNEG,
                                                    op0=ALU.is_ge, op1=ALU.mult))
            b.o("vector", lambda E: E.tensor_tensor(out=sqb[:, 8:16], in0=sqb[:, 0:8],
                                                    in1=sqb[:, 8:16], op=ALU.add))
            b.o("vector", lambda E: E.reduce_max(out=st[:, 5:6], in_=sqb[:, 8:16], axis=AX))
            b.o("vector", lambda E: E.tensor_scalar(out=sqb[:, 8:16], in0=sqb[:, 0:8],
                                                    scalar1=st[:, 5:6], scalar2=None, op0=ALU.is_ge))
            b.o("vector", lambda E, tt=tt: E.tensor_tensor(out=wgt[:, tt * 8:(tt + 1) * 8],
                                                           in0=sqb[:, 0:8], in1=sqb[:, 8:16],
                                                           op=ALU.mult))

        banks8 = [p[:] for p in ps] + [pT[:, 0:512], pT[:, 512:1024]]
        ring = {}
        lwp = [S["lw"], S["lw2"]]
        slabs = []

        def mk_gu(u, hm, nI, kind):
            def body(wv, lw):
                b.w("tensor", lw)
                b.w("tensor", S["dv"])
                b.w("tensor", S["ac"])
                if kind == "gate":
                    def ev(gmo):
                        b.w("scalar", S["dv"])
                        b.o("scalar", lambda E, gmo=gmo, ci=body.ci:
                            E.activation(out=hb[:, gmo, ci * 512:(ci + 1) * 512],
                                         in_=banks8[gmo % 8][:, 0:512], func=AF.Silu))
                else:
                    def ev(gmo):
                        b.w("vector", S["ac"])
                        b.o("vector", lambda E, gmo=gmo, ci=body.ci:
                            E.tensor_tensor(out=r32(h2b[:, gmo, ci * 512:(ci + 1) * 512]),
                                            in0=banks8[gmo % 8][:, 0:512],
                                            in1=hb[:, gmo, ci * 512:(ci + 1) * 512], op=ALU.mult))
                wv3 = wv.rearrange("p (a d) -> p a d", d=512)
                for ci in range(HB // 512):
                    body.ci = ci
                    emit_proj_rot(b, wv3, xf, ci * 512, banks8, ev, nko=4, ring=ring,
                                  mo_off=hm * 4)
            return body

        def mk_down(u, nh_list, nI):
            is_sh = u == 0
            e = u - 1
            def body(wv, lw):
                b.w("tensor", lw)
                b.w("tensor", S["dv"])
                wv3 = wv.rearrange("p (a d) -> p a d", d=1024 if len(nh_list) == 2 else 512)
                for tt in range(HB // P):
                    for ni, nh in enumerate(nh_list):
                        gb_ = (tt * len(nh_list) + ni) % 8
                        bank = banks8[gb_]
                        if gb_ in ring:
                            _, cdv, cac = ring[gb_]
                            b.w("tensor", S["dv"], cdv)
                            b.w("tensor", S["ac"], cac)
                        for kc in range(nI):
                            rhsv = wv3[:, kc, ni * 512:(ni + 1) * 512] if len(nh_list) == 2                                 else wv3[:, kc, :]
                            b.o("tensor", lambda E, tt=tt, kc=kc, bank=bank, rhsv=rhsv, nI=nI:
                                E.matmul(out=bank[:, 0:512],
                                         lhsT=r32(h2b[:, kc, tt * P:(tt + 1) * P]),
                                         rhs=rhsv, start=(kc == 0), stop=(kc == nI - 1)),
                                inc=(S["pe"] if kc == nI - 1 else None))
                        b.w("vector", S["pe"])
                        if is_sh:
                            b.o("vector", lambda E, tt=tt, nh=nh, bank=bank:
                                E.tensor_copy(out=zb[:, tt, nh * 512:(nh + 1) * 512],
                                              in_=bank[:, 0:512]))
                        else:
                            b.o("vector", lambda E, tt=tt, nh=nh, e=e, bank=bank:
                                E.scalar_tensor_tensor(out=zb[:, tt, nh * 512:(nh + 1) * 512],
                                                       in0=bank[:, 0:512],
                                                       scalar=wgt[:, tt * 8 + e:tt * 8 + e + 1],
                                                       in1=zb[:, tt, nh * 512:(nh + 1) * 512],
                                                       op0=ALU.mult, op1=ALU.add))
                        ring[gb_] = ("both", S["dv"].n, S["ac"].n)
            return body

        for u in range(9):
            is_sh = u == 0
            e = u - 1
            if is_sh:
                gsrc = io["shg"].rearrange("(a p) d -> p a d", p=P)
                usrc = io["shu"].rearrange("(a p) d -> p a d", p=P)
                dsrc = io["shd"].rearrange("(a p) d -> p a d", p=P)
                for hm in range(2):
                    slabs.append((gsrc[:, :, hm * 512:(hm + 1) * 512], mk_gu(u, hm, FC, "gate")))
                for hm in range(2):
                    slabs.append((usrc[:, :, hm * 512:(hm + 1) * 512], mk_gu(u, hm, FC, "up")))
                for nh in range(2):
                    slabs.append((dsrc[:, :, nh * 512:(nh + 1) * 512], mk_down(u, [nh], FC)))
            else:
                gsrc = io["eg"][e].rearrange("(a p) d -> p a d", p=P)
                usrc = io["eu"][e].rearrange("(a p) d -> p a d", p=P)
                dsrc = io["ed"][e].rearrange("(a p) d -> p a d", p=P)
                slabs.append((gsrc, mk_gu(u, 0, 4, "gate")))
                slabs.append((usrc, mk_gu(u, 0, 4, "up")))
                slabs.append((dsrc, mk_down(u, [0, 1], 4)))

        for i, (ws, body) in enumerate(slabs):
            shp = ws.shape
            if i == 0:
                b.w("sync", S["pe"])
                b.o("sync", lambda E, ws=ws, shp=shp: E.dma_start(
                    out=wsl[0][:].rearrange("p (a d) -> p a d", d=shp[2])[:, 0:shp[1], :], in_=ws),
                    inc=lwp[0], k=16)
            if i + 1 < len(slabs):
                ws2 = slabs[i + 1][0]
                shp2 = ws2.shape
                b.w("sync", S["pe"])
                b.o("sync", lambda E, ws2=ws2, shp2=shp2, j=(i + 1) % 2: E.dma_start(
                    out=wsl[j][:].rearrange("p (a d) -> p a d", d=shp2[2])[:, 0:shp2[1], :], in_=ws2),
                    inc=lwp[(i + 1) % 2], k=16)
            body(wsl[i % 2][:], lwp[i % 2])

        for tt in range(HB // P):
            b.w("vector", S["ac"])
            emit_ln_tile(b, zb[:, tt, :], xt[tt % 2][:], sqb, st, epsb[:, 0:1])
            b.w("sync", S["dv"])
            b.o("sync", lambda E, r0=t0 + tt * P: E.dma_start(out=sqb[:], in_=x3[r0:r0 + P, :]),
                inc=S["lx"], k=16)
            b.w("vector", S["lx"])
            b.o("vector", lambda E, tt=tt: E.tensor_tensor(out=xt[tt % 2][:], in0=xt[tt % 2][:],
                                                           in1=sqb[:], op=ALU.add))
            b.w("sync", S["dv"])
            b.o("sync", lambda E, r0=t0 + tt * P, tt=tt:
                E.dma_start(out=x4[r0:r0 + P, :], in_=xt[tt % 2][:]), inc=S["stx"], k=16)
            b.w("vector", S["stx"])
            b.w("scalar", S["stx"])
    b.w("sync", S["stx"])


def build_l1f(nc=None, races=True):
    nc = nc or bass.Bass("TRN2", target_bir_lowering=False, debug=False, detect_race_conditions=races)
    io = {
        "x3": nc.dram_tensor("x3", [MT, DM], F32, kind="ExternalInput").ap(),
        "router": nc.dram_tensor("router", [DM, 8], F32R, kind="ExternalInput").ap(),
        "shg": nc.dram_tensor("shg", [DM, DM], F32R, kind="ExternalInput").ap(),
        "shu": nc.dram_tensor("shu", [DM, DM], F32R, kind="ExternalInput").ap(),
        "shd": nc.dram_tensor("shd", [DM, DM], F32R, kind="ExternalInput").ap(),
        "eg": nc.dram_tensor("eg", [8, DM, 512], F32R, kind="ExternalInput").ap(),
        "eu": nc.dram_tensor("eu", [8, DM, 512], F32R, kind="ExternalInput").ap(),
        "ed": nc.dram_tensor("ed", [8, 512, DM], F32R, kind="ExternalInput").ap(),
        "ident": nc.dram_tensor("ident", [P, P], F32, kind="ExternalInput").ap(),
        "x4": nc.dram_tensor("x4", [MT, DM], F32, kind="ExternalOutput").ap(),
    }
    b = B(nc)
    b.begin()
    emit_l1_ffn(b, io)
    b.end()
    b.emit()
    return nc




def build_fused1(races=True):
    nc = bass.Bass("TRN2", target_bir_lowering=False, debug=False, detect_race_conditions=races)
    dt_ = nc.dram_tensor
    ident = dt_("ident", [P, P], F32, kind="ExternalInput").ap()
    ones = dt_("ones", [1, P], F32R, kind="ExternalInput").ap()
    x1 = dt_("x1", [MT, DM], F32, kind="Internal").ap()
    x2 = dt_("x2", [MT, DM], F32, kind="ExternalOutput").ap()
    ioA = {
        "x_ext": dt_("x_ext", [TEXT, DM], F32, kind="ExternalInput").ap(),
        "wkT": dt_("wkT", [DM, DM], F32R, kind="ExternalInput").ap(),
        "wvT": dt_("wvT", [DM, DM], F32R, kind="ExternalInput").ap(),
        "wrT": dt_("wrT", [DM, DM], F32R, kind="ExternalInput").ap(),
        "woT": dt_("woT", [DM, DM], F32R, kind="ExternalInput").ap(),
        "tmk": dt_("tmk", [P, FC], F32, kind="ExternalInput").ap(),
        "tmv": dt_("tmv", [P, FC], F32, kind="ExternalInput").ap(),
        "tmr": dt_("tmr", [P, FC], F32, kind="ExternalInput").ap(),
        "td": dt_("td", [P, FC], F32, kind="ExternalInput").ap(),
        "tf": dt_("tf", [P, FC], F32, kind="ExternalInput").ap(),
        "g": dt_("g", [P, FC], F32, kind="ExternalInput").ap(),
        "bv": dt_("bv", [P, FC], F32R, kind="ExternalInput").ap(),
        "ident": ident, "x1": x1,
    }
    ioB = {
        "x1": x1, "x2": x2, "ident": ident, "ones": ones,
        "routerW": dt_("routerW", [2, DM, 6], F32R, kind="ExternalInput").ap(),
        "rb": dt_("rb", [2, 6], F32R, kind="ExternalInput").ap(),
        "gb": dt_("gb", [2, 6], F32R, kind="ExternalInput").ap(),
        "normw": dt_("normw", [2, DM], F32R, kind="ExternalInput").ap(),
        "shWup": dt_("shWup", [DM, 2 * DM], F32R, kind="ExternalInput").ap(),
        "shWg": dt_("shWg", [DM, DM], F32R, kind="ExternalInput").ap(),
        "shWdown": dt_("shWdown", [DM, DM], F32R, kind="ExternalInput").ap(),
        "expWup": dt_("expWup", [6, DM, DM], F32R, kind="ExternalInput").ap(),
        "expWg": dt_("expWg", [6, DM, DM], F32R, kind="ExternalInput").ap(),
        "expWdown": dt_("expWdown", [6, DM, DM], F32R, kind="ExternalInput").ap(),
        "bup": dt_("bup", [P, 64], F32, kind="ExternalInput").ap(),
        "bg": dt_("bg", [P, 64], F32, kind="ExternalInput").ap(),
        "bdall": dt_("bdall", [1, NUNIT * DM], F32R, kind="ExternalInput").ap(),
    }
    ioC = {
        "x2": x2, "ident": ident,
        "wq": dt_("wq", [DM, 256], F32R, kind="ExternalInput").ap(),
        "wk": dt_("wk", [DM, 256], F32R, kind="ExternalInput").ap(),
        "wvd": dt_("wvd", [DM, 256], F32R, kind="ExternalInput").ap(),
        "qf": dt_("qf", [256, MT], F32, kind="ExternalOutput").ap(),
        "kf": dt_("kf", [256, MT], F32, kind="ExternalOutput").ap(),
        "vdf": dt_("vdf", [MT, 256], F32, kind="ExternalOutput").ap(),
    }
    b = B(nc)
    b.begin(); emit_l0_attn(b, ioA); b.end()
    b.begin(); emit_l0_ffn(b, ioB); b.end()
    b.begin(); emit_l1_qkv(b, ioC); b.end()
    b.emit()
    return nc


def build_fused2(races=True):
    nc = bass.Bass("TRN2", target_bir_lowering=False, debug=False, detect_race_conditions=races)
    dt_ = nc.dram_tensor
    ident = dt_("ident", [P, P], F32, kind="ExternalInput").ap()
    x3 = dt_("x3", [MT, DM], F32, kind="Internal").ap()
    ioD = {
        "x2": dt_("x2", [MT, DM], F32, kind="ExternalInput").ap(),
        "qf": dt_("qf", [256, MT], F32R, kind="ExternalInput").ap(),
        "kfull": dt_("kfull", [256, NSEQ], F32R, kind="ExternalInput").ap(),
        "vdfull": dt_("vdfull", [NSEQ, 256], F32R, kind="ExternalInput").ap(),
        "wvup": dt_("wvup", [256, DM], F32R, kind="ExternalInput").ap(),
        "qglob": dt_("qglob", [P, 16], F32, kind="ExternalInput").ap(),
        "pcol": dt_("pcol", [P, 1], F32, kind="ExternalInput").ap(),
        "iotab": dt_("iotab", [P, 512], F32, kind="ExternalInput").ap(),
        "qrowb": dt_("qrowb", [P, 4, 512], F32, kind="ExternalInput").ap(),
        "ones": dt_("ones", [1, P], F32R, kind="ExternalInput").ap(),
        "ident": ident, "x3": x3,
    }
    ioE = {
        "x3": x3, "ident": ident,
        "router": dt_("router", [DM, 8], F32R, kind="ExternalInput").ap(),
        "shg": dt_("shg", [DM, DM], F32R, kind="ExternalInput").ap(),
        "shu": dt_("shu", [DM, DM], F32R, kind="ExternalInput").ap(),
        "shd": dt_("shd", [DM, DM], F32R, kind="ExternalInput").ap(),
        "eg": dt_("eg", [8, DM, 512], F32R, kind="ExternalInput").ap(),
        "eu": dt_("eu", [8, DM, 512], F32R, kind="ExternalInput").ap(),
        "ed": dt_("ed", [8, 512, DM], F32R, kind="ExternalInput").ap(),
        "x4": dt_("x4", [MT, DM], F32, kind="ExternalOutput").ap(),
    }
    b = B(nc)
    b.begin(); emit_l1_attn(b, ioD); b.end()
    b.begin(); emit_l1_ffn(b, ioE); b.end()
    b.emit()
    return nc


_BUILD_CACHE = {}


def _get(name, fn):
    if name not in _BUILD_CACHE:
        _BUILD_CACHE[name] = fn()
    return _BUILD_CACHE[name]


def kernel(**inputs):
    f = np.float32
    z = inputs
    x = np.asarray(z["L0_attn_x"], f)
    cores = list(range(8))
    idm = np.eye(P, dtype=f)
    onesv = np.ones((1, P), f)

    nc1 = _get("F1", build_fused1)
    sh_b = l0f_shared_inputs(z)
    del sh_b["ident"], sh_b["ones"]
    wq = np.ascontiguousarray(np.asarray(z["L1_attn_Wq"], f))
    wkq = np.ascontiguousarray(np.asarray(z["L1_attn_Wk"], f))
    wvd = np.ascontiguousarray(np.asarray(z["L1_attn_Wvdown"], f))
    ins1 = []
    for c in cores:
        d = l0a_core_inputs(c, x,
                            np.asarray(z["L0_attn_Wk"], f), np.asarray(z["L0_attn_Wv"], f),
                            np.asarray(z["L0_attn_Wr"], f), np.asarray(z["L0_attn_Wo"], f),
                            np.asarray(z["L0_attn_time_mix_k"], f), np.asarray(z["L0_attn_time_mix_v"], f),
                            np.asarray(z["L0_attn_time_mix_r"], f), np.asarray(z["L0_attn_time_decay"], f),
                            np.asarray(z["L0_attn_time_first"], f), np.asarray(z["L0_attn_ln_g"], f),
                            np.asarray(z["L0_attn_ln_b"], f))
        d.update(sh_b)
        d.update({"ones": onesv, "wq": wq, "wk": wkq, "wvd": wvd})
        ins1.append(d)
    res = run_bass_kernel_spmd(nc1, ins1, core_ids=cores)
    x2 = [np.asarray(res.results[c]["x2"]) for c in cores]
    qf = [np.asarray(res.results[c]["qf"]) for c in cores]
    kf = [np.asarray(res.results[c]["kf"]) for c in cores]
    vdf = [np.asarray(res.results[c]["vdf"]) for c in cores]

    kfull = [np.ascontiguousarray(np.concatenate([kf[s * 4 + q] for q in range(4)], axis=1))
             for s in range(2)]
    vdfull = [np.ascontiguousarray(np.concatenate([vdf[s * 4 + q] for q in range(4)], axis=0))
              for s in range(2)]

    nc2 = _get("F2", build_fused2)
    wvu = np.ascontiguousarray(np.asarray(z["L1_attn_Wvup"], f))
    sh_e = {
        "router": np.ascontiguousarray(np.asarray(z["L1_ffn_router"], f)),
        "shg": np.ascontiguousarray(np.asarray(z["L1_ffn_shared_gate"], f)),
        "shu": np.ascontiguousarray(np.asarray(z["L1_ffn_shared_up"], f)),
        "shd": np.ascontiguousarray(np.asarray(z["L1_ffn_shared_down"], f)),
        "eg": np.ascontiguousarray(np.asarray(z["L1_ffn_experts_gate"], f)),
        "eu": np.ascontiguousarray(np.asarray(z["L1_ffn_experts_up"], f)),
        "ed": np.ascontiguousarray(np.asarray(z["L1_ffn_experts_down"], f)),
    }
    ins2 = []
    for c in cores:
        d = l1a_consts(c)
        d.update(sh_e)
        d.update({"x2": x2[c], "qf": qf[c], "kfull": kfull[c // 4],
                  "vdfull": vdfull[c // 4], "wvup": wvu})
        ins2.append(d)
    res = run_bass_kernel_spmd(nc2, ins2, core_ids=cores)
    out = np.empty((2, 8192, DM), f)
    for c in cores:
        seq, q = c // 4, c % 4
        out[seq, q * MT:(q + 1) * MT] = np.asarray(res.results[c]["x4"])
    return out


# version 06
# speedup vs baseline: 1.4912x; 1.0384x over previous
import numpy as np
import concourse.bass as bass
import concourse.mybir as mybir
from concourse.bass_utils import run_bass_kernel_spmd

F32 = mybir.dt.float32
F32R = mybir.dt.float32r
AF = mybir.ActivationFunctionType
ALU = mybir.AluOpType
AX = mybir.AxisListType.X

P = 128
DM = 1024
FC = DM // P
MT = 2048
WARM = 384
TEXT = MT + WARM
LN_EPS = 1e-5

ENGS = ("sync", "tensor", "vector", "scalar", "gpsimd")


class Sem:
    __slots__ = ("h", "n")

    def __init__(self, nc, name):
        self.h = nc.alloc_semaphore(name)
        self.n = 0


AUTO_GUARD = {"vector": "dv", "scalar": "ac", "gpsimd": "gp"}


class B:

    def __init__(self, nc):
        self.nc = nc
        self.q = {e: [] for e in ENGS}
        self.S = {k: Sem(nc, "s_" + k) for k in
                  ("lc", "lx", "lw", "lw2", "stx", "pe", "dv", "ac", "gp")}
        self._es = None

    def begin(self):
        from contextlib import ExitStack
        self._es = ExitStack()

    def sb(self, name, shape, dtype):
        return self._es.enter_context(self.nc.sbuf_tensor(name, shape, dtype))

    def psm(self, name, shape, dtype):
        return self._es.enter_context(self.nc.psum_tensor(name, shape, dtype))

    def end(self):
        for e in ENGS:
            for k in ("dv", "ac", "pe", "gp", "lc", "lx", "lw", "stx"):
                self.w(e, self.S[k])
        self._es.close()
        self._es = None

    def o(self, eng, fn, inc=None, k=1, dma=False):
        incs = []
        if eng in AUTO_GUARD and not dma:
            au = self.S[AUTO_GUARD[eng]]
            if au.n > 0:
                self.q[eng].append((("w", au.h, au.n), ()))
            incs.append((au.h, 1))
            au.n += 1
        elif inc is not None:
            inc.n += k
            incs.append((inc.h, k))
        self.q[eng].append((fn, tuple(incs)))

    def w(self, eng, sem, n=None):
        c = sem.n if n is None else n
        if c > 0:
            self.q[eng].append((("w", sem.h, c), ()))

    def emit(self):
        with self.nc.Block() as blk:
            for e in ENGS:
                ops = self.q[e]
                if not ops:
                    continue

                def runner(E, ops=ops):
                    for fn, incs in ops:
                        if isinstance(fn, tuple):
                            E.wait_ge(fn[1], fn[2])
                        else:
                            ins = fn(E)
                            for semh, k in incs:
                                ins = ins.then_inc(semh, k)

                getattr(blk, e)(runner)


def r32(ap):
    return ap.bitcast(F32R)



def emit_ln_tile(b, src_tile, dst_tile, sqb, st, eps_ap):
    S = b.S
    b.w("scalar", S["dv"])
    b.o("scalar", lambda E: E.activation(out=sqb[:], in_=src_tile, func=AF.Square), inc=S["ac"])
    b.w("vector", S["ac"])
    b.o("vector", lambda E: E.reduce_sum(out=st[:, 0:1], in_=src_tile, axis=AX))
    b.o("vector", lambda E: E.reduce_sum(out=st[:, 1:2], in_=sqb[:], axis=AX))
    b.o("vector", lambda E: E.tensor_tensor(out=st[:, 2:3], in0=st[:, 0:1], in1=st[:, 0:1], op=ALU.mult))
    b.o("vector", lambda E: E.scalar_tensor_tensor(out=st[:, 3:4], in0=st[:, 2:3], scalar=1.0 / DM,
                                                   in1=st[:, 1:2], op0=ALU.mult, op1=ALU.subtract),
        inc=S["dv"])
    b.w("scalar", S["dv"])
    b.o("scalar", lambda E: E.activation(out=st[:, 4:5], in_=st[:, 3:4], func=AF.Sqrt,
                                         scale=-1.0 / DM, bias=eps_ap), inc=S["ac"])
    b.w("vector", S["ac"])
    b.o("vector", lambda E: E.reciprocal(out=st[:, 5:6], in_=st[:, 4:5]))
    b.o("vector", lambda E: E.scalar_tensor_tensor(out=st[:, 6:7], in0=st[:, 0:1], scalar=-1.0 / DM,
                                                   in1=st[:, 5:6], op0=ALU.mult, op1=ALU.mult))
    b.o("vector", lambda E: E.tensor_scalar(out=dst_tile, in0=src_tile, scalar1=st[:, 5:6],
                                            scalar2=st[:, 6:7], op0=ALU.mult, op1=ALU.add),
        inc=S["dv"])


def emit_tm2fm(b, tm_tile, fm_tensor, col, pT, idt, out_r32=False):
    S = b.S
    b.w("tensor", S["dv"])
    b.w("tensor", S["ac"])
    for fc in range(FC):
        b.o("tensor", lambda E, fc=fc: E.transpose(out=pT[:, fc * P:(fc + 1) * P],
                                                   in_=tm_tile[:, fc * P:(fc + 1) * P],
                                                   identity=idt),
            inc=(S["pe"] if fc == FC - 1 else None))
    b.w("scalar", S["pe"])
    dst = fm_tensor[:, :, col:col + P]
    if out_r32:
        dst = dst.bitcast(F32R)
    b.o("scalar", lambda E, dst=dst: E.copy(out=dst,
                                   in_=pT[:].rearrange("p (a d) -> p a d", d=P)), inc=S["ac"])


def emit_fm2tm(b, fm_tensor, col, tm_tile, pT, idt):
    S = b.S
    b.w("tensor", S["dv"])
    b.w("tensor", S["ac"])
    for fc in range(FC):
        b.o("tensor", lambda E, fc=fc: E.transpose(out=pT[:, fc * P:(fc + 1) * P],
                                                   in_=fm_tensor[:, fc, col:col + P],
                                                   identity=idt),
            inc=(S["pe"] if fc == FC - 1 else None))
    b.w("scalar", S["pe"])
    b.o("scalar", lambda E: E.copy(out=tm_tile, in_=pT[:]), inc=S["ac"])


def emit_proj(b, wb, rhs_fm, rhs_c0, cw, banks, evac, nko=FC, kfc=None):
    S = b.S
    b.w("tensor", S["lw"])
    b.w("tensor", S["dv"])
    b.w("tensor", S["ac"])
    kr = range(FC) if kfc is None else kfc
    kl = list(kr)
    for mo in range(nko):
        for j, kc in enumerate(kl):
            b.o("tensor", lambda E, mo=mo, kc=kc, j=j:
                E.matmul(out=banks[mo][:, 0:cw], lhsT=r32(wb[:, kc, mo * P:(mo + 1) * P]),
                         rhs=r32(rhs_fm[:, kc, rhs_c0:rhs_c0 + cw]),
                         start=(j == 0), stop=(j == len(kl) - 1)),
                inc=(S["pe"] if (mo == nko - 1 and j == len(kl) - 1) else None))
    b.w("scalar", S["pe"])
    b.w("vector", S["pe"])
    for mo in range(nko):
        evac(mo)



def emit_l0_attn(b, io):
    nc = b.nc
    S = b.S
    a = lambda n_, sh_, dt_: b.sb(n_, sh_, dt_)
    H1 = WARM + MT // 2
    H2 = MT // 2
    HM = MT // 2

    xn = a("l0a_xn", [P, FC, H1 + 1], F32)
    kv = a("l0a_kv", [P, FC, H1], F32)
    rb = a("l0a_rb", [P, FC, HM], F32)
    sb = a("l0a_sb", [P, H1 + 1], F32)
    mix = a("l0a_mix", [P, FC, 512], F32R)
    wb = a("l0a_wb", [P, FC, DM], F32R)
    xt = [a(f"l0a_xt{i}", [P, DM], F32) for i in range(2)]
    xm = [a(f"l0a_xm{i}", [P, DM], F32) for i in range(2)]
    sqb = a("l0a_sq", [P, DM], F32)
    st = a("l0a_st", [P, 8], F32)
    idt = a("l0a_id", [P, P], F32)
    carry = a("l0a_carry", [P, FC], F32)
    V = {n: a("l0a_" + n, [P, FC], F32) for n in
         ("tmk", "tmv", "tmr", "omk", "omv", "omr", "decay", "tf", "g", "bWoT")}
    V["bvec"] = a("l0a_bvec", [P, FC], F32R)
    ztm = a("l0a_ztm", [P, DM], F32)
    epsb = a("l0a_eps", [P, 1], F32)
    zln = a("l0a_zln", [P, DM], F32)
    xold = a("l0a_xold", [P, DM], F32)

    scr_bwo = nc.dram_tensor("l0a_scr_bwo", [DM], F32, kind="Internal").ap()
    ps = [b.psm(f"l0a_p{i}", [P, 512], F32) for i in range(6)]
    pT = b.psm("l0a_pT", [P, 1024], F32)
    banks = [p[:] for p in ps] + [pT[:, 0:512], pT[:, 512:1024]]

    x_ext = io["x_ext"]
    x1 = io["x1"]

    b.o("sync", lambda E: E.dma_start(out=idt[:], in_=io["ident"]), inc=S["lc"], k=16)
    for dst, src in (("tmk", "tmk"), ("tmv", "tmv"), ("tmr", "tmr"),
                     ("decay", "td"), ("tf", "tf"), ("g", "g"), ("bvec", "bv")):
        b.o("sync", lambda E, o=V[dst][:], s_=io[src]: E.dma_start(out=o, in_=s_), inc=S["lc"], k=16)
    b.w("vector", S["lc"])
    for dst, m in (("omk", "tmk"), ("omv", "tmv"), ("omr", "tmr")):
        b.o("vector", lambda E, o=V[dst][:], i=V[m][:]:
            E.tensor_scalar(out=o, in0=i, scalar1=-1.0, scalar2=1.0, op0=ALU.mult, op1=ALU.add))
    b.o("vector", lambda E: E.memset(xn[:, :, 0:1], 0.0))
    b.o("vector", lambda E: E.memset(epsb[:], LN_EPS), inc=S["dv"])
    b.w("scalar", S["lc"])
    b.o("scalar", lambda E: E.activation(out=V["decay"][:], in_=V["decay"][:], func=AF.Exp), inc=S["ac"])

    for h in (0, 1):
        if h == 0:
            Th, m0, tt0, ntt = H1, WARM, 0, H1 // P
        else:
            Th, m0, tt0, ntt = H2, 0, H1 // P, H2 // P

        if h == 1:
            b.w("vector", S["pe"])
            b.w("vector", S["ac"])
            for fc in range(FC):
                b.o("vector", lambda E, fc=fc: E.tensor_copy(out=xn[:, fc, 0:1], in_=xn[:, fc, H1:H1 + 1]),
                    inc=(S["dv"] if fc == FC - 1 else None))

        for i in range(ntt):
            tt = tt0 + i
            b.w("sync", S["dv"])
            b.w("sync", S["pe"])
            b.o("sync", lambda E, o=xt[i % 2][:], r0=tt * P: E.dma_start(out=o, in_=x_ext[r0:r0 + P, :]),
                inc=S["lx"], k=16)
            b.w("vector", S["lx"])
            b.w("scalar", S["lx"])
            emit_ln_tile(b, xt[i % 2][:], xm[i % 2][:], sqb, st, epsb[:, 0:1])
            emit_tm2fm(b, xm[i % 2][:], xn, 1 + i * P, pT, idt[:])

        chunks = []
        c = 0
        while c < Th:
            chunks.append((c, min(512, Th - c)))
            c += 512
        for proj in ("k", "v"):
            wsrc = io["wkT"] if proj == "k" else io["wvT"]
            tmn, omn = ("tmk", "omk") if proj == "k" else ("tmv", "omv")
            b.w("sync", S["pe"])
            b.w("sync", S["ac"])
            b.o("sync", lambda E, s_=wsrc: E.dma_start(out=wb[:], in_=s_.rearrange("(a p) d -> p a d", p=P)),
                inc=S["lw"], k=16)
            for (c0, cw) in chunks:
                b.w("vector", S["ac"])
                b.w("vector", S["pe"])
                for fc in range(FC):
                    b.o("vector", lambda E, fc=fc, c0=c0, cw=cw, omn=omn:
                        E.tensor_scalar(out=mix[:, fc, 0:cw], in0=xn[:, fc, c0:c0 + cw],
                                        scalar1=V[omn][:, fc:fc + 1], scalar2=None, op0=ALU.mult))
                    b.o("vector", lambda E, fc=fc, c0=c0, cw=cw, tmn=tmn:
                        E.scalar_tensor_tensor(out=mix[:, fc, 0:cw], in0=xn[:, fc, 1 + c0:1 + c0 + cw],
                                               scalar=V[tmn][:, fc:fc + 1], in1=mix[:, fc, 0:cw],
                                               op0=ALU.mult, op1=ALU.add),
                        inc=(S["dv"] if fc == FC - 1 else None))

                if proj == "k":
                    def evac(mo, c0=c0, cw=cw):
                        b.o("scalar", lambda E, mo=mo: E.copy(out=kv[:, mo, c0:c0 + cw],
                                                              in_=banks[mo][:, 0:cw]),
                            inc=(S["ac"] if mo == FC - 1 else None))
                else:
                    def evac(mo, c0=c0, cw=cw):
                        b.o("vector", lambda E, mo=mo: E.tensor_tensor(out=kv[:, mo, c0:c0 + cw],
                                                                       in0=banks[mo][:, 0:cw],
                                                                       in1=kv[:, mo, c0:c0 + cw],
                                                                       op=ALU.mult),
                            inc=(S["dv"] if mo == FC - 1 else None))
                emit_proj(b, wb, mix, 0, cw, banks, evac)

        b.w("sync", S["pe"])
        b.w("sync", S["ac"])
        b.o("sync", lambda E: E.dma_start(out=wb[:], in_=io["wrT"].rearrange("(a p) d -> p a d", p=P)),
            inc=S["lw"], k=16)
        for ci in range(HM // 512):
            c0 = m0 + ci * 512
            b.w("vector", S["ac"])
            b.w("vector", S["pe"])
            for fc in range(FC):
                b.o("vector", lambda E, fc=fc, c0=c0:
                    E.tensor_scalar(out=mix[:, fc, 0:512], in0=xn[:, fc, c0:c0 + 512],
                                    scalar1=V["omr"][:, fc:fc + 1], scalar2=None, op0=ALU.mult))
                b.o("vector", lambda E, fc=fc, c0=c0:
                    E.scalar_tensor_tensor(out=mix[:, fc, 0:512], in0=xn[:, fc, 1 + c0:1 + c0 + 512],
                                           scalar=V["tmr"][:, fc:fc + 1], in1=mix[:, fc, 0:512],
                                           op0=ALU.mult, op1=ALU.add),
                    inc=(S["dv"] if fc == FC - 1 else None))

            def evac(mo, ci=ci):
                b.o("scalar", lambda E, mo=mo: E.copy(out=r32(rb[:, mo, ci * 512:(ci + 1) * 512]),
                                                      in_=banks[mo][:, 0:512]),
                    inc=(S["ac"] if mo == FC - 1 else None))
            emit_proj(b, wb, mix, 0, 512, banks, evac)

        b.w("vector", S["ac"])
        for fc in range(FC):
            if h == 0:
                b.o("vector", lambda E: E.memset(sb[:, 0:1], 0.0))
            else:
                b.o("vector", lambda E, fc=fc: E.tensor_copy(out=sb[:, 0:1], in_=carry[:, fc:fc + 1]))
            b.o("vector", lambda E, fc=fc, Th=Th:
                E.tensor_tensor_scan(out=sb[:, 1:1 + Th],
                                     data0=V["decay"][:, fc:fc + 1].broadcast_to([P, Th]),
                                     data1=kv[:, fc, 0:Th], initial=sb[:, 0:1],
                                     op0=ALU.mult, op1=ALU.add))
            if h == 0:
                b.o("vector", lambda E, fc=fc, Th=Th:
                    E.tensor_copy(out=carry[:, fc:fc + 1], in_=sb[:, Th:Th + 1]))
            b.o("vector", lambda E, fc=fc, m0=m0, Th=Th:
                E.scalar_tensor_tensor(out=kv[:, fc, m0:Th], in0=kv[:, fc, m0:Th],
                                       scalar=V["tf"][:, fc:fc + 1], in1=sb[:, 1 + m0:1 + Th],
                                       op0=ALU.mult, op1=ALU.add))
            b.o("vector", lambda E, fc=fc, m0=m0, Th=Th:
                E.tensor_tensor(out=kv[:, fc, m0:Th], in0=kv[:, fc, m0:Th],
                                in1=rb[:, fc, 0:HM], op=ALU.mult),
                inc=(S["dv"] if fc == FC - 1 else None))

        for i in range(HM // P):
            emit_fm2tm(b, kv, m0 + i * P, xt[i % 2][:], pT, idt[:])
            b.w("vector", S["ac"])
            emit_ln_tile(b, xt[i % 2][:], xm[i % 2][:], sqb, st, epsb[:, 0:1])
            emit_tm2fm(b, xm[i % 2][:], rb, i * P, pT, idt[:], out_r32=True)

        b.w("sync", S["pe"])
        b.w("sync", S["ac"])
        b.o("sync", lambda E: E.dma_start(out=wb[:], in_=io["woT"].rearrange("(a p) d -> p a d", p=P)),
            inc=S["lw"], k=16)
        if h == 0:
            b.w("tensor", S["lw"])
            b.w("tensor", S["ac"])
            b.w("tensor", S["dv"])
            b.w("tensor", S["lc"])
            for half_n in range(2):
                for kc in range(FC):
                    b.o("tensor", lambda E, half_n=half_n, kc=kc:
                        E.matmul(out=ps[half_n][0:1, 0:512], lhsT=r32(V["bvec"][:, kc:kc + 1]),
                                 rhs=r32(wb[:, kc, half_n * 512:(half_n + 1) * 512]),
                                 start=(kc == 0), stop=(kc == FC - 1)),
                        inc=(S["pe"] if (half_n == 1 and kc == FC - 1) else None))
            b.w("scalar", S["pe"])
            for half_n in range(2):
                b.o("scalar", lambda E, half_n=half_n:
                    E.copy(out=zln[0:1, half_n * 512:(half_n + 1) * 512], in_=ps[half_n][0:1, 0:512]),
                    inc=(S["ac"] if half_n == 1 else None))
            b.w("sync", S["ac"])
            b.o("sync", lambda E: E.dma_start(out=scr_bwo, in_=zln[0:1, :]), inc=S["lc"], k=16)
            b.w("sync", S["lc"])
            b.o("sync", lambda E: E.dma_start(out=V["bWoT"][:], in_=scr_bwo.rearrange("(a p) -> p a", p=P),
                                              allow_slow_non_contiguous=True), inc=S["lc"], k=16)
        b.w("vector", S["lw"])
        b.w("vector", S["pe"])
        for kc in range(FC):
            b.o("vector", lambda E, kc=kc: E.tensor_single_scalar(out=wb[:, kc, :], in_=wb[:, kc, :],
                                                                  scalar=V["g"][:, kc:kc + 1], op=ALU.mult),
                inc=(S["dv"] if kc == FC - 1 else None))
        zv = xn
        b.w("vector", S["lc"])
        for ci in range(HM // 512):
            def evac(mo, ci=ci):
                b.o("vector", lambda E, mo=mo: E.tensor_single_scalar(out=zv[:, mo, ci * 512:(ci + 1) * 512],
                                                                      in_=banks[mo][:, 0:512],
                                                                      scalar=V["bWoT"][:, mo:mo + 1],
                                                                      op=ALU.add),
                    inc=(S["dv"] if mo == FC - 1 else None))
            emit_proj(b, wb, rb, ci * 512, 512, banks, evac)

        for i in range(HM // P):
            emit_fm2tm(b, zv, i * P, ztm[:], pT, idt[:])
            b.w("vector", S["ac"])
            emit_ln_tile(b, ztm[:], zln[:], sqb, st, epsb[:, 0:1])
            r0 = WARM + h * HM + i * P
            b.w("sync", S["dv"], S["dv"].n - 1)
            b.o("sync", lambda E, r0=r0: E.dma_start(out=xold[:], in_=x_ext[r0:r0 + P, :]),
                inc=S["lx"], k=16)
            b.w("vector", S["lx"])
            b.o("vector", lambda E: E.tensor_tensor(out=zln[:], in0=zln[:], in1=xold[:], op=ALU.add),
                inc=S["dv"])
            b.w("sync", S["dv"])
            b.o("sync", lambda E, q0=h * HM + i * P: E.dma_start(out=x1[q0:q0 + P, :], in_=zln[:]),
                inc=S["stx"], k=16)
            b.w("vector", S["stx"])

    b.w("sync", S["stx"])



RMS_EPS = 1e-6
NUNIT = 8
HB = 1024


def emit_l0_ffn(b, io):
    dbg = io.get("dbg")
    nc = b.nc
    S = b.S
    a = lambda n_, sh_, dt_: b.sb(n_, sh_, dt_)

    xf = a("l0f_xf", [P, FC, HB], F32)
    hb = a("l0f_h", [P, FC, HB], F32)
    h2b = a("l0f_h2", [P, FC, HB], F32)
    zb = a("l0f_z", [P, 8, HB], F32)
    wsl = [a(f"l0f_w{i}", [P, 4096], F32R) for i in range(2)]
    rwb = a("l0f_rw", [P, FC, 6], F32R)
    wgt = a("l0f_wgt", [P, 48], F32)
    silb = a("l0f_sil", [P, 512], F32)
    xt = [a(f"l0f_xt{i}", [P, DM], F32) for i in range(2)]
    xm = [a(f"l0f_xm{i}", [P, DM], F32) for i in range(2)]
    sqb = a("l0f_sq", [P, DM], F32)
    st = a("l0f_st", [P, 8], F32)
    idt = a("l0f_id", [P, P], F32)
    epsb = a("l0f_eps", [P, 2], F32)
    bupv = a("l0f_bup", [P, 64], F32)
    bgv = a("l0f_bg", [P, 64], F32)
    onesr = a("l0f_ones", [1, P], F32R)
    brow = a("l0f_brow", [1, 12], F32R)
    bdals = a("l0f_bd", [1, DM], F32R)
    nwrow = a("l0f_nwr", [1, DM], F32R)
    nwb = a("l0f_nw", [P, DM], F32)
    gtile = a("l0f_gt", [P, 16], F32)

    ps = [b.psm(f"l0f_p{i}", [P, 512], F32) for i in range(6)]
    pT = b.psm("l0f_pT", [P, 1024], F32)

    x1 = io["x1"]
    x2 = io["x2"]

    b.o("sync", lambda E: E.dma_start(out=idt[:], in_=io["ident"]), inc=S["lc"], k=16)
    b.o("sync", lambda E: E.dma_start(out=bupv[:], in_=io["bup"]), inc=S["lc"], k=16)
    b.o("sync", lambda E: E.dma_start(out=bgv[:], in_=io["bg"]), inc=S["lc"], k=16)
    b.o("sync", lambda E: E.dma_start(out=onesr[:], in_=io["ones"]), inc=S["lc"], k=16)
    b.w("vector", S["lc"])
    b.o("vector", lambda E: E.memset(epsb[:, 0:1], LN_EPS))
    b.o("vector", lambda E: E.memset(epsb[:, 1:2], RMS_EPS))

    def rms_tile(src, dst, mul_nw):
        b.w("scalar", S["dv"])
        b.o("scalar", lambda E: E.activation(out=sqb[:], in_=src, func=AF.Square))
        b.w("vector", S["ac"])
        b.o("vector", lambda E: E.reduce_sum(out=st[:, 0:1], in_=sqb[:], axis=AX))
        b.w("scalar", S["dv"])
        b.o("scalar", lambda E: E.activation(out=st[:, 4:5], in_=st[:, 0:1], func=AF.Sqrt,
                                             scale=1.0 / DM, bias=epsb[:, 1:2]))
        b.w("vector", S["ac"])
        b.o("vector", lambda E: E.reciprocal(out=st[:, 5:6], in_=st[:, 4:5]))
        b.o("vector", lambda E: E.tensor_single_scalar(out=dst, in_=src, scalar=st[:, 5:6],
                                                       op=ALU.mult))
        if mul_nw:
            b.o("vector", lambda E: E.tensor_tensor(out=dst, in0=dst, in1=nwb[:], op=ALU.mult))

    for h in (0, 1):
        t0 = h * HB

        for i in range(HB // P):
            b.w("sync", S["dv"])
            b.w("sync", S["pe"])
            b.o("sync", lambda E, r0=t0 + i * P, i=i: E.dma_start(out=xt[i % 2][:], in_=x1[r0:r0 + P, :]),
                inc=S["lx"], k=16)
            b.w("vector", S["lx"])
            b.w("scalar", S["lx"])
            emit_ln_tile(b, xt[i % 2][:], xm[i % 2][:], sqb, st, epsb[:, 0:1])
            emit_tm2fm(b, xm[i % 2][:], xf, i * P, pT, idt[:], out_r32=True)

        for j in (0, 1):
            b.w("sync", S["dv"])
            b.w("sync", S["ac"])
            b.w("sync", S["pe"])
            b.o("sync", lambda E, j=j: E.dma_start(out=brow[0:1, 0:6], in_=io["rb"][j:j + 1, :]),
                inc=S["lc"], k=16)
            b.o("sync", lambda E, j=j: E.dma_start(out=brow[0:1, 6:12], in_=io["gb"][j:j + 1, :]),
                inc=S["lc"], k=16)
            b.o("sync", lambda E, j=j: E.dma_start(
                out=rwb[:], in_=io["routerW"][j].rearrange("(a p) e -> p a e", p=P)),
                inc=S["lw"], k=16)
            b.o("sync", lambda E, j=j: E.dma_start(out=nwrow[:], in_=io["normw"][j:j + 1, :]),
                inc=S["lc"], k=16)
            b.w("tensor", S["lc"])
            b.w("tensor", S["ac"])
            b.w("tensor", S["dv"])
            b.o("tensor", lambda E: E.matmul(out=pT[:, 0:512], lhsT=onesr[:],
                                             rhs=nwrow[0:1, 0:512], start=True, stop=True))
            b.o("tensor", lambda E: E.matmul(out=pT[:, 512:1024], lhsT=onesr[:],
                                             rhs=nwrow[0:1, 512:1024], start=True, stop=True),
                inc=S["pe"])
            b.w("vector", S["pe"])
            b.o("vector", lambda E: E.tensor_copy(out=nwb[:], in_=pT[:]))

            for tt in range(HB // P):
                b.w("tensor", S["lw"])
                b.w("tensor", S["dv"])
                b.w("tensor", S["ac"])
                for grp in range(2):
                    pb = ps[4 + grp]
                    for kc in range(FC):
                        b.o("tensor", lambda E, kc=kc, tt=tt, pb=pb:
                            E.matmul(out=pb[:, 0:6], lhsT=r32(xf[:, kc, tt * P:(tt + 1) * P]),
                                     rhs=rwb[:, kc, :], start=(kc == 0), stop=False))
                    if grp == 0:
                        b.o("tensor", lambda E, pb=pb:
                            E.matmul(out=pb[:, 0:6], lhsT=onesr[:], rhs=brow[0:1, 0:6],
                                     start=False, stop=True))
                    else:
                        b.o("tensor", lambda E, pb=pb:
                            E.matmul(out=pb[:, 0:6], lhsT=onesr[:], rhs=brow[0:1, 0:6],
                                     start=False, stop=False))
                        b.o("tensor", lambda E, pb=pb:
                            E.matmul(out=pb[:, 0:6], lhsT=onesr[:], rhs=brow[0:1, 6:12],
                                     start=False, stop=True), inc=S["pe"])
                b.w("vector", S["pe"])
                b.o("vector", lambda E: E.tensor_copy(out=gtile[:, 0:6], in_=ps[4][:, 0:6]))
                b.o("vector", lambda E: E.tensor_copy(out=gtile[:, 8:14], in_=ps[5][:, 0:6]))
                b.o("vector", lambda E: E.reduce_max(out=st[:, 0:1], in_=gtile[:, 8:14], axis=AX))
                b.o("vector", lambda E: E.tensor_scalar(out=sqb[:, 0:6], in0=gtile[:, 8:14],
                                                        scalar1=st[:, 0:1], scalar2=-1e30,
                                                        op0=ALU.is_ge, op1=ALU.mult))
                b.o("vector", lambda E: E.tensor_tensor(out=sqb[:, 8:14], in0=gtile[:, 8:14],
                                                        in1=sqb[:, 0:6], op=ALU.add))
                b.o("vector", lambda E: E.reduce_max(out=st[:, 1:2], in_=sqb[:, 8:14], axis=AX))
                b.o("vector", lambda E: E.tensor_scalar(out=sqb[:, 0:6], in0=gtile[:, 8:14],
                                                        scalar1=st[:, 1:2], scalar2=None,
                                                        op0=ALU.is_ge))
                b.o("vector", lambda E: E.tensor_scalar(out=sqb[:, 8:14], in0=sqb[:, 0:6],
                                                        scalar1=1e30, scalar2=-1e30,
                                                        op0=ALU.mult, op1=ALU.add))
                b.o("vector", lambda E: E.tensor_tensor(out=sqb[:, 16:22], in0=gtile[:, 0:6],
                                                        in1=sqb[:, 0:6], op=ALU.mult))
                b.o("vector", lambda E: E.tensor_tensor(out=sqb[:, 16:22], in0=sqb[:, 16:22],
                                                        in1=sqb[:, 8:14], op=ALU.add))
                b.o("vector", lambda E: E.reduce_max(out=st[:, 2:3], in_=sqb[:, 16:22], axis=AX))
                b.o("vector", lambda E: E.tensor_single_scalar(out=st[:, 3:4], in_=st[:, 2:3],
                                                               scalar=-1.0, op=ALU.mult))
                b.w("scalar", S["dv"])
                b.o("scalar", lambda E: E.activation(out=sqb[:, 24:30], in_=sqb[:, 16:22],
                                                     func=AF.Exp, bias=st[:, 3:4],
                                                     accum_out=st[:, 6:7]))
                b.w("vector", S["ac"])
                b.o("vector", lambda E: E.reciprocal(out=st[:, 7:8], in_=st[:, 6:7]))
                b.o("vector", lambda E, tt=tt: E.tensor_single_scalar(out=wgt[:, tt * 6:(tt + 1) * 6],
                                                                      in_=sqb[:, 24:30],
                                                                      scalar=st[:, 7:8], op=ALU.mult))
                if dbg is not None and h == 0 and j == 0 and tt == 0:
                    b.w("sync", S["dv"])
                    b.o("sync", lambda E: E.dma_start(out=dbg[0][:, 0:6], in_=gtile[:, 0:6]), inc=S["stx"], k=16)
                    b.o("sync", lambda E: E.dma_start(out=dbg[0][:, 8:14], in_=gtile[:, 8:14]), inc=S["stx"], k=16)
                    b.o("sync", lambda E: E.dma_start(out=dbg[1][:, 0:6], in_=wgt[:, 0:6]), inc=S["stx"], k=16)

            banks8 = [p[:] for p in ps] + [pT[:, 0:512], pT[:, 512:1024]]
            ring = {}
            lwp = [S["lw"], S["lw2"]]
            slabs = []

            def mk_half(u, hm, kind):
                def body(wv, lw):
                    b.w("tensor", lw)
                    b.w("tensor", S["dv"])
                    b.w("tensor", S["ac"])
                    if kind == "up":
                        def evac_h(gmo, u=u):
                            b.o("vector", lambda E, gmo=gmo, u=u, ci=body.ci:
                                E.tensor_scalar(out=r32(hb[:, gmo, ci * 512:(ci + 1) * 512]),
                                                in0=banks8[gmo % 8][:, 0:512],
                                                scalar1=bupv[:, u * FC + gmo:u * FC + gmo + 1],
                                                scalar2=None, op0=ALU.add))
                        ev = evac_h
                        rhs = xf
                    else:
                        def evac_g(gmo, u=u):
                            b.w("scalar", S["dv"])
                            b.o("scalar", lambda E, gmo=gmo, u=u:
                                E.activation(out=silb[:, 0:512], in_=banks8[gmo % 8][:, 0:512],
                                             func=AF.Silu,
                                             bias=bgv[:, u * FC + gmo:u * FC + gmo + 1]))
                            b.w("vector", S["ac"])
                            b.o("vector", lambda E, gmo=gmo, ci=body.ci:
                                E.tensor_tensor(out=r32(h2b[:, gmo, ci * 512:(ci + 1) * 512]),
                                                in0=silb[:, 0:512],
                                                in1=hb[:, gmo, ci * 512:(ci + 1) * 512], op=ALU.mult))
                        ev = evac_g
                        rhs = hb
                    wv3 = wv.rearrange("p (a d) -> p a d", d=512)
                    for ci in range(HB // 512):
                        body.ci = ci
                        emit_proj_rot(b, wv3, rhs, ci * 512, banks8, ev, nko=4,
                                      ring=ring, mo_off=hm * 4)
                return body

            def mk_down(u, nh):
                is_sh = u < 2
                e = u - 2
                def body(wv, lw):
                    b.w("tensor", lw)
                    b.w("tensor", S["dv"])
                    if nh == 0:
                        b.w("sync", S["pe"])
                        b.o("sync", lambda E, u=u: E.dma_start(
                            out=bdals[:], in_=io["bdall"][0:1, u * DM:(u + 1) * DM]),
                            inc=S["lc"], k=16)
                    b.w("tensor", S["lc"])
                    wv3 = wv.rearrange("p (a d) -> p a d", d=512)
                    for tt in range(HB // P):
                        gb_ = tt % 8
                        bank = banks8[gb_]
                        if gb_ in ring:
                            _, cdv, cac = ring[gb_]
                            b.w("tensor", S["dv"], cdv)
                            b.w("tensor", S["ac"], cac)
                        for kc in range(FC):
                            b.o("tensor", lambda E, tt=tt, kc=kc, bank=bank:
                                E.matmul(out=bank[:, 0:512],
                                         lhsT=r32(h2b[:, kc, tt * P:(tt + 1) * P]),
                                         rhs=wv3[:, kc, :],
                                         start=(kc == 0), stop=False))
                        b.o("tensor", lambda E, nh=nh, bank=bank:
                            E.matmul(out=bank[:, 0:512], lhsT=onesr[:],
                                     rhs=bdals[0:1, nh * 512:(nh + 1) * 512],
                                     start=False, stop=True), inc=S["pe"])
                        b.w("vector", S["pe"])
                        if u == 0:
                            b.o("vector", lambda E, tt=tt, nh=nh, bank=bank:
                                E.tensor_copy(out=zb[:, tt, nh * 512:(nh + 1) * 512],
                                              in_=bank[:, 0:512]))
                        elif is_sh:
                            b.o("vector", lambda E, tt=tt, nh=nh, bank=bank:
                                E.tensor_tensor(out=zb[:, tt, nh * 512:(nh + 1) * 512],
                                                in0=bank[:, 0:512],
                                                in1=zb[:, tt, nh * 512:(nh + 1) * 512], op=ALU.add))
                        else:
                            b.o("vector", lambda E, tt=tt, nh=nh, e=e, bank=bank:
                                E.scalar_tensor_tensor(out=zb[:, tt, nh * 512:(nh + 1) * 512],
                                                       in0=bank[:, 0:512],
                                                       scalar=wgt[:, tt * 6 + e:tt * 6 + e + 1],
                                                       in1=zb[:, tt, nh * 512:(nh + 1) * 512],
                                                       op0=ALU.mult, op1=ALU.add))
                        ring[gb_] = ("both", S["dv"].n, S["ac"].n)
                return body

            for u in range(NUNIT):
                is_sh = u < 2
                e = u - 2
                if is_sh:
                    upsrc = io["shWup"].rearrange("(a p) d -> p a d", p=P)[:, :, u * DM:(u + 1) * DM]
                    gsrc = io["shWg"].rearrange("(a p) d -> p a d", p=P)
                    dsrc = io["shWdown"].rearrange("(a p) d -> p a d", p=P)
                else:
                    upsrc = io["expWup"][e].rearrange("(a p) d -> p a d", p=P)
                    gsrc = io["expWg"][e].rearrange("(a p) d -> p a d", p=P)
                    dsrc = io["expWdown"][e].rearrange("(a p) d -> p a d", p=P)
                for hm in range(2):
                    slabs.append((upsrc[:, :, hm * 512:(hm + 1) * 512], mk_half(u, hm, "up")))
                for hm in range(2):
                    slabs.append((gsrc[:, :, hm * 512:(hm + 1) * 512], mk_half(u, hm, "g")))
                for nh in range(2):
                    slabs.append((dsrc[:, :, nh * 512:(nh + 1) * 512], mk_down(u, nh)))

            for i, (ws, body) in enumerate(slabs):
                if i == 0:
                    b.w("sync", S["pe"])
                    b.o("sync", lambda E, ws=ws: E.dma_start(
                        out=wsl[0][:].rearrange("p (a d) -> p a d", d=512), in_=ws),
                        inc=lwp[0], k=16)
                if i + 1 < len(slabs):
                    ws2 = slabs[i + 1][0]
                    b.w("sync", S["pe"])
                    b.o("sync", lambda E, ws2=ws2, j=(i + 1) % 2: E.dma_start(
                        out=wsl[j][:].rearrange("p (a d) -> p a d", d=512), in_=ws2),
                        inc=lwp[(i + 1) % 2], k=16)
                body(wsl[i % 2][:], lwp[i % 2])

            last = (j == 1)
            for tt in range(HB // P):
                emit_fm2tm(b, xf, tt * P, xt[tt % 2][:], pT, idt[:])
                b.w("vector", S["ac"])
                b.o("vector", lambda E, tt=tt: E.tensor_tensor(out=xt[tt % 2][:], in0=xt[tt % 2][:],
                                                               in1=zb[:, tt, :], op=ALU.add))
                rms_tile(xt[tt % 2][:], xm[tt % 2][:], True)
                if not last:
                    emit_tm2fm(b, xm[tt % 2][:], xf, tt * P, pT, idt[:], out_r32=True)
                else:
                    emit_ln_tile(b, xm[tt % 2][:], xt[tt % 2][:], sqb, st, epsb[:, 0:1])
                    b.w("sync", S["dv"])
                    b.o("sync", lambda E, r0=t0 + tt * P: E.dma_start(out=sqb[:], in_=x1[r0:r0 + P, :]),
                        inc=S["lx"], k=16)
                    b.w("vector", S["lx"])
                    b.o("vector", lambda E, tt=tt: E.tensor_tensor(out=xt[tt % 2][:], in0=xt[tt % 2][:],
                                                                   in1=sqb[:], op=ALU.add))
                    b.w("sync", S["dv"])
                    b.o("sync", lambda E, r0=t0 + tt * P, tt=tt:
                        E.dma_start(out=x2[r0:r0 + P, :], in_=xt[tt % 2][:]), inc=S["stx"], k=16)
                    b.w("vector", S["stx"])
                    b.w("scalar", S["stx"])

    b.w("sync", S["stx"])


def emit_proj_rot(b, wb, rhs_fm, rhs_c0, banks8, evac, nko=FC, ring=None, mo_off=0):
    S = b.S
    for mo in range(nko):
        gmo = mo_off + mo
        bi = gmo % 8
        if ring is not None and bi in ring:
            _, cdv, cac = ring[bi]
            b.w("tensor", S["dv"], cdv)
            b.w("tensor", S["ac"], cac)
        for kc in range(FC):
            b.o("tensor", lambda E, mo=mo, kc=kc, bi=bi:
                E.matmul(out=banks8[bi][:, 0:512], lhsT=wb[:, kc, mo * P:(mo + 1) * P],
                         rhs=r32(rhs_fm[:, kc, rhs_c0:rhs_c0 + 512]),
                         start=(kc == 0), stop=(kc == FC - 1)),
                inc=(S["pe"] if kc == FC - 1 else None))
        b.w("vector", S["pe"])
        b.w("scalar", S["pe"])
        evac(gmo)
        if ring is not None:
            ring[bi] = ("both", S["dv"].n, S["ac"].n)


def _vec_fm(v):
    return np.ascontiguousarray(v.reshape(FC, P).T)


def build_l0a(nc=None, races=True):
    nc = nc or bass.Bass("TRN2", target_bir_lowering=False, debug=False, detect_race_conditions=races)
    io = {
        "x_ext": nc.dram_tensor("x_ext", [TEXT, DM], F32, kind="ExternalInput").ap(),
        "wkT": nc.dram_tensor("wkT", [DM, DM], F32R, kind="ExternalInput").ap(),
        "wvT": nc.dram_tensor("wvT", [DM, DM], F32R, kind="ExternalInput").ap(),
        "wrT": nc.dram_tensor("wrT", [DM, DM], F32R, kind="ExternalInput").ap(),
        "woT": nc.dram_tensor("woT", [DM, DM], F32R, kind="ExternalInput").ap(),
        "tmk": nc.dram_tensor("tmk", [P, FC], F32, kind="ExternalInput").ap(),
        "tmv": nc.dram_tensor("tmv", [P, FC], F32, kind="ExternalInput").ap(),
        "tmr": nc.dram_tensor("tmr", [P, FC], F32, kind="ExternalInput").ap(),
        "td": nc.dram_tensor("td", [P, FC], F32, kind="ExternalInput").ap(),
        "tf": nc.dram_tensor("tf", [P, FC], F32, kind="ExternalInput").ap(),
        "g": nc.dram_tensor("g", [P, FC], F32, kind="ExternalInput").ap(),
        "bv": nc.dram_tensor("bv", [P, FC], F32R, kind="ExternalInput").ap(),
        "ident": nc.dram_tensor("ident", [P, P], F32, kind="ExternalInput").ap(),
        "x1": nc.dram_tensor("x1", [MT, DM], F32, kind="ExternalOutput").ap(),
    }
    b = B(nc)
    b.begin()
    emit_l0_attn(b, io)
    b.end()
    b.emit()
    return nc


def l0a_core_inputs(c, x, wk, wv, wr, wo, tmk, tmv, tmr, td, tf, g, bv):
    seq, q = c // 4, c % 4
    start = q * MT
    if start >= WARM:
        ext = x[seq, start - WARM:start + MT]
    else:
        ext = np.concatenate([np.zeros((WARM - start, DM), np.float32),
                              x[seq, 0:start + MT]], axis=0)
    return {
        "x_ext": np.ascontiguousarray(ext, np.float32),
        "wkT": np.ascontiguousarray(wk.T), "wvT": np.ascontiguousarray(wv.T),
        "wrT": np.ascontiguousarray(wr.T), "woT": np.ascontiguousarray(wo.T),
        "tmk": _vec_fm(np.asarray(tmk).reshape(-1)), "tmv": _vec_fm(np.asarray(tmv).reshape(-1)),
        "tmr": _vec_fm(np.asarray(tmr).reshape(-1)),
        "td": _vec_fm(np.asarray(td).reshape(-1)), "tf": _vec_fm(np.asarray(tf).reshape(-1)),
        "g": _vec_fm(np.asarray(g).reshape(-1)), "bv": _vec_fm(np.asarray(bv).reshape(-1)),
        "ident": np.eye(P, dtype=np.float32),
    }




NSEQ = 8192
NKB5 = NSEQ // 512
NKB1 = NSEQ // P
BIGNEG = -1e30


def emit_l1_qkv(b, io):
    nc = b.nc
    S = b.S
    a = lambda n_, sh_, dt_: b.sb(n_, sh_, dt_)
    xn = a("l1q_xn", [P, FC, MT], F32)
    wq = a("l1q_wq", [P, FC, 256], F32R)
    wk = a("l1q_wk", [P, FC, 256], F32R)
    wv = a("l1q_wv", [P, FC, 256], F32R)
    xt = [a(f"l1q_xt{i}", [P, DM], F32) for i in range(2)]
    xm = [a(f"l1q_xm{i}", [P, DM], F32) for i in range(2)]
    sqb = a("l1q_sq", [P, DM], F32)
    st = a("l1q_st", [P, 8], F32)
    idt = a("l1q_id", [P, P], F32)
    epsb = a("l1q_eps", [P, 1], F32)
    qkst = a("l1q_qk", [P, 2, MT], F32)
    vdst = a("l1q_vd", [P, 16, 256], F32)
    ps = [b.psm(f"l1q_p{i}", [P, 512], F32) for i in range(6)]
    pT = b.psm("l1q_pT", [P, 1024], F32)

    x2 = io["x2"]
    b.o("sync", lambda E: E.dma_start(out=idt[:], in_=io["ident"]), inc=S["lc"], k=16)
    b.o("sync", lambda E: E.dma_start(out=wq[:], in_=io["wq"].rearrange("(a p) d -> p a d", p=P)),
        inc=S["lw"], k=16)
    b.o("sync", lambda E: E.dma_start(out=wk[:], in_=io["wk"].rearrange("(a p) d -> p a d", p=P)),
        inc=S["lw"], k=16)
    b.o("sync", lambda E: E.dma_start(out=wv[:], in_=io["wvd"].rearrange("(a p) d -> p a d", p=P)),
        inc=S["lw"], k=16)
    b.w("vector", S["lc"])
    b.o("vector", lambda E: E.memset(epsb[:], LN_EPS))

    for i in range(MT // P):
        b.w("sync", S["dv"])
        b.w("sync", S["pe"])
        b.o("sync", lambda E, r0=i * P, i=i: E.dma_start(out=xt[i % 2][:], in_=x2[r0:r0 + P, :]),
            inc=S["lx"], k=16)
        b.w("vector", S["lx"])
        b.w("scalar", S["lx"])
        emit_ln_tile(b, xt[i % 2][:], xm[i % 2][:], sqb, st, epsb[:, 0:1])
        emit_tm2fm(b, xm[i % 2][:], xn, i * P, pT, idt[:], out_r32=True)

    for which, wbt, outd in (("q", wq, io["qf"]), ("k", wk, io["kf"])):
        b.w("tensor", S["lw"])
        b.w("tensor", S["ac"])
        b.w("tensor", S["dv"])
        for dc in range(2):
            for ci in range(MT // 512):
                bank = ps[(dc * 4 + ci) % 4]
                b.w("tensor", S["dv"])
                for kc in range(FC):
                    b.o("tensor", lambda E, dc=dc, ci=ci, kc=kc, bank=bank, wbt=wbt:
                        E.matmul(out=bank[:, 0:512], lhsT=wbt[:, kc, dc * P:(dc + 1) * P],
                                 rhs=r32(xn[:, kc, ci * 512:(ci + 1) * 512]),
                                 start=(kc == 0), stop=(kc == FC - 1)),
                        inc=(S["pe"] if kc == FC - 1 else None))
                b.w("vector", S["pe"])
                b.o("vector", lambda E, dc=dc, ci=ci, bank=bank:
                    E.tensor_copy(out=qkst[:, dc, ci * 512:(ci + 1) * 512], in_=bank[:, 0:512]))
        b.w("sync", S["dv"])
        b.o("sync", lambda E, outd=outd: E.dma_start(
            out=outd.rearrange("(a p) t -> p a t", p=P), in_=qkst[:]), inc=S["stx"], k=16)
        b.w("vector", S["stx"])

    b.w("tensor", S["lw"])
    for half in range(2):
        for i in range(8):
            tt = half * 8 + i
            bank = ps[i % 4]
            b.w("tensor", S["dv"])
            for kc in range(FC):
                b.o("tensor", lambda E, tt=tt, kc=kc, bank=bank:
                    E.matmul(out=bank[:, 0:256], lhsT=r32(xn[:, kc, tt * P:(tt + 1) * P]),
                             rhs=wv[:, kc, :], start=(kc == 0), stop=(kc == FC - 1)),
                    inc=(S["pe"] if kc == FC - 1 else None))
            b.w("vector", S["pe"])
            b.o("vector", lambda E, i=i, bank=bank:
                E.tensor_copy(out=vdst[:, i, :], in_=bank[:, 0:256]))
        b.w("sync", S["dv"])
        b.o("sync", lambda E, half=half: E.dma_start(
            out=io["vdf"][half * 1024:(half + 1) * 1024, :].rearrange("(a p) d -> p a d", p=P),
            in_=vdst[:, 0:8, :]), inc=S["stx"], k=16)
        b.w("vector", S["stx"])
    b.w("sync", S["stx"])


def emit_l1_attn(b, io):
    nc = b.nc
    S = b.S
    a = lambda n_, sh_, dt_: b.sb(n_, sh_, dt_)
    qfs = a("l1a_q", [P, 2, MT], F32R)
    kfs = a("l1a_k", [P, 2, NSEQ], F32R)
    vds = a("l1a_v", [P, NKB1, 256], F32R)
    attn_d = a("l1a_ad", [P, 2, MT], F32)
    wvupb = a("l1a_wu", [P, 2, DM], F32R)
    PTs = a("l1a_pt", [P, 512], F32)
    msk = a("l1a_msk", [P, 512], F32)
    tmpc = a("l1a_tmp", [P, 512], F32)
    mbc = a("l1a_mbc", [P, 512], F32)
    stat = a("l1a_stat", [P, 20], F32)
    qgl = a("l1a_qgl", [P, 16], F32)
    pcol = a("l1a_pc", [P, 1], F32)
    iotab = a("l1a_io", [P, 512], F32)
    qoffc = a("l1a_qo", [P, 1], F32)
    onesr = a("l1a_ones", [1, P], F32R)
    drow = a("l1a_dr", [1, 512], F32R)
    dstg = a("l1a_ds", [4, P], F32)
    idt = a("l1a_id", [P, P], F32)
    epsb = a("l1a_eps", [P, 1], F32)
    xt = [a(f"l1a_xt{i}", [P, DM], F32) for i in range(2)]
    ztm = a("l1a_ztm", [P, DM], F32)
    sqb = a("l1a_sq", [P, DM], F32)
    st = a("l1a_st", [P, 8], F32)
    scr_d = nc.dram_tensor("l1a_scr", [512], F32R, kind="Internal").ap()

    ps = [b.psm(f"l1a_p{i}", [P, 512], F32) for i in range(6)]
    pT = b.psm("l1a_pT", [P, 1024], F32)

    x2 = io["x2"]
    x3 = io["x3"]
    SCL = 1.0 / 16.0

    b.o("sync", lambda E: E.dma_start(out=qfs[:], in_=io["qf"].rearrange("(a p) t -> p a t", p=P)),
        inc=S["lw"], k=16)
    b.o("sync", lambda E: E.dma_start(out=kfs[:], in_=io["kfull"].rearrange("(a p) t -> p a t", p=P)),
        inc=S["lw"], k=16)
    b.o("sync", lambda E: E.dma_start(out=vds[:], in_=io["vdfull"].rearrange("(a p) d -> p a d", p=P)),
        inc=S["lw"], k=16)
    b.o("sync", lambda E: E.dma_start(out=wvupb[:], in_=io["wvup"].rearrange("(a p) d -> p a d", p=P)),
        inc=S["lw"], k=16)
    b.o("sync", lambda E: E.dma_start(out=qgl[:], in_=io["qglob"]), inc=S["lc"], k=16)
    b.o("sync", lambda E: E.dma_start(out=pcol[:], in_=io["pcol"]), inc=S["lc"], k=16)
    b.o("sync", lambda E: E.dma_start(out=iotab[:], in_=io["iotab"]), inc=S["lc"], k=16)
    b.o("sync", lambda E: E.dma_start(out=qoffc[:], in_=io["qoffc"]), inc=S["lc"], k=16)
    b.o("sync", lambda E: E.dma_start(out=onesr[:], in_=io["ones"]), inc=S["lc"], k=16)
    b.o("sync", lambda E: E.dma_start(out=idt[:], in_=io["ident"]), inc=S["lc"], k=16)
    b.w("vector", S["lc"])
    b.w("vector", S["lw"])
    b.o("vector", lambda E: E.memset(epsb[:], LN_EPS))

    PTs2 = [PTs, a("l1a_pt2", [P, 512], F32)]
    tmp2 = [tmpc, a("l1a_tmp2", [P, 512], F32)]
    msk2 = [msk, a("l1a_msk2", [P, 512], F32)]
    b.w("tensor", S["lw"])
    b.w("tensor", S["lc"])
    for qg in range(4):
        ring1 = {}
        ringT = {}
        for sq_ in range(4):
            qt = qg * 4 + sq_
            b.o("vector", lambda E, sq_=sq_: E.memset(stat[:, sq_:sq_ + 1], 0.0))
            for kb in range(NKB5):
                bi = kb % 4
                bank = ps[bi]
                if bi in ring1:
                    b.w("tensor", S["dv"], ring1[bi])
                for dc in range(2):
                    b.o("tensor", lambda E, qt=qt, kb=kb, dc=dc, bank=bank:
                        E.matmul(out=bank[:, 0:512], lhsT=qfs[:, dc, qt * P:(qt + 1) * P],
                                 rhs=kfs[:, dc, kb * 512:(kb + 1) * 512],
                                 start=(dc == 0), stop=(dc == 1)),
                        inc=(S["pe"] if dc == 1 else None))
                pe_cnt = S["pe"].n
                ti = kb % 2
                b.o("vector", lambda E, qt=qt, kb=kb: E.tensor_scalar(
                    out=stat[:, 8:9], in0=qgl[:, qt:qt + 1], scalar1=float(-kb * 512),
                    scalar2=None, op0=ALU.add))
                b.o("vector", lambda E, ti=ti: E.tensor_scalar(
                    out=msk2[ti][:], in0=iotab[:], scalar1=stat[:, 8:9], scalar2=BIGNEG,
                    op0=ALU.is_gt, op1=ALU.mult))
                b.w("vector", S["pe"], pe_cnt)
                if ti in ringT:
                    b.w("vector", S["ac"], ringT[ti])
                b.o("vector", lambda E, bank=bank, ti=ti: E.scalar_tensor_tensor(
                    out=tmp2[ti][:], in0=bank[:, 0:512], scalar=SCL, in1=msk2[ti][:],
                    op0=ALU.mult, op1=ALU.add))
                ring1[bi] = S["dv"].n
                b.w("scalar", S["dv"], S["dv"].n)
                b.o("scalar", lambda E, ti=ti: E.activation(out=r32(PTs2[ti][:]), in_=tmp2[ti][:],
                                                            func=AF.Exp,
                                                            accum_out=stat[:, 9 + ti:10 + ti]))
                ringT[ti] = S["ac"].n
                b.w("vector", S["ac"], S["ac"].n)
                b.o("vector", lambda E, sq_=sq_, ti=ti: E.tensor_tensor(
                    out=stat[:, sq_:sq_ + 1], in0=stat[:, sq_:sq_ + 1],
                    in1=stat[:, 9 + ti:10 + ti], op=ALU.add))
        b.o("vector", lambda E: E.reciprocal(out=stat[:, 4:8], in_=stat[:, 0:4]))
        b.w("tensor", S["dv"])
        b.w("tensor", S["ac"])
        b.o("tensor", lambda E: E.transpose(out=pT[0:4, 0:P], in_=stat[:, 4:8], identity=idt[:]),
            inc=S["pe"])
        b.w("vector", S["pe"])
        b.o("vector", lambda E: E.tensor_copy(out=dstg[:], in_=pT[0:4, 0:P]))
        b.w("sync", S["dv"])
        b.o("sync", lambda E: E.dma_start(out=scr_d.rearrange("(a t) -> a t", a=4).bitcast(F32),
                                          in_=dstg[:]), inc=S["lc"], k=16)
        b.w("sync", S["lc"])
        b.o("sync", lambda E: E.dma_start(out=drow[:], in_=scr_d.rearrange("(o t) -> o t", o=1)),
            inc=S["lc"], k=16)
        b.w("tensor", S["lc"])
        b.o("tensor", lambda E: E.matmul(out=pT[:, 0:512], lhsT=onesr[:], rhs=drow[0:1, :],
                                         start=True, stop=True), inc=S["pe"])
        b.w("vector", S["pe"])
        b.o("vector", lambda E: E.tensor_copy(out=mbc[:], in_=pT[:, 0:512]))

        pvb = (ps[2], ps[3])
        ringS = {}
        ringP = {}
        for kb in range(NKB1):
            bi = kb % 2
            bank = ps[bi]
            ti = kb % 2
            if bi in ringS:
                b.w("tensor", S["ac"], ringS[bi])
            for dc in range(2):
                b.o("tensor", lambda E, qg=qg, kb=kb, dc=dc, bank=bank:
                    E.matmul(out=bank[:, 0:512], lhsT=kfs[:, dc, kb * P:(kb + 1) * P],
                             rhs=qfs[:, dc, qg * 512:(qg + 1) * 512],
                             start=(dc == 0), stop=(dc == 1)),
                    inc=(S["pe"] if dc == 1 else None))
            pe_cnt = S["pe"].n
            b.o("vector", lambda E, kb=kb, qg=qg: E.tensor_scalar(
                out=stat[:, 8:9], in0=pcol[:], scalar1=float(kb * P - qg * 512),
                scalar2=None, op0=ALU.add))
            b.o("vector", lambda E: E.tensor_tensor(
                out=stat[:, 8:9], in0=stat[:, 8:9], in1=qoffc[:], op=ALU.subtract))
            b.o("vector", lambda E, ti=ti: E.tensor_scalar(
                out=msk2[ti][:], in0=iotab[:], scalar1=stat[:, 8:9], scalar2=None,
                op0=ALU.is_ge))
            b.w("scalar", S["pe"], pe_cnt)
            if ti in ringP:
                b.w("scalar", S["pe"], ringP[ti])
            b.o("scalar", lambda E, bank=bank, ti=ti: E.activation(
                out=r32(PTs2[ti][:]), in_=bank[:, 0:512], func=AF.Exp, scale=SCL))
            ringS[bi] = S["ac"].n
            b.w("vector", S["ac"], S["ac"].n)
            b.o("vector", lambda E, ti=ti: E.tensor_tensor(out=r32(PTs2[ti][:]), in0=PTs2[ti][:],
                                                           in1=msk2[ti][:], op=ALU.mult))
            dv_cnt = S["dv"].n
            b.w("tensor", S["dv"], dv_cnt)
            for dc in range(2):
                b.o("tensor", lambda E, kb=kb, dc=dc, ti=ti:
                    E.matmul(out=pvb[dc][:, 0:512], lhsT=vds[:, kb, dc * P:(dc + 1) * P],
                             rhs=r32(PTs2[ti][:]), start=(kb == 0), stop=(kb == NKB1 - 1)),
                    inc=(S["pe"] if (kb == NKB1 - 1 and dc == 1) else None))
            ringP[ti] = S["pe"].n
        b.w("vector", S["pe"])
        for dc in range(2):
            b.o("vector", lambda E, dc=dc, qg=qg: E.tensor_tensor(
                out=r32(attn_d[:, dc, qg * 512:(qg + 1) * 512]), in0=pvb[dc][:, 0:512],
                in1=mbc[:], op=ALU.mult))

    for tt in range(MT // P):
        b.w("vector", S["ac"])
        for nc_ in range(2):
            bank = ps[(tt * 2 + nc_) % 4]
            b.w("tensor", S["dv"])
            b.w("tensor", S["ac"])
            for kc in range(2):
                b.o("tensor", lambda E, tt=tt, nc_=nc_, kc=kc, bank=bank:
                    E.matmul(out=bank[:, 0:512], lhsT=r32(attn_d[:, kc, tt * P:(tt + 1) * P]),
                             rhs=wvupb[:, kc, nc_ * 512:(nc_ + 1) * 512],
                             start=(kc == 0), stop=(kc == 1)),
                    inc=(S["pe"] if kc == 1 else None))
            b.w("vector", S["pe"])
            b.o("vector", lambda E, nc_=nc_, bank=bank: E.tensor_copy(
                out=ztm[:, nc_ * 512:(nc_ + 1) * 512], in_=bank[:, 0:512]))
        emit_ln_tile(b, ztm[:], xt[tt % 2][:], sqb, st, epsb[:, 0:1])
        b.w("sync", S["dv"])
        b.o("sync", lambda E, r0=tt * P: E.dma_start(out=sqb[:], in_=x2[r0:r0 + P, :]),
            inc=S["lx"], k=16)
        b.w("vector", S["lx"])
        b.o("vector", lambda E, tt=tt: E.tensor_tensor(out=xt[tt % 2][:], in0=xt[tt % 2][:],
                                                       in1=sqb[:], op=ALU.add))
        b.w("sync", S["dv"])
        b.o("sync", lambda E, r0=tt * P, tt=tt: E.dma_start(out=x3[r0:r0 + P, :], in_=xt[tt % 2][:]),
            inc=S["stx"], k=16)
        b.w("vector", S["stx"])
        b.w("scalar", S["stx"])
    b.w("sync", S["stx"])



def build_l0f(nc=None, races=True, dbg=False):
    nc = nc or bass.Bass("TRN2", target_bir_lowering=False, debug=False, detect_race_conditions=races)
    io = {
        "x1": nc.dram_tensor("x1", [MT, DM], F32, kind="ExternalInput").ap(),
        "routerW": nc.dram_tensor("routerW", [2, DM, 6], F32R, kind="ExternalInput").ap(),
        "rb": nc.dram_tensor("rb", [2, 6], F32R, kind="ExternalInput").ap(),
        "gb": nc.dram_tensor("gb", [2, 6], F32R, kind="ExternalInput").ap(),
        "normw": nc.dram_tensor("normw", [2, DM], F32R, kind="ExternalInput").ap(),
        "shWup": nc.dram_tensor("shWup", [DM, 2 * DM], F32R, kind="ExternalInput").ap(),
        "shWg": nc.dram_tensor("shWg", [DM, DM], F32R, kind="ExternalInput").ap(),
        "shWdown": nc.dram_tensor("shWdown", [DM, DM], F32R, kind="ExternalInput").ap(),
        "expWup": nc.dram_tensor("expWup", [6, DM, DM], F32R, kind="ExternalInput").ap(),
        "expWg": nc.dram_tensor("expWg", [6, DM, DM], F32R, kind="ExternalInput").ap(),
        "expWdown": nc.dram_tensor("expWdown", [6, DM, DM], F32R, kind="ExternalInput").ap(),
        "bup": nc.dram_tensor("bup", [P, 64], F32, kind="ExternalInput").ap(),
        "bg": nc.dram_tensor("bg", [P, 64], F32, kind="ExternalInput").ap(),
        "bdall": nc.dram_tensor("bdall", [1, NUNIT * DM], F32R, kind="ExternalInput").ap(),
        "ones": nc.dram_tensor("ones", [1, P], F32R, kind="ExternalInput").ap(),
        "ident": nc.dram_tensor("ident", [P, P], F32, kind="ExternalInput").ap(),
        "x2": nc.dram_tensor("x2", [MT, DM], F32, kind="ExternalOutput").ap(),
    }
    if dbg:
        io["dbg"] = [nc.dram_tensor(f"dbg{i}", [P, DM], F32, kind="ExternalOutput").ap() for i in range(12)]
    b = B(nc)
    b.begin()
    emit_l0_ffn(b, io)
    b.end()
    b.emit()
    return nc


def l0f_shared_inputs(z):
    f = np.float32
    bup = np.empty((P, 64), f)
    bg = np.empty((P, 64), f)
    bd = np.empty((NUNIT, DM), f)
    sbup = np.asarray(z["L0_ffn_share_bup"], f)
    for u in range(2):
        bup[:, u * 8:(u + 1) * 8] = sbup[u * DM:(u + 1) * DM].reshape(FC, P).T
        bg[:, u * 8:(u + 1) * 8] = np.asarray(z["L0_ffn_share_bg"], f).reshape(FC, P).T
        bd[u] = np.asarray(z["L0_ffn_share_bdown"], f)
    for e in range(6):
        bup[:, (e + 2) * 8:(e + 3) * 8] = np.asarray(z["L0_ffn_exp_bup"], f)[e].reshape(FC, P).T
        bg[:, (e + 2) * 8:(e + 3) * 8] = np.asarray(z["L0_ffn_exp_bg"], f)[e].reshape(FC, P).T
        bd[e + 2] = np.asarray(z["L0_ffn_exp_bdown"], f)[e]
    return {
        "routerW": np.ascontiguousarray(np.asarray(z["L0_ffn_router_W"], f)),
        "rb": np.ascontiguousarray(np.asarray(z["L0_ffn_router_b"], f)),
        "gb": np.ascontiguousarray(np.asarray(z["L0_ffn_gate_bias"], f)),
        "normw": np.ascontiguousarray(np.asarray(z["L0_ffn_norm_w"], f)),
        "shWup": np.ascontiguousarray(np.asarray(z["L0_ffn_share_Wup"], f)),
        "shWg": np.ascontiguousarray(np.asarray(z["L0_ffn_share_Wg"], f)),
        "shWdown": np.ascontiguousarray(np.asarray(z["L0_ffn_share_Wdown"], f)),
        "expWup": np.ascontiguousarray(np.asarray(z["L0_ffn_exp_Wup"], f)),
        "expWg": np.ascontiguousarray(np.asarray(z["L0_ffn_exp_Wg"], f)),
        "expWdown": np.ascontiguousarray(np.asarray(z["L0_ffn_exp_Wdown"], f)),
        "bup": bup, "bg": bg, "bdall": bd.reshape(1, -1).copy(),
        "ones": np.ones((1, P), f),
        "ident": np.eye(P, dtype=f),
    }


def build_l1q(nc=None, races=True):
    nc = nc or bass.Bass("TRN2", target_bir_lowering=False, debug=False, detect_race_conditions=races)
    io = {
        "x2": nc.dram_tensor("x2", [MT, DM], F32, kind="ExternalInput").ap(),
        "wq": nc.dram_tensor("wq", [DM, 256], F32R, kind="ExternalInput").ap(),
        "wk": nc.dram_tensor("wk", [DM, 256], F32R, kind="ExternalInput").ap(),
        "wvd": nc.dram_tensor("wvd", [DM, 256], F32R, kind="ExternalInput").ap(),
        "ident": nc.dram_tensor("ident", [P, P], F32, kind="ExternalInput").ap(),
        "qf": nc.dram_tensor("qf", [256, MT], F32, kind="ExternalOutput").ap(),
        "kf": nc.dram_tensor("kf", [256, MT], F32, kind="ExternalOutput").ap(),
        "vdf": nc.dram_tensor("vdf", [MT, 256], F32, kind="ExternalOutput").ap(),
    }
    b = B(nc)
    b.begin()
    emit_l1_qkv(b, io)
    b.end()
    b.emit()
    return nc


def build_l1a(nc=None, races=True):
    nc = nc or bass.Bass("TRN2", target_bir_lowering=False, debug=False, detect_race_conditions=races)
    io = {
        "x2": nc.dram_tensor("x2", [MT, DM], F32, kind="ExternalInput").ap(),
        "qf": nc.dram_tensor("qf", [256, MT], F32R, kind="ExternalInput").ap(),
        "kfull": nc.dram_tensor("kfull", [256, NSEQ], F32R, kind="ExternalInput").ap(),
        "vdfull": nc.dram_tensor("vdfull", [NSEQ, 256], F32R, kind="ExternalInput").ap(),
        "wvup": nc.dram_tensor("wvup", [256, DM], F32R, kind="ExternalInput").ap(),
        "qglob": nc.dram_tensor("qglob", [P, 16], F32, kind="ExternalInput").ap(),
        "pcol": nc.dram_tensor("pcol", [P, 1], F32, kind="ExternalInput").ap(),
        "iotab": nc.dram_tensor("iotab", [P, 512], F32, kind="ExternalInput").ap(),
        "qoffc": nc.dram_tensor("qoffc", [P, 1], F32, kind="ExternalInput").ap(),
        "ones": nc.dram_tensor("ones", [1, P], F32R, kind="ExternalInput").ap(),
        "ident": nc.dram_tensor("ident", [P, P], F32, kind="ExternalInput").ap(),
        "x3": nc.dram_tensor("x3", [MT, DM], F32, kind="ExternalOutput").ap(),
    }
    b = B(nc)
    b.begin()
    emit_l1_attn(b, io)
    b.end()
    b.emit()
    return nc


def l1a_consts(c):
    qpos = c % 4
    f = np.float32
    qglob = (qpos * MT + np.arange(16)[None, :] * P + np.arange(P)[:, None]).astype(f)
    pcol = np.arange(P, dtype=f).reshape(P, 1)
    iotab = np.broadcast_to(np.arange(512, dtype=f), (P, 512)).copy()
    qoffc = np.full((P, 1), float(qpos * MT), f)
    return {"qglob": qglob, "pcol": pcol, "iotab": iotab, "qoffc": qoffc,
            "ones": np.ones((1, P), f), "ident": np.eye(P, dtype=f)}



def emit_l1_ffn(b, io):
    nc = b.nc
    S = b.S
    a = lambda n_, sh_, dt_: b.sb(n_, sh_, dt_)

    xf = a("l1f_xf", [P, FC, HB], F32)
    hb = a("l1f_h", [P, FC, HB], F32)
    h2b = a("l1f_h2", [P, FC, HB], F32)
    zb = a("l1f_z", [P, 8, HB], F32)
    wsl = [a(f"l1f_w{i}", [P, 4096], F32R) for i in range(2)]
    rwb = a("l1f_rw", [P, FC, 8], F32R)
    wgt = a("l1f_wgt", [P, 128], F32)
    xt = [a(f"l1f_xt{i}", [P, DM], F32) for i in range(2)]
    xm = [a(f"l1f_xm{i}", [P, DM], F32) for i in range(2)]
    sqb = a("l1f_sq", [P, DM], F32)
    st = a("l1f_st", [P, 8], F32)
    idt = a("l1f_id", [P, P], F32)
    epsb = a("l1f_eps", [P, 1], F32)
    gtile = a("l1f_gt", [P, 8], F32)

    ps = [b.psm(f"l1f_p{i}", [P, 512], F32) for i in range(6)]
    pT = b.psm("l1f_pT", [P, 1024], F32)

    x3 = io["x3"]
    x4 = io["x4"]

    b.o("sync", lambda E: E.dma_start(out=idt[:], in_=io["ident"]), inc=S["lc"], k=16)
    b.o("sync", lambda E: E.dma_start(out=rwb[:], in_=io["router"].rearrange("(a p) e -> p a e", p=P)),
        inc=S["lw"], k=16)
    b.w("vector", S["lc"])
    b.o("vector", lambda E: E.memset(epsb[:], LN_EPS))

    for h in (0, 1):
        t0 = h * HB

        for i in range(HB // P):
            b.w("sync", S["dv"])
            b.w("sync", S["pe"])
            b.o("sync", lambda E, r0=t0 + i * P, i=i: E.dma_start(out=xt[i % 2][:], in_=x3[r0:r0 + P, :]),
                inc=S["lx"], k=16)
            b.w("vector", S["lx"])
            b.w("scalar", S["lx"])
            emit_ln_tile(b, xt[i % 2][:], xm[i % 2][:], sqb, st, epsb[:, 0:1])
            emit_tm2fm(b, xm[i % 2][:], xf, i * P, pT, idt[:], out_r32=True)

        for tt in range(HB // P):
            b.w("tensor", S["lw"])
            b.w("tensor", S["dv"])
            b.w("tensor", S["ac"])
            for kc in range(FC):
                b.o("tensor", lambda E, kc=kc, tt=tt:
                    E.matmul(out=ps[4][:, 0:8], lhsT=r32(xf[:, kc, tt * P:(tt + 1) * P]),
                             rhs=rwb[:, kc, :], start=(kc == 0), stop=(kc == FC - 1)),
                    inc=(S["pe"] if kc == FC - 1 else None))
            b.w("vector", S["pe"])
            b.o("vector", lambda E: E.tensor_copy(out=gtile[:], in_=ps[4][:, 0:8]))
            b.o("vector", lambda E: E.reduce_max(out=st[:, 0:1], in_=gtile[:], axis=AX))
            b.o("vector", lambda E: E.tensor_single_scalar(out=st[:, 1:2], in_=st[:, 0:1],
                                                           scalar=-1.0, op=ALU.mult))
            b.w("scalar", S["dv"])
            b.o("scalar", lambda E: E.activation(out=sqb[:, 0:8], in_=gtile[:], func=AF.Exp,
                                                 bias=st[:, 1:2], accum_out=st[:, 2:3]))
            b.w("vector", S["ac"])
            b.o("vector", lambda E: E.reciprocal(out=st[:, 3:4], in_=st[:, 2:3]))
            b.o("vector", lambda E: E.tensor_single_scalar(out=sqb[:, 0:8], in_=sqb[:, 0:8],
                                                           scalar=st[:, 3:4], op=ALU.mult))
            b.o("vector", lambda E: E.reduce_max(out=st[:, 4:5], in_=sqb[:, 0:8], axis=AX))
            b.o("vector", lambda E: E.tensor_scalar(out=sqb[:, 8:16], in0=sqb[:, 0:8],
                                                    scalar1=st[:, 4:5], scalar2=BIGNEG,
                                                    op0=ALU.is_ge, op1=ALU.mult))
            b.o("vector", lambda E: E.tensor_tensor(out=sqb[:, 8:16], in0=sqb[:, 0:8],
                                                    in1=sqb[:, 8:16], op=ALU.add))
            b.o("vector", lambda E: E.reduce_max(out=st[:, 5:6], in_=sqb[:, 8:16], axis=AX))
            b.o("vector", lambda E: E.tensor_scalar(out=sqb[:, 8:16], in0=sqb[:, 0:8],
                                                    scalar1=st[:, 5:6], scalar2=None, op0=ALU.is_ge))
            b.o("vector", lambda E, tt=tt: E.tensor_tensor(out=wgt[:, tt * 8:(tt + 1) * 8],
                                                           in0=sqb[:, 0:8], in1=sqb[:, 8:16],
                                                           op=ALU.mult))

        banks8 = [p[:] for p in ps] + [pT[:, 0:512], pT[:, 512:1024]]
        ring = {}
        lwp = [S["lw"], S["lw2"]]
        slabs = []

        def mk_gu(u, hm, nI, kind):
            def body(wv, lw):
                b.w("tensor", lw)
                b.w("tensor", S["dv"])
                b.w("tensor", S["ac"])
                if kind == "gate":
                    def ev(gmo):
                        b.w("scalar", S["dv"])
                        b.o("scalar", lambda E, gmo=gmo, ci=body.ci:
                            E.activation(out=hb[:, gmo, ci * 512:(ci + 1) * 512],
                                         in_=banks8[gmo % 8][:, 0:512], func=AF.Silu))
                else:
                    def ev(gmo):
                        b.w("vector", S["ac"])
                        b.o("vector", lambda E, gmo=gmo, ci=body.ci:
                            E.tensor_tensor(out=r32(h2b[:, gmo, ci * 512:(ci + 1) * 512]),
                                            in0=banks8[gmo % 8][:, 0:512],
                                            in1=hb[:, gmo, ci * 512:(ci + 1) * 512], op=ALU.mult))
                wv3 = wv.rearrange("p (a d) -> p a d", d=512)
                for ci in range(HB // 512):
                    body.ci = ci
                    emit_proj_rot(b, wv3, xf, ci * 512, banks8, ev, nko=4, ring=ring,
                                  mo_off=hm * 4)
            return body

        def mk_down(u, nh_list, nI):
            is_sh = u == 0
            e = u - 1
            def body(wv, lw):
                b.w("tensor", lw)
                b.w("tensor", S["dv"])
                wv3 = wv.rearrange("p (a d) -> p a d", d=1024 if len(nh_list) == 2 else 512)
                for tt in range(HB // P):
                    for ni, nh in enumerate(nh_list):
                        gb_ = (tt * len(nh_list) + ni) % 8
                        bank = banks8[gb_]
                        if gb_ in ring:
                            _, cdv, cac = ring[gb_]
                            b.w("tensor", S["dv"], cdv)
                            b.w("tensor", S["ac"], cac)
                        for kc in range(nI):
                            rhsv = wv3[:, kc, ni * 512:(ni + 1) * 512] if len(nh_list) == 2                                 else wv3[:, kc, :]
                            b.o("tensor", lambda E, tt=tt, kc=kc, bank=bank, rhsv=rhsv, nI=nI:
                                E.matmul(out=bank[:, 0:512],
                                         lhsT=r32(h2b[:, kc, tt * P:(tt + 1) * P]),
                                         rhs=rhsv, start=(kc == 0), stop=(kc == nI - 1)),
                                inc=(S["pe"] if kc == nI - 1 else None))
                        b.w("vector", S["pe"])
                        if is_sh:
                            b.o("vector", lambda E, tt=tt, nh=nh, bank=bank:
                                E.tensor_copy(out=zb[:, tt, nh * 512:(nh + 1) * 512],
                                              in_=bank[:, 0:512]))
                        else:
                            b.o("vector", lambda E, tt=tt, nh=nh, e=e, bank=bank:
                                E.scalar_tensor_tensor(out=zb[:, tt, nh * 512:(nh + 1) * 512],
                                                       in0=bank[:, 0:512],
                                                       scalar=wgt[:, tt * 8 + e:tt * 8 + e + 1],
                                                       in1=zb[:, tt, nh * 512:(nh + 1) * 512],
                                                       op0=ALU.mult, op1=ALU.add))
                        ring[gb_] = ("both", S["dv"].n, S["ac"].n)
            return body

        for u in range(9):
            is_sh = u == 0
            e = u - 1
            if is_sh:
                gsrc = io["shg"].rearrange("(a p) d -> p a d", p=P)
                usrc = io["shu"].rearrange("(a p) d -> p a d", p=P)
                dsrc = io["shd"].rearrange("(a p) d -> p a d", p=P)
                for hm in range(2):
                    slabs.append((gsrc[:, :, hm * 512:(hm + 1) * 512], mk_gu(u, hm, FC, "gate")))
                for hm in range(2):
                    slabs.append((usrc[:, :, hm * 512:(hm + 1) * 512], mk_gu(u, hm, FC, "up")))
                for nh in range(2):
                    slabs.append((dsrc[:, :, nh * 512:(nh + 1) * 512], mk_down(u, [nh], FC)))
            else:
                gsrc = io["eg"][e].rearrange("(a p) d -> p a d", p=P)
                usrc = io["eu"][e].rearrange("(a p) d -> p a d", p=P)
                dsrc = io["ed"][e].rearrange("(a p) d -> p a d", p=P)
                slabs.append((gsrc, mk_gu(u, 0, 4, "gate")))
                slabs.append((usrc, mk_gu(u, 0, 4, "up")))
                slabs.append((dsrc, mk_down(u, [0, 1], 4)))

        for i, (ws, body) in enumerate(slabs):
            shp = ws.shape
            if i == 0:
                b.w("sync", S["pe"])
                b.o("sync", lambda E, ws=ws, shp=shp: E.dma_start(
                    out=wsl[0][:].rearrange("p (a d) -> p a d", d=shp[2])[:, 0:shp[1], :], in_=ws),
                    inc=lwp[0], k=16)
            if i + 1 < len(slabs):
                ws2 = slabs[i + 1][0]
                shp2 = ws2.shape
                b.w("sync", S["pe"])
                b.o("sync", lambda E, ws2=ws2, shp2=shp2, j=(i + 1) % 2: E.dma_start(
                    out=wsl[j][:].rearrange("p (a d) -> p a d", d=shp2[2])[:, 0:shp2[1], :], in_=ws2),
                    inc=lwp[(i + 1) % 2], k=16)
            body(wsl[i % 2][:], lwp[i % 2])

        for tt in range(HB // P):
            b.w("vector", S["ac"])
            emit_ln_tile(b, zb[:, tt, :], xt[tt % 2][:], sqb, st, epsb[:, 0:1])
            b.w("sync", S["dv"])
            b.o("sync", lambda E, r0=t0 + tt * P: E.dma_start(out=sqb[:], in_=x3[r0:r0 + P, :]),
                inc=S["lx"], k=16)
            b.w("vector", S["lx"])
            b.o("vector", lambda E, tt=tt: E.tensor_tensor(out=xt[tt % 2][:], in0=xt[tt % 2][:],
                                                           in1=sqb[:], op=ALU.add))
            b.w("sync", S["dv"])
            b.o("sync", lambda E, r0=t0 + tt * P, tt=tt:
                E.dma_start(out=x4[r0:r0 + P, :], in_=xt[tt % 2][:]), inc=S["stx"], k=16)
            b.w("vector", S["stx"])
            b.w("scalar", S["stx"])
    b.w("sync", S["stx"])


def build_l1f(nc=None, races=True):
    nc = nc or bass.Bass("TRN2", target_bir_lowering=False, debug=False, detect_race_conditions=races)
    io = {
        "x3": nc.dram_tensor("x3", [MT, DM], F32, kind="ExternalInput").ap(),
        "router": nc.dram_tensor("router", [DM, 8], F32R, kind="ExternalInput").ap(),
        "shg": nc.dram_tensor("shg", [DM, DM], F32R, kind="ExternalInput").ap(),
        "shu": nc.dram_tensor("shu", [DM, DM], F32R, kind="ExternalInput").ap(),
        "shd": nc.dram_tensor("shd", [DM, DM], F32R, kind="ExternalInput").ap(),
        "eg": nc.dram_tensor("eg", [8, DM, 512], F32R, kind="ExternalInput").ap(),
        "eu": nc.dram_tensor("eu", [8, DM, 512], F32R, kind="ExternalInput").ap(),
        "ed": nc.dram_tensor("ed", [8, 512, DM], F32R, kind="ExternalInput").ap(),
        "ident": nc.dram_tensor("ident", [P, P], F32, kind="ExternalInput").ap(),
        "x4": nc.dram_tensor("x4", [MT, DM], F32, kind="ExternalOutput").ap(),
    }
    b = B(nc)
    b.begin()
    emit_l1_ffn(b, io)
    b.end()
    b.emit()
    return nc




def build_fused1(races=True):
    nc = bass.Bass("TRN2", target_bir_lowering=False, debug=False, detect_race_conditions=races)
    dt_ = nc.dram_tensor
    ident = dt_("ident", [P, P], F32, kind="ExternalInput").ap()
    ones = dt_("ones", [1, P], F32R, kind="ExternalInput").ap()
    x1 = dt_("x1", [MT, DM], F32, kind="Internal").ap()
    x2 = dt_("x2", [MT, DM], F32, kind="ExternalOutput").ap()
    ioA = {
        "x_ext": dt_("x_ext", [TEXT, DM], F32, kind="ExternalInput").ap(),
        "wkT": dt_("wkT", [DM, DM], F32R, kind="ExternalInput").ap(),
        "wvT": dt_("wvT", [DM, DM], F32R, kind="ExternalInput").ap(),
        "wrT": dt_("wrT", [DM, DM], F32R, kind="ExternalInput").ap(),
        "woT": dt_("woT", [DM, DM], F32R, kind="ExternalInput").ap(),
        "tmk": dt_("tmk", [P, FC], F32, kind="ExternalInput").ap(),
        "tmv": dt_("tmv", [P, FC], F32, kind="ExternalInput").ap(),
        "tmr": dt_("tmr", [P, FC], F32, kind="ExternalInput").ap(),
        "td": dt_("td", [P, FC], F32, kind="ExternalInput").ap(),
        "tf": dt_("tf", [P, FC], F32, kind="ExternalInput").ap(),
        "g": dt_("g", [P, FC], F32, kind="ExternalInput").ap(),
        "bv": dt_("bv", [P, FC], F32R, kind="ExternalInput").ap(),
        "ident": ident, "x1": x1,
    }
    ioB = {
        "x1": x1, "x2": x2, "ident": ident, "ones": ones,
        "routerW": dt_("routerW", [2, DM, 6], F32R, kind="ExternalInput").ap(),
        "rb": dt_("rb", [2, 6], F32R, kind="ExternalInput").ap(),
        "gb": dt_("gb", [2, 6], F32R, kind="ExternalInput").ap(),
        "normw": dt_("normw", [2, DM], F32R, kind="ExternalInput").ap(),
        "shWup": dt_("shWup", [DM, 2 * DM], F32R, kind="ExternalInput").ap(),
        "shWg": dt_("shWg", [DM, DM], F32R, kind="ExternalInput").ap(),
        "shWdown": dt_("shWdown", [DM, DM], F32R, kind="ExternalInput").ap(),
        "expWup": dt_("expWup", [6, DM, DM], F32R, kind="ExternalInput").ap(),
        "expWg": dt_("expWg", [6, DM, DM], F32R, kind="ExternalInput").ap(),
        "expWdown": dt_("expWdown", [6, DM, DM], F32R, kind="ExternalInput").ap(),
        "bup": dt_("bup", [P, 64], F32, kind="ExternalInput").ap(),
        "bg": dt_("bg", [P, 64], F32, kind="ExternalInput").ap(),
        "bdall": dt_("bdall", [1, NUNIT * DM], F32R, kind="ExternalInput").ap(),
    }
    ioC = {
        "x2": x2, "ident": ident,
        "wq": dt_("wq", [DM, 256], F32R, kind="ExternalInput").ap(),
        "wk": dt_("wk", [DM, 256], F32R, kind="ExternalInput").ap(),
        "wvd": dt_("wvd", [DM, 256], F32R, kind="ExternalInput").ap(),
        "qf": dt_("qf", [256, MT], F32, kind="ExternalOutput").ap(),
        "kf": dt_("kf", [256, MT], F32, kind="ExternalOutput").ap(),
        "vdf": dt_("vdf", [MT, 256], F32, kind="ExternalOutput").ap(),
    }
    b = B(nc)
    b.begin(); emit_l0_attn(b, ioA); b.end()
    b.begin(); emit_l0_ffn(b, ioB); b.end()
    b.begin(); emit_l1_qkv(b, ioC); b.end()
    b.emit()
    return nc


def build_fused2(races=True):
    nc = bass.Bass("TRN2", target_bir_lowering=False, debug=False, detect_race_conditions=races)
    dt_ = nc.dram_tensor
    ident = dt_("ident", [P, P], F32, kind="ExternalInput").ap()
    x3 = dt_("x3", [MT, DM], F32, kind="Internal").ap()
    ioD = {
        "x2": dt_("x2", [MT, DM], F32, kind="ExternalInput").ap(),
        "qf": dt_("qf", [256, MT], F32R, kind="ExternalInput").ap(),
        "kfull": dt_("kfull", [256, NSEQ], F32R, kind="ExternalInput").ap(),
        "vdfull": dt_("vdfull", [NSEQ, 256], F32R, kind="ExternalInput").ap(),
        "wvup": dt_("wvup", [256, DM], F32R, kind="ExternalInput").ap(),
        "qglob": dt_("qglob", [P, 16], F32, kind="ExternalInput").ap(),
        "pcol": dt_("pcol", [P, 1], F32, kind="ExternalInput").ap(),
        "iotab": dt_("iotab", [P, 512], F32, kind="ExternalInput").ap(),
        "qoffc": dt_("qoffc", [P, 1], F32, kind="ExternalInput").ap(),
        "ones": dt_("ones", [1, P], F32R, kind="ExternalInput").ap(),
        "ident": ident, "x3": x3,
    }
    ioE = {
        "x3": x3, "ident": ident,
        "router": dt_("router", [DM, 8], F32R, kind="ExternalInput").ap(),
        "shg": dt_("shg", [DM, DM], F32R, kind="ExternalInput").ap(),
        "shu": dt_("shu", [DM, DM], F32R, kind="ExternalInput").ap(),
        "shd": dt_("shd", [DM, DM], F32R, kind="ExternalInput").ap(),
        "eg": dt_("eg", [8, DM, 512], F32R, kind="ExternalInput").ap(),
        "eu": dt_("eu", [8, DM, 512], F32R, kind="ExternalInput").ap(),
        "ed": dt_("ed", [8, 512, DM], F32R, kind="ExternalInput").ap(),
        "x4": dt_("x4", [MT, DM], F32, kind="ExternalOutput").ap(),
    }
    b = B(nc)
    b.begin(); emit_l1_attn(b, ioD); b.end()
    b.begin(); emit_l1_ffn(b, ioE); b.end()
    b.emit()
    return nc


_BUILD_CACHE = {}


def _get(name, fn):
    if name not in _BUILD_CACHE:
        _BUILD_CACHE[name] = fn()
    return _BUILD_CACHE[name]


def kernel(**inputs):
    f = np.float32
    z = inputs
    x = np.asarray(z["L0_attn_x"], f)
    cores = list(range(8))
    idm = np.eye(P, dtype=f)
    onesv = np.ones((1, P), f)

    nc1 = _get("F1", build_fused1)
    sh_b = l0f_shared_inputs(z)
    del sh_b["ident"], sh_b["ones"]
    wq = np.ascontiguousarray(np.asarray(z["L1_attn_Wq"], f))
    wkq = np.ascontiguousarray(np.asarray(z["L1_attn_Wk"], f))
    wvd = np.ascontiguousarray(np.asarray(z["L1_attn_Wvdown"], f))
    ins1 = []
    for c in cores:
        d = l0a_core_inputs(c, x,
                            np.asarray(z["L0_attn_Wk"], f), np.asarray(z["L0_attn_Wv"], f),
                            np.asarray(z["L0_attn_Wr"], f), np.asarray(z["L0_attn_Wo"], f),
                            np.asarray(z["L0_attn_time_mix_k"], f), np.asarray(z["L0_attn_time_mix_v"], f),
                            np.asarray(z["L0_attn_time_mix_r"], f), np.asarray(z["L0_attn_time_decay"], f),
                            np.asarray(z["L0_attn_time_first"], f), np.asarray(z["L0_attn_ln_g"], f),
                            np.asarray(z["L0_attn_ln_b"], f))
        d.update(sh_b)
        d.update({"ones": onesv, "wq": wq, "wk": wkq, "wvd": wvd})
        ins1.append(d)
    res = run_bass_kernel_spmd(nc1, ins1, core_ids=cores)
    x2 = [np.asarray(res.results[c]["x2"]) for c in cores]
    qf = [np.asarray(res.results[c]["qf"]) for c in cores]
    kf = [np.asarray(res.results[c]["kf"]) for c in cores]
    vdf = [np.asarray(res.results[c]["vdf"]) for c in cores]

    kfull = [np.ascontiguousarray(np.concatenate([kf[s * 4 + q] for q in range(4)], axis=1))
             for s in range(2)]
    vdfull = [np.ascontiguousarray(np.concatenate([vdf[s * 4 + q] for q in range(4)], axis=0))
              for s in range(2)]

    nc2 = _get("F2", build_fused2)
    wvu = np.ascontiguousarray(np.asarray(z["L1_attn_Wvup"], f))
    sh_e = {
        "router": np.ascontiguousarray(np.asarray(z["L1_ffn_router"], f)),
        "shg": np.ascontiguousarray(np.asarray(z["L1_ffn_shared_gate"], f)),
        "shu": np.ascontiguousarray(np.asarray(z["L1_ffn_shared_up"], f)),
        "shd": np.ascontiguousarray(np.asarray(z["L1_ffn_shared_down"], f)),
        "eg": np.ascontiguousarray(np.asarray(z["L1_ffn_experts_gate"], f)),
        "eu": np.ascontiguousarray(np.asarray(z["L1_ffn_experts_up"], f)),
        "ed": np.ascontiguousarray(np.asarray(z["L1_ffn_experts_down"], f)),
    }
    ins2 = []
    for c in cores:
        d = l1a_consts(c)
        d.update(sh_e)
        d.update({"x2": x2[c], "qf": qf[c], "kfull": kfull[c // 4],
                  "vdfull": vdfull[c // 4], "wvup": wvu})
        ins2.append(d)
    res = run_bass_kernel_spmd(nc2, ins2, core_ids=cores)
    out = np.empty((2, 8192, DM), f)
    for c in cores:
        seq, q = c // 4, c % 4
        out[seq, q * MT:(q + 1) * MT] = np.asarray(res.results[c]["x4"])
    return out
